# version 24
# baseline (speedup 1.0000x reference)
import numpy as np
from contextlib import ExitStack

import concourse.bass as bass
import concourse.mybir as mybir

F32 = mybir.dt.float32
BF16 = mybir.dt.bfloat16
AF = mybir.ActivationFunctionType
ALU = mybir.AluOpType
AX = mybir.AxisListType

T = 2048
D = 1024
NCH = 8
NPAIR = 8
NQQ = 4
QW = 512
KC = 16
NUC = 32
NTS = 4
EPS = 1e-5

P_KAR = 0
P_VAR = 16384
VSTR = 1056
MB = 33280
OA_QK64 = MB + 0
OA_VDUP = MB + 2048
OA_PSB = MB + 2048
OA_WQK = MB + 4096
OA_WVV = MB + 5120
OA_VTS = MB + 4096
OA_FCO = MB + 4096
OA_FCQK = MB + 6144
OA_FCVV = MB + 7168
OA_BNC = MB + 7168
OA_XB = MB + 2048
OA_XSQ2 = MB + 3072
OA_SQ = MB + 9216
OA_XSQ = MB + 8192
OB_KV = MB + 0
OB_PSB = MB + 0
OB_QROT = MB + 4096
OB_VTS = MB + 6144
OB_WPROJ = MB + 6144
OB_BNC = MB + 2048
OB_XB = MB + 2048
OB_SQ = MB + 9216
OB_XSQ = MB + 3072
OF_W2 = 0
OF_W1 = 32768
OF_XB = 40960
OF_XSQ = 41984
POOL_COLS = 43008


class Prog:
    def __init__(self, nc):
        self.nc = nc
        self._stack = None
        self.sems = {}
        self.cnt = {}

    def sem(self, name):
        s = self._stack.enter_context(self.nc.semaphore(name))
        self.sems[name] = s
        self.cnt[name] = 0
        return s

    def inc(self, instr, name, n=1):
        instr.then_inc(self.sems[name], n)
        self.cnt[name] += n
        return self.cnt[name]

    def wait(self, eng, name, v=None):
        eng.wait_ge(self.sems[name], self.cnt[name] if v is None else v)

    def barrier(self):
        nc = self.nc
        for eng in (nc.tensor, nc.scalar, nc.vector, nc.sync, nc.gpsimd):
            for nm in ("te", "act", "dve", "s16", "g16", "g16b", "gw2"):
                self.wait(eng, nm)


def build(upto=4, debug=0):
    nc = bass.Bass("TRN2", target_bir_lowering=False)
    p = Prog(nc)

    xT = nc.dram_tensor("xT", [D, T], F32, kind="ExternalInput").ap()
    wqk_d = nc.dram_tensor("wqk", [128, 1024], F32, kind="ExternalInput").ap()
    wvv_d = nc.dram_tensor("wvv", [128, 1024], F32, kind="ExternalInput").ap()
    fcqk_d = nc.dram_tensor("fcqk", [128, 1024], F32, kind="ExternalInput").ap()
    fcvv_d = nc.dram_tensor("fcvv", [128, 1024], F32, kind="ExternalInput").ap()
    fco_h = nc.dram_tensor("fco_h", [8, 128, 1024], F32, kind="ExternalInput").ap()
    wqkv_kv = nc.dram_tensor("wqkv_kv", [8, 2, 128, 1024], F32, kind="ExternalInput").ap()
    wqkv_q = nc.dram_tensor("wqkv_q", [8, 128, 1024], F32, kind="ExternalInput").ap()
    wproj_h = nc.dram_tensor("wproj_h", [8, 128, 1024], F32, kind="ExternalInput").ap()
    w1_h = nc.dram_tensor("w1_h", [8, 128, 4096], F32, kind="ExternalInput").ap()
    w2_h = nc.dram_tensor("w2_h", [128, 32768], F32, kind="ExternalInput").ap()
    biascols = nc.dram_tensor("biascols", [128, 80], F32, kind="ExternalInput").ap()
    identb_d = nc.dram_tensor("identb", [128, 128], BF16, kind="ExternalInput").ap()
    yT = nc.dram_tensor("yT", [D, T], F32, kind="ExternalOutput").ap()
    if debug:
        dbgh = nc.dram_tensor("dbgh", [D, T], F32, kind="ExternalOutput").ap()

    with ExitStack() as ctx:
        p._stack = ctx
        sbt = lambda nm, shp, dt: ctx.enter_context(nc.sbuf_tensor(nm, shp, dt))
        pst = lambda nm, shp, dt: ctx.enter_context(nc.psum_tensor(nm, shp, dt))
        for nm in ("te", "act", "dve", "s16", "g16", "g16b", "gw2"):
            p.sem(nm)

        X = sbt("X", [128, NCH * T], F32)
        BIG1 = sbt("BIG1", [128, 16384], BF16)
        BIG2 = sbt("BIG2", [128, 8192], BF16)
        POOL = sbt("POOL", [128, POOL_COLS], BF16)
        identb = sbt("identb_s", [128, 128], BF16)
        ones_b = sbt("ones_b", [128, 128], BF16)
        ones_f = sbt("ones_f", [128, 128], F32)
        bcols = sbt("bcols", [128, 80], F32)
        ebias = sbt("ebias", [128, 8], F32)
        sstat = sbt("sstat", [128, 32], F32)
        SCR = sbt("SCR", [128, 1024], F32)
        PF = sbt("PF", [128, 1024], F32)
        epsb = sbt("epsb", [1, 1], F32)

        PS_A = pst("PS_A", [128, 1024], F32)
        PS_B = pst("PS_B", [128, 1024], F32)
        PS_C = pst("PS_C", [128, 1024], F32)
        PS_D = pst("PS_D", [128, 512], F32)
        PS_E = pst("PS_E", [128, 512], F32)

        KAR = POOL[:, P_KAR:P_KAR + 16384]

        def vop(name, *a, **k):
            p.wait(nc.vector, "dve")
            return p.inc(getattr(nc.vector, name)(*a, **k), "dve", 1)

        def sop(name, *a, **k):
            p.wait(nc.scalar, "act")
            return p.inc(getattr(nc.scalar, name)(*a, **k), "act", 1)

        MEAN = SCR[0:1, 0:512]
        VARV = SCR[0:1, 512:1024]
        TSC = SCR[:, 512:1024]
        RECIP0 = PF[64:65, 0:512]
        RECIP1 = PF[64:65, 512:1024]
        S_QMAX = lambda c: sstat[0:1, 0 + c:1 + c]
        S_KMAX = lambda c: sstat[0:1, 8 + c:9 + c]
        S_KSL = lambda s: sstat[0:1, 16 + s:17 + s]
        S_EROW = sstat[0:1, 24:32]

        b_fcq = lambda c: bcols[:, 0 + c:1 + c]
        b_fck = lambda c: bcols[:, 8 + c:9 + c]
        b_fcv = lambda c: bcols[:, 16 + c:17 + c]
        b_fco = lambda c: bcols[:, 24 + c:25 + c]
        b_proj = lambda c: bcols[:, 32 + c:33 + c]
        b_b2 = lambda c: bcols[:, 40 + c:41 + c]
        b_b1 = lambda uc: bcols[:, 48 + uc:49 + uc]

        vop("memset", ones_b[:, :], 1.0)
        vop("memset", ones_f[:, :], 1.0)
        vop("memset", epsb[:, :], EPS)
        vop("memset", SCR[0:64, 0:512], 1.0)
        vop("memset", POOL[:, P_VAR:P_VAR + 16896], 1.0)

        p.inc(nc.sync.dma_start(identb[:, :], identb_d), "s16", 16)
        p.inc(nc.sync.dma_start(bcols[:, :], biascols), "s16", 16)
        for c in range(NCH):
            p.inc(nc.sync.dma_start(X[:, c * T:(c + 1) * T],
                                    xT[c * 128:(c + 1) * 128, :]), "s16", 16)
        x_loaded = p.cnt["s16"]

        p.inc(nc.gpsimd.dma_start(POOL[:, OA_WQK:OA_WQK + 1024], wqk_d), "g16", 16)
        p.inc(nc.gpsimd.dma_start(POOL[:, OA_WVV:OA_WVV + 1024], wvv_d), "g16", 16)
        p.inc(nc.gpsimd.dma_start(POOL[:, OA_FCQK:OA_FCQK + 1024], fcqk_d), "g16", 16)
        p.inc(nc.gpsimd.dma_start(POOL[:, OA_FCVV:OA_FCVV + 1024], fcvv_d), "g16", 16)
        aw_loaded = p.cnt["g16"]


        def emit_layernorm(ts, dst, dst_off, xb_off, xsq_off, wait_s16=None):
            tcol = lambda c: c * T + ts * QW
            p.wait(nc.tensor, "dve")
            mmd = {}
            for c in range(NCH):
                par = c % 2
                xb = POOL[:, xb_off + par * 512:xb_off + par * 512 + 512]
                xsq = POOL[:, xsq_off + par * 512:xsq_off + par * 512 + 512]
                if wait_s16 is not None and c == 0:
                    p.wait(nc.vector, "s16", wait_s16)
                if c >= 2:
                    p.wait(nc.vector, "te", mmd[c - 2])
                vop("tensor_copy", xb, X[:, tcol(c):tcol(c) + QW])
                v = vop("tensor_mul", xsq, xb, xb)
                p.wait(nc.tensor, "dve", v)
                nc.tensor.matmul(PS_C[0:1, 0:512], ones_b[:, 0:1], xb,
                                 start=(c == 0), stop=(c == NCH - 1))
                mmd[c] = p.inc(nc.tensor.matmul(PS_C[0:1, 512:1024], ones_b[:, 0:1],
                                                xsq,
                                                start=(c == 0), stop=(c == NCH - 1)),
                               "te", 1)
            p.wait(nc.vector, "te")
            vop("tensor_scalar_mul", MEAN, PS_C[0:1, 0:512], 1.0 / D)
            vop("tensor_mul", VARV, MEAN, MEAN)
            v = vop("scalar_tensor_tensor", VARV, PS_C[0:1, 512:1024], 1.0 / D,
                    VARV, ALU.mult, ALU.subtract)
            p.wait(nc.scalar, "dve", v)
            sop("activation", VARV, VARV, AF.Ln, bias=epsb[0:1, 0:1])
            a = sop("activation", VARV, VARV, AF.Exp, scale=-0.5)
            p.wait(nc.tensor, "act", a)
            nc.tensor.matmul(PS_C[:, 0:512], ones_f[0:1, :], MEAN, start=True, stop=True)
            t = p.inc(nc.tensor.matmul(PS_C[:, 512:1024], ones_f[0:1, :], VARV,
                                       start=True, stop=True), "te", 1)
            p.wait(nc.vector, "te", t)
            for c in range(NCH):
                vop("tensor_sub", TSC, X[:, tcol(c):tcol(c) + QW], PS_C[:, 0:512])
                vop("tensor_mul", dst[:, dst_off(c):dst_off(c) + QW],
                    TSC, PS_C[:, 512:1024])
            return p.cnt["dve"]

        def emit_layernorm_exp(ts_list, dst, xsq_off, wait_s16=None):
            tcol2 = lambda ts, c: c * T + ts * QW
            mmd = {}

            def stats(i, ts):
                if i % 2 == 0:
                    sum_dst, sq_dst = PS_D[0:1, 0:512], PS_E[0:1, 0:512]
                else:
                    sum_dst, sq_dst = PS_A[0:1, 0:512], PS_A[0:1, 512:1024]
                for c in range(NCH):
                    gi = i * NCH + c
                    xsq = POOL[:, xsq_off + (gi % 2) * 512:
                               xsq_off + (gi % 2) * 512 + 512]
                    if i == 0 and c == 0:
                        if wait_s16 is not None:
                            p.wait(nc.scalar, "s16", wait_s16)
                            p.wait(nc.tensor, "s16", wait_s16)
                        p.wait(nc.scalar, "dve")
                        p.wait(nc.tensor, "dve")
                    if gi >= 2 and (gi - 2) in mmd:
                        p.wait(nc.scalar, "te", mmd[gi - 2])
                    sq = sop("activation", xsq,
                             X[:, tcol2(ts, c):tcol2(ts, c) + QW], AF.Square)
                    nc.tensor.matmul(sum_dst, ones_f[:, 0:1],
                                     X[:, tcol2(ts, c):tcol2(ts, c) + QW],
                                     start=(c == 0), stop=(c == NCH - 1))
                    p.wait(nc.tensor, "act", sq)
                    mmd[gi] = p.inc(
                        nc.tensor.matmul(sq_dst, ones_b[:, 0:1], xsq,
                                         start=(c == 0), stop=(c == NCH - 1)),
                        "te", 1)

            def finalize_apply(i, ts):
                if i % 2 == 0:
                    sum_row, sq_row = PS_D[0:1, 0:512], PS_E[0:1, 0:512]
                else:
                    sum_row, sq_row = PS_A[0:1, 0:512], PS_A[0:1, 512:1024]
                p.wait(nc.vector, "te", mmd[i * NCH + NCH - 1])
                vop("tensor_scalar_mul", MEAN, sum_row, 1.0 / D)
                vop("tensor_mul", VARV, MEAN, MEAN)
                v = vop("scalar_tensor_tensor", VARV, sq_row, 1.0 / D,
                        VARV, ALU.mult, ALU.subtract)
                p.wait(nc.scalar, "dve", v)
                sop("activation", VARV, VARV, AF.Ln, bias=epsb[0:1, 0:1])
                a = sop("activation", VARV, VARV, AF.Exp, scale=-0.5)
                p.wait(nc.tensor, "act", a)
                p.wait(nc.tensor, "dve")
                nc.tensor.matmul(PS_C[:, 0:512], ones_f[0:1, :], MEAN,
                                 start=True, stop=True)
                t = p.inc(nc.tensor.matmul(PS_C[:, 512:1024], ones_f[0:1, :],
                                           VARV, start=True, stop=True), "te", 1)
                p.wait(nc.vector, "te", t)
                for c in range(NCH):
                    vop("tensor_sub", TSC, X[:, tcol2(ts, c):tcol2(ts, c) + QW],
                        PS_C[:, 0:512])
                    vop("tensor_mul", dst[:, tcol2(ts, c):tcol2(ts, c) + QW],
                        TSC, PS_C[:, 512:1024])

            stats(0, ts_list[0])
            out = []
            for i, ts in enumerate(ts_list):
                if i + 1 < len(ts_list):
                    stats(i + 1, ts_list[i + 1])
                finalize_apply(i, ts)
                out.append(p.cnt["dve"])
            return out

        def emit_vtranspose(vts_off, cpair):
            p.wait(nc.tensor, "dve")
            p.wait(nc.tensor, "act")
            round_ev = {}
            round_eng = {}
            for r4 in range(4):
                boff = (r4 % 2) * 512
                if r4 >= 2:
                    p.wait(nc.tensor, round_eng[r4 - 2], round_ev[r4 - 2])
                for j in range(4):
                    kt = r4 * 4 + j
                    psv = PS_C[:, boff + j * 64: boff + (j + 1) * 64].bitcast(BF16)
                    src = POOL[:, vts_off + kt * 128: vts_off + kt * 128 + 128]
                    t = p.inc(nc.tensor.transpose(psv, src, identb[:, :]), "te", 1)
                if r4 % 2 == 0:
                    p.wait(nc.vector, "te", t)
                else:
                    p.wait(nc.scalar, "te", t)
                for j in range(4):
                    kt = r4 * 4 + j
                    psv = PS_C[:, boff + j * 64: boff + (j + 1) * 64].bitcast(BF16)
                    dst = POOL[:, P_VAR + kt * VSTR + 132 * cpair:
                               P_VAR + kt * VSTR + 132 * cpair + 132]
                    da = dst.rearrange("p (h f) -> p h f", h=2)[:, :, 0:64]
                    sa = psv.rearrange("p (h f) -> p h f", h=2)
                    if r4 % 2 == 0:
                        v = vop("tensor_copy", da, sa)
                    else:
                        v = sop("activation", da, sa, AF.Identity)
                round_ev[r4] = v
                round_eng[r4] = "dve" if r4 % 2 == 0 else "act"
            return p.cnt["dve"]

        def emit_knorm(cpair, sq_off):
            p.wait(nc.scalar, "te")
            p.wait(nc.scalar, "dve")
            psl = [PS_D[0:1, 0:512], PS_E[0:1, 0:512]]
            tmm = {}
            mx = {}
            for sl in range(4):
                ksq = POOL[:, (sq_off if sl % 2 == 0 else MB + 8192):
                           (sq_off if sl % 2 == 0 else MB + 8192) + 512]
                kc_ap = KAR[:, cpair * T + sl * 512: cpair * T + sl * 512 + 512]
                if sl >= 2:
                    p.wait(nc.scalar, "te", tmm[sl - 2])
                sq = sop("activation", ksq, kc_ap, AF.Square)
                p.wait(nc.tensor, "act", sq)
                if sl >= 2:
                    p.wait(nc.tensor, "dve", mx[sl - 2])
                tmm[sl] = p.inc(nc.tensor.matmul(psl[sl % 2], ones_b[:, 0:1],
                                                 ksq, start=True, stop=True),
                                "te", 1)
                p.wait(nc.vector, "te", tmm[sl])
                mx[sl] = vop("reduce_max", S_KSL(sl), psl[sl % 2], axis=AX.X)
            vop("reduce_max", S_KMAX(cpair), sstat[0:1, 16:20], axis=AX.X)

        def emit_qnorm2_and_ebias(sq_off):
            p.wait(nc.scalar, "te")
            p.wait(nc.scalar, "dve")
            psl = [PS_D[0:1, 0:512], PS_E[0:1, 0:512]]
            tmm = {}
            mx = {}
            for c in range(NPAIR):
                for qqi in range(2):
                    gi = c * 2 + qqi
                    qsq = POOL[:, (sq_off if gi % 2 == 0 else MB + 8192):
                               (sq_off if gi % 2 == 0 else MB + 8192) + 512]
                    q_ap = BIG2[:, qqi * 4096 + c * 512: qqi * 4096 + c * 512 + 512]
                    if gi >= 2:
                        p.wait(nc.scalar, "te", tmm[gi - 2])
                    sq = sop("activation", qsq, q_ap, AF.Square)
                    p.wait(nc.tensor, "act", sq)
                    if gi >= 2:
                        p.wait(nc.tensor, "dve", mx[gi - 2])
                    tmm[gi] = p.inc(nc.tensor.matmul(psl[gi % 2], ones_b[:, 0:1],
                                                     qsq, start=True, stop=True),
                                    "te", 1)
                    p.wait(nc.vector, "te", tmm[gi])
                    mx[gi] = vop("reduce_max", sstat[0:1, 20 + qqi:21 + qqi],
                                 psl[gi % 2], axis=AX.X)
                vop("reduce_max", S_QMAX(c), sstat[0:1, 20:22], axis=AX.X)
            vop("tensor_add", S_EROW, sstat[0:1, 0:8], sstat[0:1, 8:16])
            v = vop("tensor_scalar_mul", S_EROW, S_EROW, -1.0 / 16.0)
            p.wait(nc.tensor, "dve", v)
            t = p.inc(nc.tensor.matmul(PS_D[:, 0:8], ones_f[0:1, :], S_EROW,
                                       start=True, stop=True), "te", 1)
            p.wait(nc.vector, "te", t)
            p.wait(nc.vector, "act")
            return vop("tensor_copy", ebias[:, :], PS_D[:, 0:8])

        roll = {"exp": {0: None, 1: None}, "pv": {0: None, 1: None}}

        prev_mul = {0: None, 1: None}
        pending = [None]

        def emit_pairloop(qq, qdst_off, o_dst, psb_off, bnc_off, raw_off, eb_dve,
                          o_free_dve, flush_tail=False):
            s_done_all = {}

            def obank(cp, h):
                if cp == 0:
                    return PS_C[:, h * 512:(h + 1) * 512]
                return (PS_D if h == 0 else PS_E)[:, 0:512]

            def emit_S_next(cn, kc):
                par = kc % 2
                PSx = PS_A if par == 0 else PS_B
                if roll["exp"][par] is not None:
                    p.wait(nc.tensor, "act", roll["exp"][par])
                lt0 = KAR[0:64, cn * T + kc * 128: cn * T + kc * 128 + 128]
                rh0 = BIG2[0:64, qdst_off + cn * 512: qdst_off + cn * 512 + 512]
                nc.tensor.matmul(PSx[:, 0:512], lt0, rh0, start=True, stop=True,
                                 tile_position=(0, 0))
                lt1 = KAR[64:128, cn * T + kc * 128: cn * T + kc * 128 + 128]
                rh1 = BIG2[64:128, qdst_off + cn * 512: qdst_off + cn * 512 + 512]
                s_done_all[(cn, kc)] = p.inc(
                    nc.tensor.matmul(PSx[:, 512:1024], lt1, rh1, start=True,
                                     stop=True, tile_position=(64, 0)), "te", 1)

            def flush_part(i):
                if pending[0] is None:
                    return
                rc_, raw0_, bncap_, ocols_, cp_, odst_ = pending[0]
                h = i // 2
                bank = obank(cp_, h)
                row = 0 if h == 0 else 32
                hilo = SCR[row:row + 1, 512:1024].bitcast(BF16)
                rhs = hilo[:, 0:512] if i % 2 == 0 else hilo[:, 512:1024]
                if i == 0:
                    p.wait(nc.tensor, "dve")
                p.inc(nc.tensor.matmul(bank[0:64, 0:512],
                                       ones_b[row:row + 1, 0:64], rhs,
                                       start=(i % 2 == 0), stop=(i % 2 == 1)),
                      "te", 1)

            def flush_pending():
                if pending[0] is None:
                    return
                rc_, raw0_, bncap_, ocols_, cp_, odst_ = pending[0]
                pending[0] = None
                p.wait(nc.vector, "te")
                vop("tensor_mul", odst_[0:64, ocols_:ocols_ + QW], raw0_,
                    obank(cp_, 0)[0:64, 0:512])
                bm = vop("tensor_mul", bncap_, bncap_, obank(cp_, 1)[0:64, 0:512])
                prev_mul[cp_] = bm
                p.wait(nc.sync, "dve", bm)
                p.inc(nc.sync.dma_start(odst_[64:128, ocols_:ocols_ + QW], bncap_),
                      "s16", 16)

            for c in range(NPAIR):
                exp_done = {}
                s_done = {}
                pv_done = {}

                def emit_S(kc):
                    if (c, kc) in s_done_all:
                        s_done[kc] = s_done_all.pop((c, kc))
                        return
                    emit_S_next(c, kc)
                    s_done[kc] = s_done_all.pop((c, kc))

                def emit_exp(kc):
                    par = kc % 2
                    PSx = PS_A if par == 0 else PS_B
                    psb = POOL[:, psb_off + par * 1024: psb_off + par * 1024 + 1024]
                    wv = s_done[kc]
                    if roll["pv"][par] is not None and roll["pv"][par] > wv:
                        wv = roll["pv"][par]
                    p.wait(nc.scalar, "te", wv)
                    if c == 0 and kc == 0:
                        p.wait(nc.scalar, "dve", eb_dve)
                    exp_done[kc] = p.inc(
                        nc.scalar.activation(psb, PSx[:, :], AF.Exp,
                                             bias=ebias[:, c:c + 1], scale=0.125),
                        "act", 1)
                    roll["exp"][par] = exp_done[kc]

                def emit_PV(kc):
                    par = kc % 2
                    cp = c % 2
                    psb = POOL[:, psb_off + par * 1024: psb_off + par * 1024 + 1024]
                    p.wait(nc.tensor, "act", exp_done[kc])
                    if kc == 0:
                        g = prev_mul[cp] if prev_mul[cp] is not None else 0
                        p.wait(nc.tensor, "dve", max(g, o_free_dve))
                    vbase = P_VAR + kc * VSTR + 132 * c
                    lt0 = POOL[:, vbase:vbase + 65]
                    nc.tensor.matmul(obank(cp, 0)[0:65, 0:512], lt0, psb[:, 0:512],
                                     start=(kc == 0), stop=(kc == KC - 1))
                    lt1 = POOL[:, vbase + 66:vbase + 66 + 65]
                    pv_done[kc] = p.inc(
                        nc.tensor.matmul(obank(cp, 1)[0:65, 0:512], lt1,
                                         psb[:, 512:1024],
                                         start=(kc == 0), stop=(kc == KC - 1)), "te", 1)
                    roll["pv"][par] = pv_done[kc]

                emit_S(0)
                for kc in range(KC):
                    emit_exp(kc)
                    if kc + 1 < KC:
                        emit_S(kc + 1)
                    if kc >= 1:
                        emit_PV(kc - 1)
                    if 9 <= kc <= 12:
                        flush_part(kc - 9)
                    if kc == 13:
                        flush_pending()
                    if kc == KC - 2 and c + 1 < NPAIR:
                        emit_S_next(c + 1, 0)
                    if kc == KC - 1 and c + 1 < NPAIR:
                        emit_S_next(c + 1, 1)
                emit_PV(KC - 1)

                p.wait(nc.vector, "te", pv_done[KC - 1])
                cp = c % 2
                raw0 = POOL[0:64, raw_off + cp * 512:raw_off + cp * 512 + 512]
                bncap = POOL[0:64, bnc_off + cp * 512:bnc_off + cp * 512 + 512]
                vop("tensor_copy", PF[0:1, 0:512], obank(cp, 0)[64:65, 0:512])
                hsums = vop("tensor_copy", PF[0:1, 512:1024],
                            obank(cp, 1)[64:65, 0:512])
                p.wait(nc.sync, "dve", hsums)
                p.inc(nc.sync.dma_start(SCR[0:1, 0:512], PF[0:1, 0:512]),
                      "s16", 16)
                p.inc(nc.sync.dma_start(SCR[32:33, 0:512], PF[0:1, 512:1024]),
                      "s16", 16)
                vop("tensor_copy", raw0, obank(cp, 0)[0:64, 0:512])
                vop("tensor_copy", bncap, obank(cp, 1)[0:64, 0:512])
                p.wait(nc.vector, "s16")
                vop("reciprocal", PF[0:33, 0:512], SCR[0:33, 0:512])
                hl = SCR[0:33, 512:1024].bitcast(BF16)
                vop("tensor_copy", hl[:, 0:512], PF[0:33, 0:512])
                vop("tensor_copy", PF[0:33, 512:1024], hl[:, 0:512])
                rc = vop("tensor_sub", hl[:, 512:1024], PF[0:33, 0:512],
                         PF[0:33, 512:1024])
                pending[0] = (rc, raw0, bncap, c * T + qq * QW, cp, o_dst)
            if flush_tail:
                for i in range(4):
                    flush_part(i)
                flush_pending()
            return p.cnt["dve"], p.cnt["s16"]

        def emit_outproj_all(o_src, w_dram, rot_off, bias_fn, o_dve, o_s16,
                             w_release_te):
            loads = {}
            te_done = {}
            ev_done = {}

            def load(fc):
                par = fc % 2
                sem = "g16" if par == 0 else "g16b"
                dst = POOL[:, rot_off + par * 1024: rot_off + par * 1024 + 1024]
                if fc >= 2:
                    p.wait(nc.gpsimd, "te", te_done[fc - 2])
                else:
                    p.wait(nc.gpsimd, "te", w_release_te)
                p.wait(nc.gpsimd, sem)
                loads[fc] = (sem, p.inc(nc.gpsimd.dma_start(dst, w_dram[fc]), sem, 16))

            load(0)
            load(1)
            banks = [(PS_A, 0), (PS_B, 0), (PS_A, 512), (PS_B, 512)]
            p.wait(nc.tensor, "act")
            for fc in range(NCH):
                par = fc % 2
                wt = POOL[:, rot_off + par * 1024: rot_off + par * 1024 + 1024]
                p.wait(nc.tensor, loads[fc][0])
                if fc == 0:
                    p.wait(nc.tensor, "dve", o_dve)
                    p.wait(nc.tensor, "s16", o_s16)
                for dc in range(NCH):
                    for qs in range(4):
                        PSx, coff = banks[qs]
                        if dc == 0 and ev_done.get(qs) is not None:
                            p.wait(nc.tensor, "dve", ev_done[qs])
                        tt = nc.tensor.matmul(
                            PSx[:, coff:coff + 512], wt[:, dc * 128:(dc + 1) * 128],
                            o_src[:, dc * T + qs * QW: dc * T + qs * QW + QW],
                            start=(dc == 0), stop=(dc == NCH - 1))
                te_done[fc] = p.inc(tt, "te", 1)
                if fc + 2 < NCH:
                    load(fc + 2)
                p.wait(nc.vector, "te", te_done[fc])
                for qs in range(4):
                    PSx, coff = banks[qs]
                    xs = X[:, fc * T + qs * QW: fc * T + qs * QW + QW]
                    ev_done[qs] = vop("scalar_tensor_tensor", xs,
                                      PSx[:, coff:coff + 512],
                                      bias_fn(fc), xs, ALU.add, ALU.add)
            return p.cnt["dve"], p.cnt["te"]

        def emit_ln_ffn(ts, dst, dst_off, xb_off, xsq_off):
            tcol = lambda c: c * T + ts * QW
            state = {}

            def stats():
                p.wait(nc.tensor, "dve")
                p.wait(nc.scalar, "dve")
                mmd = {}
                for c in range(NCH):
                    xb = POOL[:, xb_off + (c % 2) * 512:
                              xb_off + (c % 2) * 512 + 512]
                    xsq = POOL[:, xsq_off + (c % 2) * 512:
                               xsq_off + (c % 2) * 512 + 512]
                    if c >= 2:
                        p.wait(nc.vector, "te", mmd[c - 2])
                        p.wait(nc.scalar, "te", mmd[c - 2])
                    v = vop("tensor_copy", xb, X[:, tcol(c):tcol(c) + QW])
                    sq = sop("activation", xsq, X[:, tcol(c):tcol(c) + QW],
                             AF.Square)
                    p.wait(nc.tensor, "dve", v)
                    nc.tensor.matmul(PS_C[0:1, 0:512], ones_b[:, 0:1], xb,
                                     start=(c == 0), stop=(c == NCH - 1))
                    p.wait(nc.tensor, "act", sq)
                    mmd[c] = p.inc(
                        nc.tensor.matmul(PS_C[0:1, 512:1024], ones_b[:, 0:1],
                                         xsq, start=(c == 0),
                                         stop=(c == NCH - 1)),
                        "te", 1)
                p.wait(nc.vector, "te", mmd[NCH - 1])
                vop("tensor_scalar_mul", MEAN, PS_C[0:1, 0:512], 1.0 / D)
                vop("tensor_mul", VARV, MEAN, MEAN)
                v = vop("scalar_tensor_tensor", VARV, PS_C[0:1, 512:1024],
                        1.0 / D, VARV, ALU.mult, ALU.subtract)
                p.wait(nc.scalar, "dve", v)
                sop("activation", VARV, VARV, AF.Ln, bias=epsb[0:1, 0:1])
                state["a"] = sop("activation", VARV, VARV, AF.Exp, scale=-0.5)

            def bcast():
                p.wait(nc.tensor, "act", state["a"])
                p.wait(nc.tensor, "dve")
                nc.tensor.matmul(PS_C[:, 0:512], ones_f[0:1, :], MEAN,
                                 start=True, stop=True)
                state["t"] = p.inc(
                    nc.tensor.matmul(PS_C[:, 512:1024], ones_f[0:1, :], VARV,
                                     start=True, stop=True), "te", 1)

            def apply_chunk(c):
                if c == 0:
                    p.wait(nc.vector, "te", state["t"])
                vop("tensor_sub", TSC, X[:, tcol(c):tcol(c) + QW], PS_C[:, 0:512])
                vop("tensor_mul", dst[:, dst_off(c):dst_off(c) + QW],
                    TSC, PS_C[:, 512:1024])

            return stats, bcast, apply_chunk

        def emit_ffn():
            ld_w2 = p.inc(nc.gpsimd.dma_start(POOL[:, OF_W2:OF_W2 + 32768], w2_h,
                                              max_dma_last_dim=8192), "gw2", 16)
            w1_loads = {}
            umm_done = {}
            gelu_done = {}

            def load_w1(ts, g):
                par = g % 2
                dst = POOL[:, OF_W1 + par * 4096: OF_W1 + par * 4096 + 4096]
                prevkey = (ts, g - 2) if g >= 2 else ((ts - 1, g + 6) if ts > 0 else None)
                sem = "g16" if par == 0 else "g16b"
                if prevkey is not None and prevkey in umm_done:
                    p.wait(nc.gpsimd, "te", umm_done[prevkey])
                p.wait(nc.gpsimd, sem)
                w1_loads[(ts, g)] = (sem, p.inc(nc.gpsimd.dma_start(dst, w1_h[g]), sem, 16))

            ln_dve_next = [None]
            obank_ev = {}
            for ts in range(NTS):
                par_ts = ts % 2
                hoff = lambda c: par_ts * 4096 + c * 512
                if ts == 0:
                    ln_dve = emit_layernorm(0, BIG2, hoff, OF_XB, OF_XSQ)
                    load_w1(0, 0)
                    load_w1(0, 1)
                else:
                    ln_dve = ln_dve_next[0]
                ln_next = None
                if ts + 1 < NTS:
                    hoff_n = (lambda pn: (lambda c: pn * 4096 + c * 512))((ts + 1) % 2)
                    ln_next = emit_ln_ffn(ts + 1, BIG2, hoff_n, OF_XB, OF_XSQ)
                ubanks = [(PS_A, 0), (PS_B, 0), (PS_A, 512), (PS_B, 512)]
                for uc in range(NUC):
                    if ln_next is not None:
                        if uc == 16:
                            ln_next[0]()
                        elif uc == 28:
                            ln_next[1]()
                    g, j = uc // 4, uc % 4
                    PSx, ucoff = ubanks[uc % 4]
                    p.wait(nc.tensor, w1_loads[(ts, g)][0])
                    if uc == 0:
                        p.wait(nc.tensor, "dve", ln_dve)
                    if uc < 4 and obank_ev.get(uc % 3) is not None:
                        p.wait(nc.tensor, "dve", obank_ev[uc % 3])
                    prevk = (ts, uc - 4) if uc >= 4 else \
                        ((ts - 1, uc + 28) if ts > 0 else None)
                    if prevk is not None and prevk in gelu_done:
                        p.wait(nc.tensor, "act", gelu_done[prevk])
                    wg = POOL[:, OF_W1 + (g % 2) * 4096: OF_W1 + (g % 2) * 4096 + 4096]
                    for dc in range(NCH):
                        tt = nc.tensor.matmul(
                            PSx[:, ucoff:ucoff + 512],
                            wg[:, dc * 512 + j * 128: dc * 512 + j * 128 + 128],
                            BIG2[:, hoff(dc):hoff(dc) + 512],
                            start=(dc == 0), stop=(dc == NCH - 1))
                    umm_done[(ts, g)] = p.inc(tt, "te", 1)
                    if j == 3 and g + 2 < 8:
                        load_w1(ts, g + 2)
                    p.wait(nc.scalar, "te", umm_done[(ts, g)])
                    gelu_done[(ts, uc)] = sop("activation",
                                              BIG1[:, uc * 512:uc * 512 + 512],
                                              PSx[:, ucoff:ucoff + 512], AF.Gelu,
                                              bias=b_b1(uc))
                if ts + 1 < NTS:
                    load_w1(ts + 1, 0)
                    load_w1(ts + 1, 1)
                obanks = [(PS_A, 512), (PS_B, 512), (PS_D, 0)]
                ev = {}
                p.wait(nc.tensor, "act", gelu_done[(ts, NUC - 1)])
                if ts == 0:
                    p.wait(nc.tensor, "gw2", ld_w2)
                    p.wait(nc.tensor, "dve")
                for fo in range(NCH):
                    PSx, coff = obanks[fo % 3]
                    if obank_ev.get(fo % 3) is not None:
                        p.wait(nc.tensor, "dve", obank_ev[fo % 3])
                    for uc in range(NUC):
                        tt = nc.tensor.matmul(
                            PSx[:, coff:coff + 512],
                            POOL[:, OF_W2 + uc * 1024 + fo * 128:
                                 OF_W2 + uc * 1024 + fo * 128 + 128],
                            BIG1[:, uc * 512:uc * 512 + 512],
                            start=(uc == 0), stop=(uc == NUC - 1))
                    t = p.inc(tt, "te", 1)
                    p.wait(nc.vector, "te", t)
                    xs = X[:, fo * T + ts * QW: fo * T + ts * QW + QW]
                    ev[fo] = vop("scalar_tensor_tensor", xs, PSx[:, coff:coff + 512],
                                 b_b2(fo), xs, ALU.add, ALU.add)
                    obank_ev[fo % 3] = ev[fo]
                    if ln_next is not None:
                        ln_next[2](fo)
                if ln_next is not None:
                    ln_dve_next[0] = p.cnt["dve"]
            return p.cnt["dve"]

        ln1_ts = emit_layernorm_exp(list(range(NTS)), BIG1, OA_XSQ2,
                                    wait_s16=x_loaded)
        h1_done = p.cnt["dve"]

        if upto >= 1:
            p.wait(nc.tensor, "g16", aw_loaded)
            st1_ev = {}
            for sl in range(4):
                for wi, (woff, ooff) in enumerate(((OA_WQK, OA_QK64),
                                                   (OA_WVV, OA_VDUP))):
                    gi = sl * 2 + wi
                    PSx = PS_A if gi % 2 == 0 else PS_B
                    p.wait(nc.tensor, "dve", ln1_ts[sl])
                    if gi >= 2:
                        p.wait(nc.tensor, "act", st1_ev[gi - 2])
                    for c in range(NCH):
                        tt = nc.tensor.matmul(
                            PSx[:, 0:512],
                            POOL[:, woff + c * 128: woff + c * 128 + 128],
                            BIG1[:, c * T + sl * 512: c * T + sl * 512 + 512],
                            start=(c == 0), stop=(c == NCH - 1))
                    t = p.inc(tt, "te", 1)
                    p.wait(nc.scalar, "te", t)
                    st1_ev[gi] = sop("activation",
                                     POOL[:, ooff + sl * 512: ooff + sl * 512 + 512],
                                     PSx[:, 0:512], AF.Identity)

            k_ev = {}
            v_ev = {}
            p.wait(nc.tensor, "act")
            for c in range(NCH):
                for sl in range(4):
                    gi = c * 4 + sl
                    PSx = PS_A if sl % 2 == 0 else PS_B
                    if gi >= 2:
                        p.wait(nc.tensor, "dve", k_ev[gi - 2])
                    lt = POOL[64:128, OA_FCQK + c * 128: OA_FCQK + c * 128 + 128]
                    rh = POOL[64:128, OA_QK64 + sl * 512: OA_QK64 + sl * 512 + 512]
                    t = p.inc(nc.tensor.matmul(PSx[:, 512:1024], lt, rh, start=True,
                                               stop=True, tile_position=(64, 0)), "te", 1)
                    p.wait(nc.vector, "te", t)
                    k_ev[c * 4 + sl] = vop(
                        "tensor_scalar_add",
                        KAR[:, c * T + sl * 512: c * T + sl * 512 + 512],
                        PSx[:, 512:1024], b_fck(c))
                p.wait(nc.vector, "act")
                emit_knorm(c, OA_SQ)
                for sl in range(4):
                    gi = c * 4 + sl
                    PSx = PS_A if sl % 2 == 0 else PS_B
                    if gi >= 2:
                        p.wait(nc.tensor, "act", v_ev[gi - 2])
                    if c % 2 == 0:
                        lt = POOL[0:64, OA_FCVV + c * 128: OA_FCVV + c * 128 + 128]
                        rh = POOL[0:64, OA_VDUP + sl * 512: OA_VDUP + sl * 512 + 512]
                        tp = (0, 0)
                    else:
                        lt = POOL[64:128, OA_FCVV + c * 128: OA_FCVV + c * 128 + 128]
                        rh = POOL[64:128, OA_VDUP + sl * 512: OA_VDUP + sl * 512 + 512]
                        tp = (64, 0)
                    t = p.inc(nc.tensor.matmul(PSx[:, 0:512], lt, rh, start=True,
                                               stop=True, tile_position=tp), "te", 1)
                    p.wait(nc.scalar, "te", t)
                    v_ev[c * 4 + sl] = sop(
                        "activation", POOL[:, OA_VTS + sl * 512: OA_VTS + sl * 512 + 512],
                        PSx[:, 0:512], AF.Identity, bias=b_fcv(c))
                p.wait(nc.tensor, "act")
                emit_vtranspose(OA_VTS, c)
            kv_done = p.cnt["dve"]
            kv_te = p.cnt["te"]

            for g2 in (0, 2):
                p.wait(nc.tensor, "dve")
                q_ev = {}
                for qq in (g2, g2 + 1):
                    for c in range(NCH):
                        gi = (qq - g2) * NCH + c
                        PSx = PS_A if gi % 2 == 0 else PS_B
                        if gi >= 2:
                            p.wait(nc.tensor, "act", q_ev[gi - 2])
                        lt = POOL[0:64, OA_FCQK + c * 128: OA_FCQK + c * 128 + 128]
                        rh = POOL[0:64, OA_QK64 + qq * 512: OA_QK64 + qq * 512 + 512]
                        t = p.inc(nc.tensor.matmul(PSx[:, 0:512], lt, rh, start=True,
                                                   stop=True, tile_position=(0, 0)),
                                  "te", 1)
                        p.wait(nc.scalar, "te", t)
                        q_ev[gi] = sop(
                            "activation",
                            BIG2[:, (qq % 2) * 4096 + c * 512:
                                 (qq % 2) * 4096 + c * 512 + 512],
                            PSx[:, 0:512], AF.Identity, bias=b_fcq(c))
                p.wait(nc.vector, "act")
                eb = emit_qnorm2_and_ebias(OA_SQ)
                for qq in (g2, g2 + 1):
                    od, os16 = emit_pairloop(qq, (qq % 2) * 4096, BIG1, OA_PSB,
                                             OA_BNC, OA_XSQ, eb,
                                             kv_done if qq == 0 else p.cnt["dve"],
                                             flush_tail=(qq == 3))
            emit_outproj_all(BIG1, fco_h, OA_FCO, b_fco, od, os16, kv_te)
            p.barrier()
        if debug:
            p.barrier()
            for c in range(NCH):
                p.inc(nc.gpsimd.dma_start(dbgh[c * 128:(c + 1) * 128, :],
                                          BIG1[:, c * T:(c + 1) * T],
                                          max_dma_last_dim=2048), "g16", 16)
            p.barrier()

        if upto >= 2:
            emit_ffn()
            p.barrier()

        if upto >= 3:
            vop("memset", POOL[:, P_VAR:P_VAR + 16896]
                .rearrange("p (kc h f) -> p kc h f", h=16, f=66)[:, :, :, 64:66], 1.0)
            ln3_ts = emit_layernorm_exp(list(range(NTS)), BIG1, OA_XSQ2)
            h3_done = p.cnt["dve"]

            kv_loads = {}

            def load_kv(c):
                pr = c % 2
                if c == 0:
                    p.wait(nc.gpsimd, "dve", h3_done)
                kdst = POOL[:, OB_KV + pr * 1024: OB_KV + pr * 1024 + 1024]
                vdst = POOL[:, OB_KV + 2048 + pr * 1024: OB_KV + 2048 + pr * 1024 + 1024]
                sem = "g16" if pr == 0 else "g16b"
                if c >= 2:
                    p.wait(nc.gpsimd, "te", kv_loads[c - 2][2])
                p.wait(nc.gpsimd, sem)
                a = p.inc(nc.gpsimd.dma_start(kdst, wqkv_kv[c, 0]), sem, 16)
                b = p.inc(nc.gpsimd.dma_start(vdst, wqkv_kv[c, 1]), sem, 16)
                kv_loads[c] = [sem, b, None]

            load_kv(0)
            load_kv(1)
            kvb_ev = {}
            for c in range(NCH):
                pr = c % 2
                kt_ = POOL[:, OB_KV + pr * 1024: OB_KV + pr * 1024 + 1024]
                vt_ = POOL[:, OB_KV + 2048 + pr * 1024: OB_KV + 2048 + pr * 1024 + 1024]
                p.wait(nc.tensor, kv_loads[c][0])
                for which, wt in (("k", kt_), ("v", vt_)):
                    for sl in range(4):
                        gi = c * 8 + (4 if which == "v" else 0) + sl
                        PSx = PS_A if sl % 2 == 0 else PS_B
                        if c == 0:
                            p.wait(nc.tensor, "dve", ln3_ts[sl])
                        if gi >= 2:
                            pv_ = kvb_ev[gi - 2]
                            if isinstance(pv_, tuple):
                                p.wait(nc.tensor, pv_[0], pv_[1])
                            else:
                                p.wait(nc.tensor, "dve", pv_)
                        for dc in range(NCH):
                            tt = nc.tensor.matmul(
                                PSx[:, 0:512], wt[:, dc * 128:(dc + 1) * 128],
                                BIG1[:, dc * T + sl * 512: dc * T + sl * 512 + 512],
                                start=(dc == 0), stop=(dc == NCH - 1))
                        t = p.inc(tt, "te", 1)
                        if which == "k":
                            p.wait(nc.vector, "te", t)
                            kvb_ev[gi] = vop(
                                "tensor_copy",
                                KAR[:, c * T + sl * 512: c * T + sl * 512 + 512],
                                PSx[:, 0:512])
                        else:
                            p.wait(nc.scalar, "te", t)
                            kvb_ev[gi] = ("act", sop(
                                "activation",
                                POOL[:, OB_VTS + sl * 512: OB_VTS + sl * 512 + 512],
                                PSx[:, 0:512], AF.Identity))
                kv_loads[c][2] = p.cnt["te"]
                if c + 2 < NCH:
                    load_kv(c + 2)
                emit_knorm(c, OB_SQ)
                emit_vtranspose(OB_VTS, c)
            kvb_done = p.cnt["dve"]
            kvb_te = p.cnt["te"]

            q_loads = {}

            def load_q(qq2, c):
                pr = c % 2
                sem = "g16" if pr == 0 else "g16b"
                dst = POOL[:, OB_QROT + pr * 1024: OB_QROT + pr * 1024 + 1024]
                prevkey = (qq2, c - 2) if c >= 2 else ((qq2 - 2, c + 6) if qq2 > 0 else None)
                if prevkey is not None and prevkey in q_loads:
                    p.wait(nc.gpsimd, "te", q_loads[prevkey][1])
                p.wait(nc.gpsimd, sem)
                q_loads[(qq2, c)] = [sem,
                                     p.inc(nc.gpsimd.dma_start(dst, wqkv_q[c]), sem, 16)]

            load_q(0, 0)
            load_q(0, 1)
            for g2 in (0, 2):
                p.wait(nc.tensor, "dve")
                qb_ev = {}
                for c in range(NPAIR):
                    pr = c % 2
                    PSx = PS_A if pr == 0 else PS_B
                    wt = POOL[:, OB_QROT + pr * 1024: OB_QROT + pr * 1024 + 1024]
                    p.wait(nc.tensor, q_loads[(g2, c)][0])
                    if c >= 2:
                        p.wait(nc.tensor, "dve", qb_ev[c - 2])
                    for q2 in (g2, g2 + 1):
                        coff2 = (q2 % 2) * 512
                        for dc in range(NCH):
                            tt = nc.tensor.matmul(
                                PSx[:, coff2:coff2 + 512],
                                wt[:, dc * 128:(dc + 1) * 128],
                                BIG1[:, dc * T + q2 * 512: dc * T + q2 * 512 + 512],
                                start=(dc == 0), stop=(dc == NCH - 1))
                    t = p.inc(tt, "te", 1)
                    q_loads[(g2, c)][1] = t
                    if c + 2 < NPAIR:
                        load_q(g2, c + 2)
                    elif g2 == 0:
                        load_q(2, c - 6)
                    p.wait(nc.vector, "te", t)
                    if c == 0:
                        p.wait(nc.vector, "act")
                    vop("tensor_copy",
                        BIG2[:, 0 + c * 512: 0 + c * 512 + 512], PSx[:, 0:512])
                    qb_ev[c] = vop("tensor_copy",
                                   BIG2[:, 4096 + c * 512: 4096 + c * 512 + 512],
                                   PSx[:, 512:1024])
                eb = emit_qnorm2_and_ebias(OB_SQ)
                for qq in (g2, g2 + 1):
                    od, os16 = emit_pairloop(qq, (qq % 2) * 4096, BIG1, OB_PSB,
                                             OB_BNC, OB_XSQ, eb,
                                             kvb_done if qq == 0 else p.cnt["dve"],
                                             flush_tail=(qq == 3))
            emit_outproj_all(BIG1, wproj_h, OB_WPROJ, b_proj, od, os16, kvb_te)
            p.barrier()

        if upto >= 4:
            emit_ffn()
            p.barrier()

        p.wait(nc.sync, "dve")
        for c in range(NCH):
            p.inc(nc.sync.dma_start(yT[c * 128:(c + 1) * 128, :],
                                    X[:, c * T:(c + 1) * T]), "s16", 16)
        p.wait(nc.sync, "s16")
    return nc



def _pack_col_tiles(a):
    w = a.shape[1]
    return np.ascontiguousarray(
        a.reshape(8, 128, w).transpose(1, 0, 2).reshape(128, 8 * w))


def host_prep(inp):
    f = np.float32
    import ml_dtypes
    z = np.asarray(inp["L0_attn_a_z"], f)
    wq, wk, wv = (np.asarray(inp[k], f) for k in
                  ("L0_attn_a_wq", "L0_attn_a_wk", "L0_attn_a_wv"))
    fcq_w, fck_w, fcv_w = (np.asarray(inp[k], f) for k in
                           ("L0_attn_a_fcq_w", "L0_attn_a_fck_w", "L0_attn_a_fcv_w"))
    fco_w = np.asarray(inp["L0_attn_a_fco_w"], f)
    w_qkv = np.asarray(inp["L0_attn_b_w_qkv"], f)
    w_proj = np.asarray(inp["L0_attn_b_w_proj"], f)
    w1 = np.asarray(inp["L0_ffn_w1"], f)
    w2 = np.asarray(inp["L0_ffn_w2"], f)

    shared = {}
    shared["wqk"] = _pack_col_tiles(np.concatenate([wq, wk], axis=1))
    shared["wvv"] = _pack_col_tiles(np.concatenate([wv, wv], axis=1))
    shared["fcqk"] = np.ascontiguousarray(np.concatenate([fcq_w, fck_w], axis=0))
    shared["fcvv"] = np.ascontiguousarray(np.concatenate([fcv_w, fcv_w], axis=0))
    shared["fco_h"] = np.ascontiguousarray(
        fco_w.reshape(8, 128, 8, 128).transpose(2, 1, 0, 3).reshape(8, 128, 1024))
    kv = np.stack([
        np.stack([_pack_col_tiles(w_qkv[:, 1024 + c * 128:1024 + (c + 1) * 128]),
                  _pack_col_tiles(w_qkv[:, 2048 + c * 128:2048 + (c + 1) * 128])])
        for c in range(8)])
    shared["wqkv_kv"] = np.ascontiguousarray(kv)
    shared["wqkv_q"] = np.ascontiguousarray(
        np.stack([_pack_col_tiles(w_qkv[:, c * 128:(c + 1) * 128]) for c in range(8)]))
    shared["wproj_h"] = np.ascontiguousarray(
        w_proj.reshape(8, 128, 8, 128).transpose(2, 1, 0, 3).reshape(8, 128, 1024))
    shared["w1_h"] = np.ascontiguousarray(
        np.stack([_pack_col_tiles(w1[:, g * 512:(g + 1) * 512]) for g in range(8)]))
    shared["w2_h"] = np.ascontiguousarray(
        w2.reshape(32, 128, 8, 128).transpose(1, 0, 2, 3).reshape(128, 32768))
    bc = np.zeros((128, 80), f)
    bc[:, 0:8] = np.asarray(inp["L0_attn_a_fcq_b"], f).reshape(8, 128).T
    bc[:, 8:16] = np.asarray(inp["L0_attn_a_fck_b"], f).reshape(8, 128).T
    bc[:, 16:24] = np.asarray(inp["L0_attn_a_fcv_b"], f).reshape(8, 128).T
    bc[:, 24:32] = np.asarray(inp["L0_attn_a_fco_b"], f).reshape(8, 128).T
    bc[:, 32:40] = np.asarray(inp["L0_attn_b_b_proj"], f).reshape(8, 128).T
    bc[:, 40:48] = np.asarray(inp["L0_ffn_b2"], f).reshape(8, 128).T
    bc[:, 48:80] = np.asarray(inp["L0_ffn_b1"], f).reshape(32, 128).T
    shared["biascols"] = bc
    shared["identb"] = np.eye(128, dtype=ml_dtypes.bfloat16)

    in_maps = []
    for c in range(z.shape[0]):
        m = dict(shared)
        m["xT"] = np.ascontiguousarray(z[c].T)
        in_maps.append(m)
    return in_maps



from concourse.bass_utils import run_bass_kernel_spmd


def kernel(**inputs):
    in_maps = host_prep(inputs)
    nc = build(upto=4, debug=0)
    res = run_bass_kernel_spmd(nc, in_maps, core_ids=list(range(8)))
    out = np.stack([np.asarray(res.results[c]["yT"]).T for c in range(8)])
    return np.ascontiguousarray(out.astype(np.float32))


# version 25
# speedup vs baseline: 1.1624x; 1.1624x over previous
import numpy as np
from contextlib import ExitStack

import concourse.bass as bass
import concourse.mybir as mybir

F32 = mybir.dt.float32
BF16 = mybir.dt.bfloat16
AF = mybir.ActivationFunctionType
ALU = mybir.AluOpType
AX = mybir.AxisListType

T = 2048
D = 1024
NCH = 8
NPAIR = 8
NQQ = 4
QW = 512
KC = 16
NUC = 32
NTS = 4
EPS = 1e-5

P_KAR = 0
P_VAR = 16384
VSTR = 1056
MB = 33280
OA_QK64 = MB + 0
OA_VDUP = MB + 2048
OA_PSB = MB + 2048
OA_WQK = MB + 4096
OA_WVV = MB + 5120
OA_VTS = MB + 4096
OA_FCO = MB + 4096
OA_FCQK = MB + 6144
OA_FCVV = MB + 7168
OA_BNC = MB + 7168
OA_XB = MB + 2048
OA_XSQ2 = MB + 3072
OA_SQ = MB + 9216
OA_XSQ = MB + 8192
OB_KV = MB + 0
OB_PSB = MB + 0
OB_QROT = MB + 4096
OB_VTS = MB + 6144
OB_WPROJ = MB + 6144
OB_BNC = MB + 2048
OB_XB = MB + 2048
OB_SQ = MB + 9216
OB_XSQ = MB + 3072
OF_W2 = 0
OF_W1 = 32768
OF_XB = 40960
OF_XSQ = 41984
POOL_COLS = 43008


class Prog:
    def __init__(self, nc):
        self.nc = nc
        self._stack = None
        self.sems = {}
        self.cnt = {}

    def sem(self, name):
        s = self._stack.enter_context(self.nc.semaphore(name))
        self.sems[name] = s
        self.cnt[name] = 0
        return s

    def inc(self, instr, name, n=1):
        instr.then_inc(self.sems[name], n)
        self.cnt[name] += n
        return self.cnt[name]

    def wait(self, eng, name, v=None):
        eng.wait_ge(self.sems[name], self.cnt[name] if v is None else v)

    def barrier(self):
        nc = self.nc
        for eng in (nc.tensor, nc.scalar, nc.vector, nc.sync, nc.gpsimd):
            for nm in ("te", "act", "dve", "s16", "s16b", "g16", "g16b",
                       "gw2"):
                self.wait(eng, nm)


def build(upto=4, debug=0):
    nc = bass.Bass("TRN2", target_bir_lowering=False)
    p = Prog(nc)

    xT = nc.dram_tensor("xT", [D, T], F32, kind="ExternalInput").ap()
    wqk_d = nc.dram_tensor("wqk", [128, 1024], F32, kind="ExternalInput").ap()
    wvv_d = nc.dram_tensor("wvv", [128, 1024], F32, kind="ExternalInput").ap()
    fcqk_d = nc.dram_tensor("fcqk", [128, 1024], F32, kind="ExternalInput").ap()
    fcvv_d = nc.dram_tensor("fcvv", [128, 1024], F32, kind="ExternalInput").ap()
    fco_h = nc.dram_tensor("fco_h", [8, 128, 1024], F32, kind="ExternalInput").ap()
    wqkv_kv = nc.dram_tensor("wqkv_kv", [8, 2, 128, 1024], F32, kind="ExternalInput").ap()
    wqkv_q = nc.dram_tensor("wqkv_q", [8, 128, 1024], F32, kind="ExternalInput").ap()
    wproj_h = nc.dram_tensor("wproj_h", [8, 128, 1024], F32, kind="ExternalInput").ap()
    w1_h = nc.dram_tensor("w1_h", [8, 128, 4096], F32, kind="ExternalInput").ap()
    w2_h = nc.dram_tensor("w2_h", [128, 32768], F32, kind="ExternalInput").ap()
    biascols = nc.dram_tensor("biascols", [128, 80], F32, kind="ExternalInput").ap()
    identb_d = nc.dram_tensor("identb", [128, 128], BF16, kind="ExternalInput").ap()
    yT = nc.dram_tensor("yT", [D, T], F32, kind="ExternalOutput").ap()
    if debug:
        dbgh = nc.dram_tensor("dbgh", [D, T], F32, kind="ExternalOutput").ap()

    with ExitStack() as ctx:
        p._stack = ctx
        sbt = lambda nm, shp, dt: ctx.enter_context(nc.sbuf_tensor(nm, shp, dt))
        pst = lambda nm, shp, dt: ctx.enter_context(nc.psum_tensor(nm, shp, dt))
        for nm in ("te", "act", "dve", "s16", "s16b", "g16", "g16b", "gw2"):
            p.sem(nm)

        X = sbt("X", [128, NCH * T], F32)
        BIG1 = sbt("BIG1", [128, 16384], BF16)
        BIG2 = sbt("BIG2", [128, 8192], BF16)
        POOL = sbt("POOL", [128, POOL_COLS], BF16)
        identb = sbt("identb_s", [128, 128], BF16)
        ones_b = sbt("ones_b", [128, 128], BF16)
        ones_f = sbt("ones_f", [128, 128], F32)
        bcols = sbt("bcols", [128, 80], F32)
        ebias = sbt("ebias", [128, 8], F32)
        sstat = sbt("sstat", [128, 32], F32)
        SCR = sbt("SCR", [128, 1024], F32)
        PF = sbt("PF", [128, 1024], F32)
        epsb = sbt("epsb", [1, 1], F32)

        PS_A = pst("PS_A", [128, 1024], F32)
        PS_B = pst("PS_B", [128, 1024], F32)
        PS_C = pst("PS_C", [128, 1024], F32)
        PS_D = pst("PS_D", [128, 512], F32)
        PS_E = pst("PS_E", [128, 512], F32)

        KAR = POOL[:, P_KAR:P_KAR + 16384]

        def vop(name, *a, **k):
            p.wait(nc.vector, "dve")
            return p.inc(getattr(nc.vector, name)(*a, **k), "dve", 1)

        def sop(name, *a, **k):
            p.wait(nc.scalar, "act")
            return p.inc(getattr(nc.scalar, name)(*a, **k), "act", 1)

        MEAN = SCR[0:1, 0:512]
        VARV = SCR[0:1, 512:1024]
        TSC = SCR[:, 512:1024]
        RECIP0 = PF[64:65, 0:512]
        RECIP1 = PF[64:65, 512:1024]
        S_QMAX = lambda c: sstat[0:1, 0 + c:1 + c]
        S_KMAX = lambda c: sstat[0:1, 8 + c:9 + c]
        S_KSL = lambda s: sstat[0:1, 16 + s:17 + s]
        S_EROW = sstat[0:1, 24:32]

        b_fcq = lambda c: bcols[:, 0 + c:1 + c]
        b_fck = lambda c: bcols[:, 8 + c:9 + c]
        b_fcv = lambda c: bcols[:, 16 + c:17 + c]
        b_fco = lambda c: bcols[:, 24 + c:25 + c]
        b_proj = lambda c: bcols[:, 32 + c:33 + c]
        b_b2 = lambda c: bcols[:, 40 + c:41 + c]
        b_b1 = lambda uc: bcols[:, 48 + uc:49 + uc]

        vop("memset", ones_b[:, :], 1.0)
        vop("memset", ones_f[:, :], 1.0)
        vop("memset", epsb[:, :], EPS)
        vop("memset", SCR[0:64, 0:512], 1.0)
        vop("memset", POOL[:, P_VAR:P_VAR + 16896], 1.0)

        p.inc(nc.sync.dma_start(identb[:, :], identb_d), "s16", 16)
        p.inc(nc.sync.dma_start(bcols[:, :], biascols), "s16", 16)
        p.wait(nc.vector, "s16")
        p.wait(nc.scalar, "s16")
        p.wait(nc.tensor, "s16")
        x_chunk = []
        for c in range(NCH):
            sem = p.sem("xk%d" % c)
            eng = nc.sync if c % 2 == 0 else nc.scalar
            p.inc(eng.dma_start(X[:, c * T:(c + 1) * T],
                                xT[c * 128:(c + 1) * 128, :]), "xk%d" % c, 16)
            x_chunk.append(("xk%d" % c, 16))

        p.inc(nc.gpsimd.dma_start(POOL[:, OA_WQK:OA_WQK + 1024], wqk_d), "g16", 16)
        p.inc(nc.gpsimd.dma_start(POOL[:, OA_WVV:OA_WVV + 1024], wvv_d), "g16", 16)
        p.inc(nc.gpsimd.dma_start(POOL[:, OA_FCQK:OA_FCQK + 1024], fcqk_d), "g16", 16)
        p.inc(nc.gpsimd.dma_start(POOL[:, OA_FCVV:OA_FCVV + 1024], fcvv_d), "g16", 16)
        aw_loaded = p.cnt["g16"]


        def emit_layernorm(ts, dst, dst_off, xb_off, xsq_off, wait_s16=None):
            tcol = lambda c: c * T + ts * QW
            p.wait(nc.tensor, "dve")
            mmd = {}
            for c in range(NCH):
                par = c % 2
                xb = POOL[:, xb_off + par * 512:xb_off + par * 512 + 512]
                xsq = POOL[:, xsq_off + par * 512:xsq_off + par * 512 + 512]
                if wait_s16 is not None and c == 0:
                    p.wait(nc.vector, "s16", wait_s16)
                if c >= 2:
                    p.wait(nc.vector, "te", mmd[c - 2])
                vop("tensor_copy", xb, X[:, tcol(c):tcol(c) + QW])
                v = vop("tensor_mul", xsq, xb, xb)
                p.wait(nc.tensor, "dve", v)
                nc.tensor.matmul(PS_C[0:1, 0:512], ones_b[:, 0:1], xb,
                                 start=(c == 0), stop=(c == NCH - 1))
                mmd[c] = p.inc(nc.tensor.matmul(PS_C[0:1, 512:1024], ones_b[:, 0:1],
                                                xsq,
                                                start=(c == 0), stop=(c == NCH - 1)),
                               "te", 1)
            p.wait(nc.vector, "te")
            vop("tensor_scalar_mul", MEAN, PS_C[0:1, 0:512], 1.0 / D)
            vop("tensor_mul", VARV, MEAN, MEAN)
            v = vop("scalar_tensor_tensor", VARV, PS_C[0:1, 512:1024], 1.0 / D,
                    VARV, ALU.mult, ALU.subtract)
            p.wait(nc.scalar, "dve", v)
            sop("activation", VARV, VARV, AF.Ln, bias=epsb[0:1, 0:1])
            a = sop("activation", VARV, VARV, AF.Exp, scale=-0.5)
            p.wait(nc.tensor, "act", a)
            nc.tensor.matmul(PS_C[:, 0:512], ones_f[0:1, :], MEAN, start=True, stop=True)
            t = p.inc(nc.tensor.matmul(PS_C[:, 512:1024], ones_f[0:1, :], VARV,
                                       start=True, stop=True), "te", 1)
            p.wait(nc.vector, "te", t)
            for c in range(NCH):
                vop("tensor_sub", TSC, X[:, tcol(c):tcol(c) + QW], PS_C[:, 0:512])
                vop("tensor_mul", dst[:, dst_off(c):dst_off(c) + QW],
                    TSC, PS_C[:, 512:1024])
            return p.cnt["dve"]

        def emit_layernorm_exp(ts_list, dst, xsq_off, wait_s16=None):
            tcol2 = lambda ts, c: c * T + ts * QW
            mmd = {}

            def stats(i, ts):
                if i % 2 == 0:
                    sum_dst, sq_dst = PS_D[0:1, 0:512], PS_E[0:1, 0:512]
                else:
                    sum_dst, sq_dst = PS_A[0:1, 0:512], PS_A[0:1, 512:1024]
                for c in range(NCH):
                    gi = i * NCH + c
                    xsq = POOL[:, xsq_off + (gi % 2) * 512:
                               xsq_off + (gi % 2) * 512 + 512]
                    if i == 0 and wait_s16 is not None:
                        sem_, v_ = wait_s16[c]
                        p.wait(nc.scalar, sem_, v_)
                        p.wait(nc.tensor, sem_, v_)
                    if i == 0 and c == 0:
                        p.wait(nc.scalar, "dve")
                        p.wait(nc.tensor, "dve")
                    if gi >= 2 and (gi - 2) in mmd:
                        p.wait(nc.scalar, "te", mmd[gi - 2])
                    sq = sop("activation", xsq,
                             X[:, tcol2(ts, c):tcol2(ts, c) + QW], AF.Square)
                    nc.tensor.matmul(sum_dst, ones_f[:, 0:1],
                                     X[:, tcol2(ts, c):tcol2(ts, c) + QW],
                                     start=(c == 0), stop=(c == NCH - 1))
                    p.wait(nc.tensor, "act", sq)
                    mmd[gi] = p.inc(
                        nc.tensor.matmul(sq_dst, ones_b[:, 0:1], xsq,
                                         start=(c == 0), stop=(c == NCH - 1)),
                        "te", 1)

            def finalize_apply(i, ts):
                if i % 2 == 0:
                    sum_row, sq_row = PS_D[0:1, 0:512], PS_E[0:1, 0:512]
                else:
                    sum_row, sq_row = PS_A[0:1, 0:512], PS_A[0:1, 512:1024]
                p.wait(nc.vector, "te", mmd[i * NCH + NCH - 1])
                vop("tensor_scalar_mul", MEAN, sum_row, 1.0 / D)
                vop("tensor_mul", VARV, MEAN, MEAN)
                v = vop("scalar_tensor_tensor", VARV, sq_row, 1.0 / D,
                        VARV, ALU.mult, ALU.subtract)
                p.wait(nc.scalar, "dve", v)
                sop("activation", VARV, VARV, AF.Ln, bias=epsb[0:1, 0:1])
                a = sop("activation", VARV, VARV, AF.Exp, scale=-0.5)
                p.wait(nc.tensor, "act", a)
                p.wait(nc.tensor, "dve")
                nc.tensor.matmul(PS_C[:, 0:512], ones_f[0:1, :], MEAN,
                                 start=True, stop=True)
                t = p.inc(nc.tensor.matmul(PS_C[:, 512:1024], ones_f[0:1, :],
                                           VARV, start=True, stop=True), "te", 1)
                p.wait(nc.vector, "te", t)
                for c in range(NCH):
                    vop("tensor_sub", TSC, X[:, tcol2(ts, c):tcol2(ts, c) + QW],
                        PS_C[:, 0:512])
                    vop("tensor_mul", dst[:, tcol2(ts, c):tcol2(ts, c) + QW],
                        TSC, PS_C[:, 512:1024])

            stats(0, ts_list[0])
            out = []
            for i, ts in enumerate(ts_list):
                if i + 1 < len(ts_list):
                    stats(i + 1, ts_list[i + 1])
                finalize_apply(i, ts)
                out.append(p.cnt["dve"])
            return out

        def emit_vtranspose(vts_off, cpair):
            p.wait(nc.tensor, "dve")
            p.wait(nc.tensor, "act")
            round_ev = {}
            round_eng = {}
            for r4 in range(4):
                boff = (r4 % 2) * 512
                if r4 >= 2:
                    p.wait(nc.tensor, round_eng[r4 - 2], round_ev[r4 - 2])
                for j in range(4):
                    kt = r4 * 4 + j
                    psv = PS_C[:, boff + j * 64: boff + (j + 1) * 64].bitcast(BF16)
                    src = POOL[:, vts_off + kt * 128: vts_off + kt * 128 + 128]
                    t = p.inc(nc.tensor.transpose(psv, src, identb[:, :]), "te", 1)
                if r4 % 2 == 0:
                    p.wait(nc.vector, "te", t)
                else:
                    p.wait(nc.scalar, "te", t)
                for j in range(4):
                    kt = r4 * 4 + j
                    psv = PS_C[:, boff + j * 64: boff + (j + 1) * 64].bitcast(BF16)
                    dst = POOL[:, P_VAR + kt * VSTR + 132 * cpair:
                               P_VAR + kt * VSTR + 132 * cpair + 132]
                    da = dst.rearrange("p (h f) -> p h f", h=2)[:, :, 0:64]
                    sa = psv.rearrange("p (h f) -> p h f", h=2)
                    if r4 % 2 == 0:
                        v = vop("tensor_copy", da, sa)
                    else:
                        v = sop("activation", da, sa, AF.Identity)
                round_ev[r4] = v
                round_eng[r4] = "dve" if r4 % 2 == 0 else "act"
            return p.cnt["dve"]

        def emit_knorm(cpair, sq_off):
            p.wait(nc.scalar, "te")
            p.wait(nc.scalar, "dve")
            psl = [PS_D[0:1, 0:512], PS_E[0:1, 0:512]]
            tmm = {}
            mx = {}
            for sl in range(4):
                ksq = POOL[:, (sq_off if sl % 2 == 0 else MB + 8192):
                           (sq_off if sl % 2 == 0 else MB + 8192) + 512]
                kc_ap = KAR[:, cpair * T + sl * 512: cpair * T + sl * 512 + 512]
                if sl >= 2:
                    p.wait(nc.scalar, "te", tmm[sl - 2])
                sq = sop("activation", ksq, kc_ap, AF.Square)
                p.wait(nc.tensor, "act", sq)
                if sl >= 2:
                    p.wait(nc.tensor, "dve", mx[sl - 2])
                tmm[sl] = p.inc(nc.tensor.matmul(psl[sl % 2], ones_b[:, 0:1],
                                                 ksq, start=True, stop=True),
                                "te", 1)
                p.wait(nc.vector, "te", tmm[sl])
                mx[sl] = vop("reduce_max", S_KSL(sl), psl[sl % 2], axis=AX.X)
            vop("reduce_max", S_KMAX(cpair), sstat[0:1, 16:20], axis=AX.X)

        def emit_qnorm2_and_ebias(sq_off):
            p.wait(nc.scalar, "te")
            p.wait(nc.scalar, "dve")
            psl = [PS_D[0:1, 0:512], PS_E[0:1, 0:512]]
            tmm = {}
            mx = {}
            for c in range(NPAIR):
                for qqi in range(2):
                    gi = c * 2 + qqi
                    qsq = POOL[:, (sq_off if gi % 2 == 0 else MB + 8192):
                               (sq_off if gi % 2 == 0 else MB + 8192) + 512]
                    q_ap = BIG2[:, qqi * 4096 + c * 512: qqi * 4096 + c * 512 + 512]
                    if gi >= 2:
                        p.wait(nc.scalar, "te", tmm[gi - 2])
                    sq = sop("activation", qsq, q_ap, AF.Square)
                    p.wait(nc.tensor, "act", sq)
                    if gi >= 2:
                        p.wait(nc.tensor, "dve", mx[gi - 2])
                    tmm[gi] = p.inc(nc.tensor.matmul(psl[gi % 2], ones_b[:, 0:1],
                                                     qsq, start=True, stop=True),
                                    "te", 1)
                    p.wait(nc.vector, "te", tmm[gi])
                    mx[gi] = vop("reduce_max", sstat[0:1, 20 + qqi:21 + qqi],
                                 psl[gi % 2], axis=AX.X)
                vop("reduce_max", S_QMAX(c), sstat[0:1, 20:22], axis=AX.X)
            vop("tensor_add", S_EROW, sstat[0:1, 0:8], sstat[0:1, 8:16])
            v = vop("tensor_scalar_mul", S_EROW, S_EROW, -1.0 / 16.0)
            p.wait(nc.tensor, "dve", v)
            t = p.inc(nc.tensor.matmul(PS_D[:, 0:8], ones_f[0:1, :], S_EROW,
                                       start=True, stop=True), "te", 1)
            p.wait(nc.vector, "te", t)
            p.wait(nc.vector, "act")
            return vop("tensor_copy", ebias[:, :], PS_D[:, 0:8])

        roll = {"exp": {0: None, 1: None}, "pv": {0: None, 1: None}}

        prev_mul = {0: None, 1: None}
        pending = [None]

        def emit_pairloop(qq, qdst_off, o_dst, psb_off, bnc_off, raw_off, eb_dve,
                          o_free_dve, flush_tail=False):
            s_done_all = {}

            def obank(cp, h):
                if cp == 0:
                    return PS_C[:, h * 512:(h + 1) * 512]
                return (PS_D if h == 0 else PS_E)[:, 0:512]

            def emit_S_next(cn, kc):
                par = kc % 2
                PSx = PS_A if par == 0 else PS_B
                if roll["exp"][par] is not None:
                    p.wait(nc.tensor, "act", roll["exp"][par])
                lt0 = KAR[0:64, cn * T + kc * 128: cn * T + kc * 128 + 128]
                rh0 = BIG2[0:64, qdst_off + cn * 512: qdst_off + cn * 512 + 512]
                nc.tensor.matmul(PSx[:, 0:512], lt0, rh0, start=True, stop=True,
                                 tile_position=(0, 0))
                lt1 = KAR[64:128, cn * T + kc * 128: cn * T + kc * 128 + 128]
                rh1 = BIG2[64:128, qdst_off + cn * 512: qdst_off + cn * 512 + 512]
                s_done_all[(cn, kc)] = p.inc(
                    nc.tensor.matmul(PSx[:, 512:1024], lt1, rh1, start=True,
                                     stop=True, tile_position=(64, 0)), "te", 1)

            def flush_part(i):
                if pending[0] is None:
                    return
                rc_, raw0_, bncap_, ocols_, cp_, odst_ = pending[0]
                h = i // 2
                bank = obank(cp_, h)
                row = 0 if h == 0 else 32
                hilo = SCR[row:row + 1, 512:1024].bitcast(BF16)
                rhs = hilo[:, 0:512] if i % 2 == 0 else hilo[:, 512:1024]
                if i == 0:
                    p.wait(nc.tensor, "dve")
                p.inc(nc.tensor.matmul(bank[0:64, 0:512],
                                       ones_b[row:row + 1, 0:64], rhs,
                                       start=(i % 2 == 0), stop=(i % 2 == 1)),
                      "te", 1)

            def flush_pending():
                if pending[0] is None:
                    return
                rc_, raw0_, bncap_, ocols_, cp_, odst_ = pending[0]
                pending[0] = None
                p.wait(nc.vector, "te")
                vop("tensor_mul", odst_[0:64, ocols_:ocols_ + QW], raw0_,
                    obank(cp_, 0)[0:64, 0:512])
                bm = vop("tensor_mul", bncap_, bncap_, obank(cp_, 1)[0:64, 0:512])
                prev_mul[cp_] = bm
                p.wait(nc.sync, "dve", bm)
                p.inc(nc.sync.dma_start(odst_[64:128, ocols_:ocols_ + QW], bncap_),
                      "s16", 16)

            for c in range(NPAIR):
                exp_done = {}
                s_done = {}
                pv_done = {}

                def emit_S(kc):
                    if (c, kc) in s_done_all:
                        s_done[kc] = s_done_all.pop((c, kc))
                        return
                    emit_S_next(c, kc)
                    s_done[kc] = s_done_all.pop((c, kc))

                def emit_exp(kc):
                    par = kc % 2
                    PSx = PS_A if par == 0 else PS_B
                    psb = POOL[:, psb_off + par * 1024: psb_off + par * 1024 + 1024]
                    wv = s_done[kc]
                    if roll["pv"][par] is not None and roll["pv"][par] > wv:
                        wv = roll["pv"][par]
                    p.wait(nc.scalar, "te", wv)
                    if c == 0 and kc == 0:
                        p.wait(nc.scalar, "dve", eb_dve)
                    exp_done[kc] = p.inc(
                        nc.scalar.activation(psb, PSx[:, :], AF.Exp,
                                             bias=ebias[:, c:c + 1], scale=0.125),
                        "act", 1)
                    roll["exp"][par] = exp_done[kc]

                def emit_PV(kc):
                    par = kc % 2
                    cp = c % 2
                    psb = POOL[:, psb_off + par * 1024: psb_off + par * 1024 + 1024]
                    p.wait(nc.tensor, "act", exp_done[kc])
                    if kc == 0:
                        g = prev_mul[cp] if prev_mul[cp] is not None else 0
                        p.wait(nc.tensor, "dve", max(g, o_free_dve))
                    vbase = P_VAR + kc * VSTR + 132 * c
                    lt0 = POOL[:, vbase:vbase + 65]
                    nc.tensor.matmul(obank(cp, 0)[0:65, 0:512], lt0, psb[:, 0:512],
                                     start=(kc == 0), stop=(kc == KC - 1))
                    lt1 = POOL[:, vbase + 66:vbase + 66 + 65]
                    pv_done[kc] = p.inc(
                        nc.tensor.matmul(obank(cp, 1)[0:65, 0:512], lt1,
                                         psb[:, 512:1024],
                                         start=(kc == 0), stop=(kc == KC - 1)), "te", 1)
                    roll["pv"][par] = pv_done[kc]

                emit_S(0)
                for kc in range(KC):
                    emit_exp(kc)
                    if kc + 1 < KC:
                        emit_S(kc + 1)
                    if kc >= 1:
                        emit_PV(kc - 1)
                    if 9 <= kc <= 12:
                        flush_part(kc - 9)
                    if kc == 13:
                        flush_pending()
                    if kc == KC - 2 and c + 1 < NPAIR:
                        emit_S_next(c + 1, 0)
                    if kc == KC - 1 and c + 1 < NPAIR:
                        emit_S_next(c + 1, 1)
                emit_PV(KC - 1)

                p.wait(nc.vector, "te", pv_done[KC - 1])
                cp = c % 2
                raw0 = POOL[0:64, raw_off + cp * 512:raw_off + cp * 512 + 512]
                bncap = POOL[0:64, bnc_off + cp * 512:bnc_off + cp * 512 + 512]
                vop("tensor_copy", PF[0:1, 0:512], obank(cp, 0)[64:65, 0:512])
                hsums = vop("tensor_copy", PF[0:1, 512:1024],
                            obank(cp, 1)[64:65, 0:512])
                p.wait(nc.sync, "dve", hsums)
                p.inc(nc.sync.dma_start(SCR[0:1, 0:512], PF[0:1, 0:512]),
                      "s16", 16)
                p.inc(nc.sync.dma_start(SCR[32:33, 0:512], PF[0:1, 512:1024]),
                      "s16", 16)
                vop("tensor_copy", raw0, obank(cp, 0)[0:64, 0:512])
                vop("tensor_copy", bncap, obank(cp, 1)[0:64, 0:512])
                p.wait(nc.vector, "s16")
                vop("reciprocal", PF[0:33, 0:512], SCR[0:33, 0:512])
                hl = SCR[0:33, 512:1024].bitcast(BF16)
                vop("tensor_copy", hl[:, 0:512], PF[0:33, 0:512])
                vop("tensor_copy", PF[0:33, 512:1024], hl[:, 0:512])
                rc = vop("tensor_sub", hl[:, 512:1024], PF[0:33, 0:512],
                         PF[0:33, 512:1024])
                pending[0] = (rc, raw0, bncap, c * T + qq * QW, cp, o_dst)
            if flush_tail:
                for i in range(4):
                    flush_part(i)
                flush_pending()
            return p.cnt["dve"], p.cnt["s16"]

        def emit_outproj_all(o_src, w_dram, rot_off, bias_fn, o_dve, o_s16,
                             w_release_te):
            loads = {}
            te_done = {}
            ev_done = {}

            def load(fc):
                par = fc % 2
                sem = "g16" if par == 0 else "g16b"
                dst = POOL[:, rot_off + par * 1024: rot_off + par * 1024 + 1024]
                if fc >= 2:
                    p.wait(nc.gpsimd, "te", te_done[fc - 2])
                else:
                    p.wait(nc.gpsimd, "te", w_release_te)
                p.wait(nc.gpsimd, sem)
                loads[fc] = (sem, p.inc(nc.gpsimd.dma_start(dst, w_dram[fc]), sem, 16))

            load(0)
            load(1)
            banks = [(PS_A, 0), (PS_B, 0), (PS_A, 512), (PS_B, 512)]
            p.wait(nc.tensor, "act")
            for fc in range(NCH):
                par = fc % 2
                wt = POOL[:, rot_off + par * 1024: rot_off + par * 1024 + 1024]
                p.wait(nc.tensor, loads[fc][0])
                if fc == 0:
                    p.wait(nc.tensor, "dve", o_dve)
                    p.wait(nc.tensor, "s16", o_s16)
                for dc in range(NCH):
                    for qs in range(4):
                        PSx, coff = banks[qs]
                        if dc == 0 and ev_done.get(qs) is not None:
                            p.wait(nc.tensor, "dve", ev_done[qs])
                        tt = nc.tensor.matmul(
                            PSx[:, coff:coff + 512], wt[:, dc * 128:(dc + 1) * 128],
                            o_src[:, dc * T + qs * QW: dc * T + qs * QW + QW],
                            start=(dc == 0), stop=(dc == NCH - 1))
                te_done[fc] = p.inc(tt, "te", 1)
                if fc + 2 < NCH:
                    load(fc + 2)
                p.wait(nc.vector, "te", te_done[fc])
                for qs in range(4):
                    PSx, coff = banks[qs]
                    xs = X[:, fc * T + qs * QW: fc * T + qs * QW + QW]
                    ev_done[qs] = vop("scalar_tensor_tensor", xs,
                                      PSx[:, coff:coff + 512],
                                      bias_fn(fc), xs, ALU.add, ALU.add)
            return p.cnt["dve"], p.cnt["te"]

        def emit_ln_ffn(ts, dst, dst_off, xb_off, xsq_off):
            tcol = lambda c: c * T + ts * QW
            state = {}

            def stats():
                p.wait(nc.tensor, "dve")
                p.wait(nc.scalar, "dve")
                mmd = {}
                for c in range(NCH):
                    xb = POOL[:, xb_off + (c % 2) * 512:
                              xb_off + (c % 2) * 512 + 512]
                    xsq = POOL[:, xsq_off + (c % 2) * 512:
                               xsq_off + (c % 2) * 512 + 512]
                    if c >= 2:
                        p.wait(nc.vector, "te", mmd[c - 2])
                        p.wait(nc.scalar, "te", mmd[c - 2])
                    v = vop("tensor_copy", xb, X[:, tcol(c):tcol(c) + QW])
                    sq = sop("activation", xsq, X[:, tcol(c):tcol(c) + QW],
                             AF.Square)
                    p.wait(nc.tensor, "dve", v)
                    nc.tensor.matmul(PS_C[0:1, 0:512], ones_b[:, 0:1], xb,
                                     start=(c == 0), stop=(c == NCH - 1))
                    p.wait(nc.tensor, "act", sq)
                    mmd[c] = p.inc(
                        nc.tensor.matmul(PS_C[0:1, 512:1024], ones_b[:, 0:1],
                                         xsq, start=(c == 0),
                                         stop=(c == NCH - 1)),
                        "te", 1)
                p.wait(nc.vector, "te", mmd[NCH - 1])
                vop("tensor_scalar_mul", MEAN, PS_C[0:1, 0:512], 1.0 / D)
                vop("tensor_mul", VARV, MEAN, MEAN)
                v = vop("scalar_tensor_tensor", VARV, PS_C[0:1, 512:1024],
                        1.0 / D, VARV, ALU.mult, ALU.subtract)
                p.wait(nc.scalar, "dve", v)
                sop("activation", VARV, VARV, AF.Ln, bias=epsb[0:1, 0:1])
                state["a"] = sop("activation", VARV, VARV, AF.Exp, scale=-0.5)

            def bcast():
                p.wait(nc.tensor, "act", state["a"])
                p.wait(nc.tensor, "dve")
                nc.tensor.matmul(PS_C[:, 0:512], ones_f[0:1, :], MEAN,
                                 start=True, stop=True)
                state["t"] = p.inc(
                    nc.tensor.matmul(PS_C[:, 512:1024], ones_f[0:1, :], VARV,
                                     start=True, stop=True), "te", 1)

            def apply_chunk(c):
                if c == 0:
                    p.wait(nc.vector, "te", state["t"])
                vop("tensor_sub", TSC, X[:, tcol(c):tcol(c) + QW], PS_C[:, 0:512])
                vop("tensor_mul", dst[:, dst_off(c):dst_off(c) + QW],
                    TSC, PS_C[:, 512:1024])

            return stats, bcast, apply_chunk

        def emit_ffn(store_out=False):
            ld_w2 = p.inc(nc.gpsimd.dma_start(POOL[:, OF_W2:OF_W2 + 32768], w2_h,
                                              max_dma_last_dim=8192), "gw2", 16)
            w1_loads = {}
            umm_done = {}
            gelu_done = {}

            def load_w1(ts, g):
                par = g % 2
                dst = POOL[:, OF_W1 + par * 4096: OF_W1 + par * 4096 + 4096]
                prevkey = (ts, g - 2) if g >= 2 else ((ts - 1, g + 6) if ts > 0 else None)
                sem = "g16" if par == 0 else "g16b"
                if prevkey is not None and prevkey in umm_done:
                    p.wait(nc.gpsimd, "te", umm_done[prevkey])
                p.wait(nc.gpsimd, sem)
                w1_loads[(ts, g)] = (sem, p.inc(nc.gpsimd.dma_start(dst, w1_h[g]), sem, 16))

            ln_dve_next = [None]
            obank_ev = {}
            for ts in range(NTS):
                par_ts = ts % 2
                hoff = lambda c: par_ts * 4096 + c * 512
                if ts == 0:
                    ln_dve = emit_layernorm(0, BIG2, hoff, OF_XB, OF_XSQ)
                    load_w1(0, 0)
                    load_w1(0, 1)
                else:
                    ln_dve = ln_dve_next[0]
                ln_next = None
                if ts + 1 < NTS:
                    hoff_n = (lambda pn: (lambda c: pn * 4096 + c * 512))((ts + 1) % 2)
                    ln_next = emit_ln_ffn(ts + 1, BIG2, hoff_n, OF_XB, OF_XSQ)
                ubanks = [(PS_A, 0), (PS_B, 0), (PS_A, 512), (PS_B, 512)]
                for uc in range(NUC):
                    if ln_next is not None:
                        if uc == 16:
                            ln_next[0]()
                        elif uc == 28:
                            ln_next[1]()
                    g, j = uc // 4, uc % 4
                    PSx, ucoff = ubanks[uc % 4]
                    p.wait(nc.tensor, w1_loads[(ts, g)][0])
                    if uc == 0:
                        p.wait(nc.tensor, "dve", ln_dve)
                    if uc < 4 and obank_ev.get(uc % 3) is not None:
                        p.wait(nc.tensor, "dve", obank_ev[uc % 3])
                    prevk = (ts, uc - 4) if uc >= 4 else \
                        ((ts - 1, uc + 28) if ts > 0 else None)
                    if prevk is not None and prevk in gelu_done:
                        p.wait(nc.tensor, "act", gelu_done[prevk])
                    wg = POOL[:, OF_W1 + (g % 2) * 4096: OF_W1 + (g % 2) * 4096 + 4096]
                    for dc in range(NCH):
                        tt = nc.tensor.matmul(
                            PSx[:, ucoff:ucoff + 512],
                            wg[:, dc * 512 + j * 128: dc * 512 + j * 128 + 128],
                            BIG2[:, hoff(dc):hoff(dc) + 512],
                            start=(dc == 0), stop=(dc == NCH - 1))
                    umm_done[(ts, g)] = p.inc(tt, "te", 1)
                    if j == 3 and g + 2 < 8:
                        load_w1(ts, g + 2)
                    p.wait(nc.scalar, "te", umm_done[(ts, g)])
                    gelu_done[(ts, uc)] = sop("activation",
                                              BIG1[:, uc * 512:uc * 512 + 512],
                                              PSx[:, ucoff:ucoff + 512], AF.Gelu,
                                              bias=b_b1(uc))
                if ts + 1 < NTS:
                    load_w1(ts + 1, 0)
                    load_w1(ts + 1, 1)
                obanks = [(PS_A, 512), (PS_B, 512), (PS_D, 0)]
                ev = {}
                p.wait(nc.tensor, "act", gelu_done[(ts, NUC - 1)])
                if ts == 0:
                    p.wait(nc.tensor, "gw2", ld_w2)
                    p.wait(nc.tensor, "dve")
                for fo in range(NCH):
                    PSx, coff = obanks[fo % 3]
                    if obank_ev.get(fo % 3) is not None:
                        p.wait(nc.tensor, "dve", obank_ev[fo % 3])
                    for uc in range(NUC):
                        tt = nc.tensor.matmul(
                            PSx[:, coff:coff + 512],
                            POOL[:, OF_W2 + uc * 1024 + fo * 128:
                                 OF_W2 + uc * 1024 + fo * 128 + 128],
                            BIG1[:, uc * 512:uc * 512 + 512],
                            start=(uc == 0), stop=(uc == NUC - 1))
                    t = p.inc(tt, "te", 1)
                    p.wait(nc.vector, "te", t)
                    xs = X[:, fo * T + ts * QW: fo * T + ts * QW + QW]
                    ev[fo] = vop("scalar_tensor_tensor", xs, PSx[:, coff:coff + 512],
                                 b_b2(fo), xs, ALU.add, ALU.add)
                    obank_ev[fo % 3] = ev[fo]
                    if store_out:
                        eng, sem = (nc.sync, "s16") if fo % 2 == 0 else \
                            (nc.scalar, "s16b")
                        p.wait(eng, "dve", ev[fo])
                        p.inc(eng.dma_start(
                            yT[fo * 128:(fo + 1) * 128, ts * QW:ts * QW + QW],
                            xs), sem, 16)
                    if ln_next is not None:
                        ln_next[2](fo)
                if ln_next is not None:
                    ln_dve_next[0] = p.cnt["dve"]
            return p.cnt["dve"]

        ln1_ts = emit_layernorm_exp(list(range(NTS)), BIG1, OA_XSQ2,
                                    wait_s16=x_chunk)
        h1_done = p.cnt["dve"]

        if upto >= 1:
            p.wait(nc.tensor, "g16", aw_loaded)
            st1_ev = {}
            for sl in range(4):
                for wi, (woff, ooff) in enumerate(((OA_WQK, OA_QK64),
                                                   (OA_WVV, OA_VDUP))):
                    gi = sl * 2 + wi
                    PSx = PS_A if gi % 2 == 0 else PS_B
                    p.wait(nc.tensor, "dve", ln1_ts[sl])
                    if gi >= 2:
                        p.wait(nc.tensor, "act", st1_ev[gi - 2])
                    for c in range(NCH):
                        tt = nc.tensor.matmul(
                            PSx[:, 0:512],
                            POOL[:, woff + c * 128: woff + c * 128 + 128],
                            BIG1[:, c * T + sl * 512: c * T + sl * 512 + 512],
                            start=(c == 0), stop=(c == NCH - 1))
                    t = p.inc(tt, "te", 1)
                    p.wait(nc.scalar, "te", t)
                    st1_ev[gi] = sop("activation",
                                     POOL[:, ooff + sl * 512: ooff + sl * 512 + 512],
                                     PSx[:, 0:512], AF.Identity)

            k_ev = {}
            v_ev = {}
            p.wait(nc.tensor, "act")
            for c in range(NCH):
                for sl in range(4):
                    gi = c * 4 + sl
                    PSx = PS_A if sl % 2 == 0 else PS_B
                    if gi >= 2:
                        p.wait(nc.tensor, "dve", k_ev[gi - 2])
                    lt = POOL[64:128, OA_FCQK + c * 128: OA_FCQK + c * 128 + 128]
                    rh = POOL[64:128, OA_QK64 + sl * 512: OA_QK64 + sl * 512 + 512]
                    t = p.inc(nc.tensor.matmul(PSx[:, 512:1024], lt, rh, start=True,
                                               stop=True, tile_position=(64, 0)), "te", 1)
                    p.wait(nc.vector, "te", t)
                    k_ev[c * 4 + sl] = vop(
                        "tensor_scalar_add",
                        KAR[:, c * T + sl * 512: c * T + sl * 512 + 512],
                        PSx[:, 512:1024], b_fck(c))
                p.wait(nc.vector, "act")
                emit_knorm(c, OA_SQ)
                for sl in range(4):
                    gi = c * 4 + sl
                    PSx = PS_A if sl % 2 == 0 else PS_B
                    if gi >= 2:
                        p.wait(nc.tensor, "act", v_ev[gi - 2])
                    if c % 2 == 0:
                        lt = POOL[0:64, OA_FCVV + c * 128: OA_FCVV + c * 128 + 128]
                        rh = POOL[0:64, OA_VDUP + sl * 512: OA_VDUP + sl * 512 + 512]
                        tp = (0, 0)
                    else:
                        lt = POOL[64:128, OA_FCVV + c * 128: OA_FCVV + c * 128 + 128]
                        rh = POOL[64:128, OA_VDUP + sl * 512: OA_VDUP + sl * 512 + 512]
                        tp = (64, 0)
                    t = p.inc(nc.tensor.matmul(PSx[:, 0:512], lt, rh, start=True,
                                               stop=True, tile_position=tp), "te", 1)
                    p.wait(nc.scalar, "te", t)
                    v_ev[c * 4 + sl] = sop(
                        "activation", POOL[:, OA_VTS + sl * 512: OA_VTS + sl * 512 + 512],
                        PSx[:, 0:512], AF.Identity, bias=b_fcv(c))
                p.wait(nc.tensor, "act")
                emit_vtranspose(OA_VTS, c)
            kv_done = p.cnt["dve"]
            kv_te = p.cnt["te"]

            for g2 in (0, 2):
                p.wait(nc.tensor, "dve")
                q_ev = {}
                for qq in (g2, g2 + 1):
                    for c in range(NCH):
                        gi = (qq - g2) * NCH + c
                        PSx = PS_A if gi % 2 == 0 else PS_B
                        if gi >= 2:
                            p.wait(nc.tensor, "act", q_ev[gi - 2])
                        lt = POOL[0:64, OA_FCQK + c * 128: OA_FCQK + c * 128 + 128]
                        rh = POOL[0:64, OA_QK64 + qq * 512: OA_QK64 + qq * 512 + 512]
                        t = p.inc(nc.tensor.matmul(PSx[:, 0:512], lt, rh, start=True,
                                                   stop=True, tile_position=(0, 0)),
                                  "te", 1)
                        p.wait(nc.scalar, "te", t)
                        q_ev[gi] = sop(
                            "activation",
                            BIG2[:, (qq % 2) * 4096 + c * 512:
                                 (qq % 2) * 4096 + c * 512 + 512],
                            PSx[:, 0:512], AF.Identity, bias=b_fcq(c))
                p.wait(nc.vector, "act")
                eb = emit_qnorm2_and_ebias(OA_SQ)
                for qq in (g2, g2 + 1):
                    od, os16 = emit_pairloop(qq, (qq % 2) * 4096, BIG1, OA_PSB,
                                             OA_BNC, OA_XSQ, eb,
                                             kv_done if qq == 0 else p.cnt["dve"],
                                             flush_tail=(qq == 3))
            emit_outproj_all(BIG1, fco_h, OA_FCO, b_fco, od, os16, kv_te)
            p.barrier()
        if debug:
            p.barrier()
            for c in range(NCH):
                p.inc(nc.gpsimd.dma_start(dbgh[c * 128:(c + 1) * 128, :],
                                          BIG1[:, c * T:(c + 1) * T],
                                          max_dma_last_dim=2048), "g16", 16)
            p.barrier()

        if upto >= 2:
            emit_ffn()
            p.barrier()

        if upto >= 3:
            vop("memset", POOL[:, P_VAR:P_VAR + 16896]
                .rearrange("p (kc h f) -> p kc h f", h=16, f=66)[:, :, :, 64:66], 1.0)
            ln3_ts = emit_layernorm_exp(list(range(NTS)), BIG1, OA_XSQ2)
            h3_done = p.cnt["dve"]

            kv_loads = {}

            def load_kv(c):
                pr = c % 2
                if c == 0:
                    p.wait(nc.gpsimd, "dve", h3_done)
                kdst = POOL[:, OB_KV + pr * 1024: OB_KV + pr * 1024 + 1024]
                vdst = POOL[:, OB_KV + 2048 + pr * 1024: OB_KV + 2048 + pr * 1024 + 1024]
                sem = "g16" if pr == 0 else "g16b"
                if c >= 2:
                    p.wait(nc.gpsimd, "te", kv_loads[c - 2][2])
                p.wait(nc.gpsimd, sem)
                a = p.inc(nc.gpsimd.dma_start(kdst, wqkv_kv[c, 0]), sem, 16)
                b = p.inc(nc.gpsimd.dma_start(vdst, wqkv_kv[c, 1]), sem, 16)
                kv_loads[c] = [sem, b, None]

            load_kv(0)
            load_kv(1)
            kvb_ev = {}
            for c in range(NCH):
                pr = c % 2
                kt_ = POOL[:, OB_KV + pr * 1024: OB_KV + pr * 1024 + 1024]
                vt_ = POOL[:, OB_KV + 2048 + pr * 1024: OB_KV + 2048 + pr * 1024 + 1024]
                p.wait(nc.tensor, kv_loads[c][0])
                for which, wt in (("k", kt_), ("v", vt_)):
                    for sl in range(4):
                        gi = c * 8 + (4 if which == "v" else 0) + sl
                        PSx = PS_A if sl % 2 == 0 else PS_B
                        if c == 0:
                            p.wait(nc.tensor, "dve", ln3_ts[sl])
                        if gi >= 2:
                            pv_ = kvb_ev[gi - 2]
                            if isinstance(pv_, tuple):
                                p.wait(nc.tensor, pv_[0], pv_[1])
                            else:
                                p.wait(nc.tensor, "dve", pv_)
                        for dc in range(NCH):
                            tt = nc.tensor.matmul(
                                PSx[:, 0:512], wt[:, dc * 128:(dc + 1) * 128],
                                BIG1[:, dc * T + sl * 512: dc * T + sl * 512 + 512],
                                start=(dc == 0), stop=(dc == NCH - 1))
                        t = p.inc(tt, "te", 1)
                        if which == "k":
                            p.wait(nc.vector, "te", t)
                            kvb_ev[gi] = vop(
                                "tensor_copy",
                                KAR[:, c * T + sl * 512: c * T + sl * 512 + 512],
                                PSx[:, 0:512])
                        else:
                            p.wait(nc.scalar, "te", t)
                            kvb_ev[gi] = ("act", sop(
                                "activation",
                                POOL[:, OB_VTS + sl * 512: OB_VTS + sl * 512 + 512],
                                PSx[:, 0:512], AF.Identity))
                kv_loads[c][2] = p.cnt["te"]
                if c + 2 < NCH:
                    load_kv(c + 2)
                emit_knorm(c, OB_SQ)
                emit_vtranspose(OB_VTS, c)
            kvb_done = p.cnt["dve"]
            kvb_te = p.cnt["te"]

            q_loads = {}

            def load_q(qq2, c):
                pr = c % 2
                sem = "g16" if pr == 0 else "g16b"
                dst = POOL[:, OB_QROT + pr * 1024: OB_QROT + pr * 1024 + 1024]
                prevkey = (qq2, c - 2) if c >= 2 else ((qq2 - 2, c + 6) if qq2 > 0 else None)
                if prevkey is not None and prevkey in q_loads:
                    p.wait(nc.gpsimd, "te", q_loads[prevkey][1])
                p.wait(nc.gpsimd, sem)
                q_loads[(qq2, c)] = [sem,
                                     p.inc(nc.gpsimd.dma_start(dst, wqkv_q[c]), sem, 16)]

            load_q(0, 0)
            load_q(0, 1)
            for g2 in (0, 2):
                p.wait(nc.tensor, "dve")
                qb_ev = {}
                for c in range(NPAIR):
                    pr = c % 2
                    PSx = PS_A if pr == 0 else PS_B
                    wt = POOL[:, OB_QROT + pr * 1024: OB_QROT + pr * 1024 + 1024]
                    p.wait(nc.tensor, q_loads[(g2, c)][0])
                    if c >= 2:
                        p.wait(nc.tensor, "dve", qb_ev[c - 2])
                    for q2 in (g2, g2 + 1):
                        coff2 = (q2 % 2) * 512
                        for dc in range(NCH):
                            tt = nc.tensor.matmul(
                                PSx[:, coff2:coff2 + 512],
                                wt[:, dc * 128:(dc + 1) * 128],
                                BIG1[:, dc * T + q2 * 512: dc * T + q2 * 512 + 512],
                                start=(dc == 0), stop=(dc == NCH - 1))
                    t = p.inc(tt, "te", 1)
                    q_loads[(g2, c)][1] = t
                    if c + 2 < NPAIR:
                        load_q(g2, c + 2)
                    elif g2 == 0:
                        load_q(2, c - 6)
                    p.wait(nc.vector, "te", t)
                    if c == 0:
                        p.wait(nc.vector, "act")
                    vop("tensor_copy",
                        BIG2[:, 0 + c * 512: 0 + c * 512 + 512], PSx[:, 0:512])
                    qb_ev[c] = vop("tensor_copy",
                                   BIG2[:, 4096 + c * 512: 4096 + c * 512 + 512],
                                   PSx[:, 512:1024])
                eb = emit_qnorm2_and_ebias(OB_SQ)
                for qq in (g2, g2 + 1):
                    od, os16 = emit_pairloop(qq, (qq % 2) * 4096, BIG1, OB_PSB,
                                             OB_BNC, OB_XSQ, eb,
                                             kvb_done if qq == 0 else p.cnt["dve"],
                                             flush_tail=(qq == 3))
            emit_outproj_all(BIG1, wproj_h, OB_WPROJ, b_proj, od, os16, kvb_te)
            p.barrier()

        if upto >= 4:
            emit_ffn(store_out=True)
            p.barrier()

        p.wait(nc.sync, "s16")
        p.wait(nc.sync, "s16b")
        p.wait(nc.scalar, "s16")
        p.wait(nc.scalar, "s16b")
    return nc



def _pack_col_tiles(a):
    w = a.shape[1]
    return np.ascontiguousarray(
        a.reshape(8, 128, w).transpose(1, 0, 2).reshape(128, 8 * w))


def host_prep(inp):
    f = np.float32
    import ml_dtypes
    z = np.asarray(inp["L0_attn_a_z"], f)
    wq, wk, wv = (np.asarray(inp[k], f) for k in
                  ("L0_attn_a_wq", "L0_attn_a_wk", "L0_attn_a_wv"))
    fcq_w, fck_w, fcv_w = (np.asarray(inp[k], f) for k in
                           ("L0_attn_a_fcq_w", "L0_attn_a_fck_w", "L0_attn_a_fcv_w"))
    fco_w = np.asarray(inp["L0_attn_a_fco_w"], f)
    w_qkv = np.asarray(inp["L0_attn_b_w_qkv"], f)
    w_proj = np.asarray(inp["L0_attn_b_w_proj"], f)
    w1 = np.asarray(inp["L0_ffn_w1"], f)
    w2 = np.asarray(inp["L0_ffn_w2"], f)

    shared = {}
    shared["wqk"] = _pack_col_tiles(np.concatenate([wq, wk], axis=1))
    shared["wvv"] = _pack_col_tiles(np.concatenate([wv, wv], axis=1))
    shared["fcqk"] = np.ascontiguousarray(np.concatenate([fcq_w, fck_w], axis=0))
    shared["fcvv"] = np.ascontiguousarray(np.concatenate([fcv_w, fcv_w], axis=0))
    shared["fco_h"] = np.ascontiguousarray(
        fco_w.reshape(8, 128, 8, 128).transpose(2, 1, 0, 3).reshape(8, 128, 1024))
    kv = np.stack([
        np.stack([_pack_col_tiles(w_qkv[:, 1024 + c * 128:1024 + (c + 1) * 128]),
                  _pack_col_tiles(w_qkv[:, 2048 + c * 128:2048 + (c + 1) * 128])])
        for c in range(8)])
    shared["wqkv_kv"] = np.ascontiguousarray(kv)
    shared["wqkv_q"] = np.ascontiguousarray(
        np.stack([_pack_col_tiles(w_qkv[:, c * 128:(c + 1) * 128]) for c in range(8)]))
    shared["wproj_h"] = np.ascontiguousarray(
        w_proj.reshape(8, 128, 8, 128).transpose(2, 1, 0, 3).reshape(8, 128, 1024))
    shared["w1_h"] = np.ascontiguousarray(
        np.stack([_pack_col_tiles(w1[:, g * 512:(g + 1) * 512]) for g in range(8)]))
    shared["w2_h"] = np.ascontiguousarray(
        w2.reshape(32, 128, 8, 128).transpose(1, 0, 2, 3).reshape(128, 32768))
    bc = np.zeros((128, 80), f)
    bc[:, 0:8] = np.asarray(inp["L0_attn_a_fcq_b"], f).reshape(8, 128).T
    bc[:, 8:16] = np.asarray(inp["L0_attn_a_fck_b"], f).reshape(8, 128).T
    bc[:, 16:24] = np.asarray(inp["L0_attn_a_fcv_b"], f).reshape(8, 128).T
    bc[:, 24:32] = np.asarray(inp["L0_attn_a_fco_b"], f).reshape(8, 128).T
    bc[:, 32:40] = np.asarray(inp["L0_attn_b_b_proj"], f).reshape(8, 128).T
    bc[:, 40:48] = np.asarray(inp["L0_ffn_b2"], f).reshape(8, 128).T
    bc[:, 48:80] = np.asarray(inp["L0_ffn_b1"], f).reshape(32, 128).T
    shared["biascols"] = bc
    shared["identb"] = np.eye(128, dtype=ml_dtypes.bfloat16)

    in_maps = []
    for c in range(z.shape[0]):
        m = dict(shared)
        m["xT"] = np.ascontiguousarray(z[c].T)
        in_maps.append(m)
    return in_maps



from concourse.bass_utils import run_bass_kernel_spmd


def kernel(**inputs):
    in_maps = host_prep(inputs)
    nc = build(upto=4, debug=0)
    res = run_bass_kernel_spmd(nc, in_maps, core_ids=list(range(8)))
    out = np.stack([np.asarray(res.results[c]["yT"]).T for c in range(8)])
    return np.ascontiguousarray(out.astype(np.float32))


# version 26
# speedup vs baseline: 1.1851x; 1.0195x over previous
import numpy as np
from contextlib import ExitStack

import concourse.bass as bass
import concourse.mybir as mybir

F32 = mybir.dt.float32
BF16 = mybir.dt.bfloat16
AF = mybir.ActivationFunctionType
ALU = mybir.AluOpType
AX = mybir.AxisListType

T = 2048
D = 1024
NCH = 8
NPAIR = 8
NQQ = 4
QW = 512
KC = 16
NUC = 32
NTS = 4
EPS = 1e-5

P_KAR = 0
P_VAR = 16384
VSTR = 1056
MB = 33280
OA_QK64 = MB + 0
OA_VDUP = MB + 2048
OA_PSB = MB + 2048
OA_WQK = MB + 4096
OA_WVV = MB + 5120
OA_VTS = MB + 4096
OA_FCO = MB + 4096
OA_FCQK = MB + 6144
OA_FCVV = MB + 7168
OA_BNC = MB + 7168
OA_XB = MB + 2048
OA_XSQ2 = MB + 3072
OA_SQ = MB + 9216
OA_XSQ = MB + 8192
OB_KV = MB + 0
OB_PSB = MB + 0
OB_QROT = MB + 4096
OB_VTS = MB + 6144
OB_WPROJ = MB + 6144
OB_BNC = MB + 2048
OB_XB = MB + 2048
OB_SQ = MB + 9216
OB_XSQ = MB + 3072
OF_W2 = 0
OF_W1 = 32768
OF_XB = 40960
OF_XSQ = 41984
POOL_COLS = 43008


class Prog:
    def __init__(self, nc):
        self.nc = nc
        self._stack = None
        self.sems = {}
        self.cnt = {}

    def sem(self, name):
        s = self._stack.enter_context(self.nc.semaphore(name))
        self.sems[name] = s
        self.cnt[name] = 0
        return s

    def inc(self, instr, name, n=1):
        instr.then_inc(self.sems[name], n)
        self.cnt[name] += n
        return self.cnt[name]

    def wait(self, eng, name, v=None):
        eng.wait_ge(self.sems[name], self.cnt[name] if v is None else v)

    def barrier(self):
        nc = self.nc
        for eng in (nc.tensor, nc.scalar, nc.vector, nc.sync, nc.gpsimd):
            for nm in ("te", "act", "dve", "s16", "s16b", "g16", "g16b",
                       "gw2"):
                self.wait(eng, nm)


def build(upto=4, debug=0):
    nc = bass.Bass("TRN2", target_bir_lowering=False)
    p = Prog(nc)

    xT = nc.dram_tensor("xT", [D, T], F32, kind="ExternalInput").ap()
    wqk_d = nc.dram_tensor("wqk", [128, 1024], F32, kind="ExternalInput").ap()
    wvv_d = nc.dram_tensor("wvv", [128, 1024], F32, kind="ExternalInput").ap()
    fcqk_d = nc.dram_tensor("fcqk", [128, 1024], F32, kind="ExternalInput").ap()
    fcvv_d = nc.dram_tensor("fcvv", [128, 1024], F32, kind="ExternalInput").ap()
    fco_h = nc.dram_tensor("fco_h", [8, 128, 1024], F32, kind="ExternalInput").ap()
    wqkv_kv = nc.dram_tensor("wqkv_kv", [8, 2, 128, 1024], F32, kind="ExternalInput").ap()
    wqkv_q = nc.dram_tensor("wqkv_q", [8, 128, 1024], F32, kind="ExternalInput").ap()
    wproj_h = nc.dram_tensor("wproj_h", [8, 128, 1024], F32, kind="ExternalInput").ap()
    w1_h = nc.dram_tensor("w1_h", [8, 128, 4096], F32, kind="ExternalInput").ap()
    w2_h = nc.dram_tensor("w2_h", [128, 32768], F32, kind="ExternalInput").ap()
    biascols = nc.dram_tensor("biascols", [128, 80], F32, kind="ExternalInput").ap()
    identb_d = nc.dram_tensor("identb", [128, 128], BF16, kind="ExternalInput").ap()
    yT = nc.dram_tensor("yT", [D, T], F32, kind="ExternalOutput").ap()
    if debug:
        dbgh = nc.dram_tensor("dbgh", [D, T], F32, kind="ExternalOutput").ap()

    with ExitStack() as ctx:
        p._stack = ctx
        sbt = lambda nm, shp, dt: ctx.enter_context(nc.sbuf_tensor(nm, shp, dt))
        pst = lambda nm, shp, dt: ctx.enter_context(nc.psum_tensor(nm, shp, dt))
        for nm in ("te", "act", "dve", "s16", "s16b", "g16", "g16b", "gw2"):
            p.sem(nm)

        X = sbt("X", [128, NCH * T], F32)
        BIG1 = sbt("BIG1", [128, 16384], BF16)
        BIG2 = sbt("BIG2", [128, 8192], BF16)
        POOL = sbt("POOL", [128, POOL_COLS], BF16)
        identb = sbt("identb_s", [128, 128], BF16)
        ones_b = sbt("ones_b", [128, 128], BF16)
        ones_f = sbt("ones_f", [128, 128], F32)
        bcols = sbt("bcols", [128, 80], F32)
        ebias = sbt("ebias", [128, 8], F32)
        sstat = sbt("sstat", [128, 32], F32)
        SCR = sbt("SCR", [128, 1024], F32)
        PF = sbt("PF", [128, 1024], F32)
        epsb = sbt("epsb", [1, 1], F32)

        PS_A = pst("PS_A", [128, 1024], F32)
        PS_B = pst("PS_B", [128, 1024], F32)
        PS_C = pst("PS_C", [128, 1024], F32)
        PS_D = pst("PS_D", [128, 512], F32)
        PS_E = pst("PS_E", [128, 512], F32)

        KAR = POOL[:, P_KAR:P_KAR + 16384]

        def vop(name, *a, **k):
            p.wait(nc.vector, "dve")
            return p.inc(getattr(nc.vector, name)(*a, **k), "dve", 1)

        def sop(name, *a, **k):
            p.wait(nc.scalar, "act")
            return p.inc(getattr(nc.scalar, name)(*a, **k), "act", 1)

        MEAN = SCR[0:1, 0:512]
        VARV = SCR[0:1, 512:1024]
        TSC = SCR[:, 512:1024]
        RECIP0 = PF[64:65, 0:512]
        RECIP1 = PF[64:65, 512:1024]
        S_QMAX = lambda c: sstat[0:1, 0 + c:1 + c]
        S_KMAX = lambda c: sstat[0:1, 8 + c:9 + c]
        S_KSL = lambda s: sstat[0:1, 16 + s:17 + s]
        S_EROW = sstat[0:1, 24:32]

        b_fcq = lambda c: bcols[:, 0 + c:1 + c]
        b_fck = lambda c: bcols[:, 8 + c:9 + c]
        b_fcv = lambda c: bcols[:, 16 + c:17 + c]
        b_fco = lambda c: bcols[:, 24 + c:25 + c]
        b_proj = lambda c: bcols[:, 32 + c:33 + c]
        b_b2 = lambda c: bcols[:, 40 + c:41 + c]
        b_b1 = lambda uc: bcols[:, 48 + uc:49 + uc]

        vop("memset", ones_b[:, :], 1.0)
        vop("memset", ones_f[:, :], 1.0)
        vop("memset", epsb[:, :], EPS)
        vop("memset", SCR[0:64, 0:512], 1.0)
        vop("memset", POOL[:, P_VAR:P_VAR + 16896], 1.0)

        p.inc(nc.sync.dma_start(identb[:, :], identb_d), "s16", 16)
        p.inc(nc.sync.dma_start(bcols[:, :], biascols), "s16", 16)
        p.wait(nc.vector, "s16")
        p.wait(nc.scalar, "s16")
        p.wait(nc.tensor, "s16")
        x_chunk = []
        for c in range(NCH):
            sem = p.sem("xk%d" % c)
            eng = nc.sync if c % 2 == 0 else nc.scalar
            p.inc(eng.dma_start(X[:, c * T:(c + 1) * T],
                                xT[c * 128:(c + 1) * 128, :]), "xk%d" % c, 16)
            x_chunk.append(("xk%d" % c, 16))

        p.inc(nc.gpsimd.dma_start(POOL[:, OA_WQK:OA_WQK + 1024], wqk_d), "g16", 16)
        p.inc(nc.gpsimd.dma_start(POOL[:, OA_WVV:OA_WVV + 1024], wvv_d), "g16", 16)
        p.inc(nc.gpsimd.dma_start(POOL[:, OA_FCQK:OA_FCQK + 1024], fcqk_d), "g16", 16)
        p.inc(nc.gpsimd.dma_start(POOL[:, OA_FCVV:OA_FCVV + 1024], fcvv_d), "g16", 16)
        aw_loaded = p.cnt["g16"]


        def emit_layernorm(ts, dst, dst_off, xb_off, xsq_off, wait_s16=None):
            tcol = lambda c: c * T + ts * QW
            p.wait(nc.tensor, "dve")
            mmd = {}
            for c in range(NCH):
                par = c % 2
                xb = POOL[:, xb_off + par * 512:xb_off + par * 512 + 512]
                xsq = POOL[:, xsq_off + par * 512:xsq_off + par * 512 + 512]
                if wait_s16 is not None and c == 0:
                    p.wait(nc.vector, "s16", wait_s16)
                if c >= 2:
                    p.wait(nc.vector, "te", mmd[c - 2])
                vop("tensor_copy", xb, X[:, tcol(c):tcol(c) + QW])
                v = vop("tensor_mul", xsq, xb, xb)
                p.wait(nc.tensor, "dve", v)
                nc.tensor.matmul(PS_C[0:1, 0:512], ones_b[:, 0:1], xb,
                                 start=(c == 0), stop=(c == NCH - 1))
                mmd[c] = p.inc(nc.tensor.matmul(PS_C[0:1, 512:1024], ones_b[:, 0:1],
                                                xsq,
                                                start=(c == 0), stop=(c == NCH - 1)),
                               "te", 1)
            p.wait(nc.vector, "te")
            vop("tensor_scalar_mul", MEAN, PS_C[0:1, 0:512], 1.0 / D)
            vop("tensor_mul", VARV, MEAN, MEAN)
            v = vop("scalar_tensor_tensor", VARV, PS_C[0:1, 512:1024], 1.0 / D,
                    VARV, ALU.mult, ALU.subtract)
            p.wait(nc.scalar, "dve", v)
            sop("activation", VARV, VARV, AF.Ln, bias=epsb[0:1, 0:1])
            a = sop("activation", VARV, VARV, AF.Exp, scale=-0.5)
            p.wait(nc.tensor, "act", a)
            nc.tensor.matmul(PS_C[:, 0:512], ones_f[0:1, :], MEAN, start=True, stop=True)
            t = p.inc(nc.tensor.matmul(PS_C[:, 512:1024], ones_f[0:1, :], VARV,
                                       start=True, stop=True), "te", 1)
            p.wait(nc.vector, "te", t)
            for c in range(NCH):
                vop("tensor_sub", TSC, X[:, tcol(c):tcol(c) + QW], PS_C[:, 0:512])
                vop("tensor_mul", dst[:, dst_off(c):dst_off(c) + QW],
                    TSC, PS_C[:, 512:1024])
            return p.cnt["dve"]

        def emit_layernorm_exp(ts_list, dst, xsq_off, wait_s16=None):
            tcol2 = lambda ts, c: c * T + ts * QW
            mmd = {}

            def stats(i, ts):
                if i % 2 == 0:
                    sum_dst, sq_dst = PS_D[0:1, 0:512], PS_E[0:1, 0:512]
                else:
                    sum_dst, sq_dst = PS_A[0:1, 0:512], PS_A[0:1, 512:1024]
                for c in range(NCH):
                    gi = i * NCH + c
                    xsq = POOL[:, xsq_off + (gi % 2) * 512:
                               xsq_off + (gi % 2) * 512 + 512]
                    if i == 0 and wait_s16 is not None:
                        sem_, v_ = wait_s16[c]
                        p.wait(nc.scalar, sem_, v_)
                        p.wait(nc.tensor, sem_, v_)
                    if i == 0 and c == 0:
                        p.wait(nc.scalar, "dve")
                        p.wait(nc.tensor, "dve")
                    if gi >= 2 and (gi - 2) in mmd:
                        p.wait(nc.scalar, "te", mmd[gi - 2])
                    sq = sop("activation", xsq,
                             X[:, tcol2(ts, c):tcol2(ts, c) + QW], AF.Square)
                    nc.tensor.matmul(sum_dst, ones_f[:, 0:1],
                                     X[:, tcol2(ts, c):tcol2(ts, c) + QW],
                                     start=(c == 0), stop=(c == NCH - 1))
                    p.wait(nc.tensor, "act", sq)
                    mmd[gi] = p.inc(
                        nc.tensor.matmul(sq_dst, ones_b[:, 0:1], xsq,
                                         start=(c == 0), stop=(c == NCH - 1)),
                        "te", 1)

            def finalize_apply(i, ts):
                if i % 2 == 0:
                    sum_row, sq_row = PS_D[0:1, 0:512], PS_E[0:1, 0:512]
                else:
                    sum_row, sq_row = PS_A[0:1, 0:512], PS_A[0:1, 512:1024]
                p.wait(nc.vector, "te", mmd[i * NCH + NCH - 1])
                vop("tensor_scalar_mul", MEAN, sum_row, 1.0 / D)
                vop("tensor_mul", VARV, MEAN, MEAN)
                v = vop("scalar_tensor_tensor", VARV, sq_row, 1.0 / D,
                        VARV, ALU.mult, ALU.subtract)
                p.wait(nc.scalar, "dve", v)
                sop("activation", VARV, VARV, AF.Ln, bias=epsb[0:1, 0:1])
                a = sop("activation", VARV, VARV, AF.Exp, scale=-0.5)
                p.wait(nc.tensor, "act", a)
                p.wait(nc.tensor, "dve")
                nc.tensor.matmul(PS_C[:, 0:512], ones_f[0:1, :], MEAN,
                                 start=True, stop=True)
                t = p.inc(nc.tensor.matmul(PS_C[:, 512:1024], ones_f[0:1, :],
                                           VARV, start=True, stop=True), "te", 1)
                p.wait(nc.vector, "te", t)
                for c in range(NCH):
                    vop("tensor_sub", TSC, X[:, tcol2(ts, c):tcol2(ts, c) + QW],
                        PS_C[:, 0:512])
                    vop("tensor_mul", dst[:, tcol2(ts, c):tcol2(ts, c) + QW],
                        TSC, PS_C[:, 512:1024])

            stats(0, ts_list[0])
            out = []
            for i, ts in enumerate(ts_list):
                if i + 1 < len(ts_list):
                    stats(i + 1, ts_list[i + 1])
                finalize_apply(i, ts)
                out.append(p.cnt["dve"])
            return out

        vt_bank_ev = {}
        def emit_vtranspose(vts_off, cpair, entry=False):
            p.wait(nc.tensor, "act")
            if entry:
                p.wait(nc.tensor, "dve")
            banks = [PS_C[:, 0:512], PS_C[:, 512:1024], PS_D[:, 0:512]]
            for r4 in range(4):
                bi = r4 % 3
                g = vt_bank_ev.get(bi)
                if g is not None:
                    p.wait(nc.tensor, g[0], g[1])
                for j in range(4):
                    kt = r4 * 4 + j
                    psv = banks[bi][:, j * 64:(j + 1) * 64].bitcast(BF16)
                    src = POOL[:, vts_off + kt * 128: vts_off + kt * 128 + 128]
                    t = p.inc(nc.tensor.transpose(psv, src, identb[:, :]), "te", 1)
                if r4 % 2 == 0:
                    p.wait(nc.vector, "te", t)
                else:
                    p.wait(nc.scalar, "te", t)
                for j in range(4):
                    kt = r4 * 4 + j
                    psv = banks[bi][:, j * 64:(j + 1) * 64].bitcast(BF16)
                    dst = POOL[:, P_VAR + kt * VSTR + 132 * cpair:
                               P_VAR + kt * VSTR + 132 * cpair + 132]
                    da = dst.rearrange("p (h f) -> p h f", h=2)[:, :, 0:64]
                    sa = psv.rearrange("p (h f) -> p h f", h=2)
                    if r4 % 2 == 0:
                        v = vop("tensor_copy", da, sa)
                    else:
                        v = sop("activation", da, sa, AF.Identity)
                vt_bank_ev[bi] = ("dve" if r4 % 2 == 0 else "act", v)
            return p.cnt["dve"]

        kn_prev = {}
        def emit_knorm(cpair, sq_off):
            if cpair == 0:
                p.wait(nc.scalar, "te")
                kn_prev.clear()
            p.wait(nc.scalar, "dve")
            tmm = {}
            mx = {}
            for sl in range(4):
                off = sq_off if sl % 2 == 0 else MB + 8192
                ksq = POOL[:, off:off + 512]
                kc_ap = KAR[:, cpair * T + sl * 512: cpair * T + sl * 512 + 512]
                g = tmm.get(sl - 2) or kn_prev.get("t%d" % (sl + 2))
                if g is not None:
                    p.wait(nc.scalar, "te", g)
                sq = sop("activation", ksq, kc_ap, AF.Square)
                p.wait(nc.tensor, "act", sq)
                gm = mx.get(sl - 1) or kn_prev.get("m3")
                if gm is not None:
                    p.wait(nc.tensor, "dve", gm)
                tmm[sl] = p.inc(nc.tensor.matmul(PS_E[0:1, 0:512],
                                                 ones_b[:, 0:1], ksq,
                                                 start=True, stop=True),
                                "te", 1)
                p.wait(nc.vector, "te", tmm[sl])
                mx[sl] = vop("reduce_max", S_KSL(sl), PS_E[0:1, 0:512],
                             axis=AX.X)
            vop("reduce_max", S_KMAX(cpair), sstat[0:1, 16:20], axis=AX.X)
            kn_prev["t2"] = tmm[2]
            kn_prev["t3"] = tmm[3]
            kn_prev["m3"] = mx[3]

        def emit_qnorm2_and_ebias(sq_off):
            p.wait(nc.scalar, "te")
            p.wait(nc.scalar, "dve")
            psl = [PS_A[0:1, 0:512], PS_A[0:1, 512:1024]]
            tmm = {}
            mx = {}
            for c in range(NPAIR):
                for qqi in range(2):
                    gi = c * 2 + qqi
                    qsq = POOL[:, (sq_off if gi % 2 == 0 else MB + 8192):
                               (sq_off if gi % 2 == 0 else MB + 8192) + 512]
                    q_ap = BIG2[:, qqi * 4096 + c * 512: qqi * 4096 + c * 512 + 512]
                    if gi >= 2:
                        p.wait(nc.scalar, "te", tmm[gi - 2])
                    sq = sop("activation", qsq, q_ap, AF.Square)
                    p.wait(nc.tensor, "act", sq)
                    if gi >= 2:
                        p.wait(nc.tensor, "dve", mx[gi - 2])
                    tmm[gi] = p.inc(nc.tensor.matmul(psl[gi % 2], ones_b[:, 0:1],
                                                     qsq, start=True, stop=True),
                                    "te", 1)
                    p.wait(nc.vector, "te", tmm[gi])
                    mx[gi] = vop("reduce_max", sstat[0:1, 20 + qqi:21 + qqi],
                                 psl[gi % 2], axis=AX.X)
                vop("reduce_max", S_QMAX(c), sstat[0:1, 20:22], axis=AX.X)
            vop("tensor_add", S_EROW, sstat[0:1, 0:8], sstat[0:1, 8:16])
            v = vop("tensor_scalar_mul", S_EROW, S_EROW, -1.0 / 16.0)
            p.wait(nc.tensor, "dve", v)
            t = p.inc(nc.tensor.matmul(PS_D[:, 0:8], ones_f[0:1, :], S_EROW,
                                       start=True, stop=True), "te", 1)
            p.wait(nc.vector, "te", t)
            p.wait(nc.vector, "act")
            return vop("tensor_copy", ebias[:, :], PS_D[:, 0:8])

        roll = {"exp": {0: None, 1: None}, "pv": {0: None, 1: None}}

        prev_mul = {0: None, 1: None}
        pending = [None]

        def emit_pairloop(qq, qdst_off, o_dst, psb_off, bnc_off, raw_off, eb_dve,
                          o_free_dve, flush_tail=False):
            s_done_all = {}

            def obank(cp, h):
                if cp == 0:
                    return PS_C[:, h * 512:(h + 1) * 512]
                return (PS_D if h == 0 else PS_E)[:, 0:512]

            def emit_S_next(cn, kc):
                par = kc % 2
                PSx = PS_A if par == 0 else PS_B
                if roll["exp"][par] is not None:
                    p.wait(nc.tensor, "act", roll["exp"][par])
                lt0 = KAR[0:64, cn * T + kc * 128: cn * T + kc * 128 + 128]
                rh0 = BIG2[0:64, qdst_off + cn * 512: qdst_off + cn * 512 + 512]
                nc.tensor.matmul(PSx[:, 0:512], lt0, rh0, start=True, stop=True,
                                 tile_position=(0, 0))
                lt1 = KAR[64:128, cn * T + kc * 128: cn * T + kc * 128 + 128]
                rh1 = BIG2[64:128, qdst_off + cn * 512: qdst_off + cn * 512 + 512]
                s_done_all[(cn, kc)] = p.inc(
                    nc.tensor.matmul(PSx[:, 512:1024], lt1, rh1, start=True,
                                     stop=True, tile_position=(64, 0)), "te", 1)

            def flush_part(i):
                if pending[0] is None:
                    return
                rc_, raw0_, bncap_, ocols_, cp_, odst_ = pending[0]
                h = i // 2
                bank = obank(cp_, h)
                row = 0 if h == 0 else 32
                hilo = SCR[row:row + 1, 512:1024].bitcast(BF16)
                rhs = hilo[:, 0:512] if i % 2 == 0 else hilo[:, 512:1024]
                if i == 0:
                    p.wait(nc.tensor, "dve")
                p.inc(nc.tensor.matmul(bank[0:64, 0:512],
                                       ones_b[row:row + 1, 0:64], rhs,
                                       start=(i % 2 == 0), stop=(i % 2 == 1)),
                      "te", 1)

            def flush_pending():
                if pending[0] is None:
                    return
                rc_, raw0_, bncap_, ocols_, cp_, odst_ = pending[0]
                pending[0] = None
                p.wait(nc.vector, "te")
                vop("tensor_mul", odst_[0:64, ocols_:ocols_ + QW], raw0_,
                    obank(cp_, 0)[0:64, 0:512])
                bm = vop("tensor_mul", bncap_, bncap_, obank(cp_, 1)[0:64, 0:512])
                prev_mul[cp_] = bm
                p.wait(nc.sync, "dve", bm)
                p.inc(nc.sync.dma_start(odst_[64:128, ocols_:ocols_ + QW], bncap_),
                      "s16", 16)

            for c in range(NPAIR):
                exp_done = {}
                s_done = {}
                pv_done = {}

                def emit_S(kc):
                    if (c, kc) in s_done_all:
                        s_done[kc] = s_done_all.pop((c, kc))
                        return
                    emit_S_next(c, kc)
                    s_done[kc] = s_done_all.pop((c, kc))

                def emit_exp(kc):
                    par = kc % 2
                    PSx = PS_A if par == 0 else PS_B
                    psb = POOL[:, psb_off + par * 1024: psb_off + par * 1024 + 1024]
                    wv = s_done[kc]
                    if roll["pv"][par] is not None and roll["pv"][par] > wv:
                        wv = roll["pv"][par]
                    p.wait(nc.scalar, "te", wv)
                    if c == 0 and kc == 0:
                        p.wait(nc.scalar, "dve", eb_dve)
                    exp_done[kc] = p.inc(
                        nc.scalar.activation(psb, PSx[:, :], AF.Exp,
                                             bias=ebias[:, c:c + 1], scale=0.125),
                        "act", 1)
                    roll["exp"][par] = exp_done[kc]

                def emit_PV(kc):
                    par = kc % 2
                    cp = c % 2
                    psb = POOL[:, psb_off + par * 1024: psb_off + par * 1024 + 1024]
                    p.wait(nc.tensor, "act", exp_done[kc])
                    if kc == 0:
                        g = prev_mul[cp] if prev_mul[cp] is not None else 0
                        p.wait(nc.tensor, "dve", max(g, o_free_dve))
                    vbase = P_VAR + kc * VSTR + 132 * c
                    lt0 = POOL[:, vbase:vbase + 65]
                    nc.tensor.matmul(obank(cp, 0)[0:65, 0:512], lt0, psb[:, 0:512],
                                     start=(kc == 0), stop=(kc == KC - 1))
                    lt1 = POOL[:, vbase + 66:vbase + 66 + 65]
                    pv_done[kc] = p.inc(
                        nc.tensor.matmul(obank(cp, 1)[0:65, 0:512], lt1,
                                         psb[:, 512:1024],
                                         start=(kc == 0), stop=(kc == KC - 1)), "te", 1)
                    roll["pv"][par] = pv_done[kc]

                emit_S(0)
                for kc in range(KC):
                    emit_exp(kc)
                    if kc + 1 < KC:
                        emit_S(kc + 1)
                    if kc >= 1:
                        emit_PV(kc - 1)
                    if 9 <= kc <= 12:
                        flush_part(kc - 9)
                    if kc == 13:
                        flush_pending()
                    if kc == KC - 2 and c + 1 < NPAIR:
                        emit_S_next(c + 1, 0)
                    if kc == KC - 1 and c + 1 < NPAIR:
                        emit_S_next(c + 1, 1)
                emit_PV(KC - 1)

                p.wait(nc.vector, "te", pv_done[KC - 1])
                cp = c % 2
                raw0 = POOL[0:64, raw_off + cp * 512:raw_off + cp * 512 + 512]
                bncap = POOL[0:64, bnc_off + cp * 512:bnc_off + cp * 512 + 512]
                vop("tensor_copy", PF[0:1, 0:512], obank(cp, 0)[64:65, 0:512])
                hsums = vop("tensor_copy", PF[0:1, 512:1024],
                            obank(cp, 1)[64:65, 0:512])
                p.wait(nc.sync, "dve", hsums)
                p.inc(nc.sync.dma_start(SCR[0:1, 0:512], PF[0:1, 0:512]),
                      "s16", 16)
                p.inc(nc.sync.dma_start(SCR[32:33, 0:512], PF[0:1, 512:1024]),
                      "s16", 16)
                vop("tensor_copy", raw0, obank(cp, 0)[0:64, 0:512])
                vop("tensor_copy", bncap, obank(cp, 1)[0:64, 0:512])
                p.wait(nc.vector, "s16")
                vop("reciprocal", PF[0:33, 0:512], SCR[0:33, 0:512])
                hl = SCR[0:33, 512:1024].bitcast(BF16)
                vop("tensor_copy", hl[:, 0:512], PF[0:33, 0:512])
                vop("tensor_copy", PF[0:33, 512:1024], hl[:, 0:512])
                rc = vop("tensor_sub", hl[:, 512:1024], PF[0:33, 0:512],
                         PF[0:33, 512:1024])
                pending[0] = (rc, raw0, bncap, c * T + qq * QW, cp, o_dst)
            if flush_tail:
                for i in range(4):
                    flush_part(i)
                flush_pending()
            return p.cnt["dve"], p.cnt["s16"]

        def emit_outproj_all(o_src, w_dram, rot_off, bias_fn, o_dve, o_s16,
                             w_release_te):
            loads = {}
            te_done = {}
            ev_done = {}

            def load(fc):
                par = fc % 2
                sem = "g16" if par == 0 else "g16b"
                dst = POOL[:, rot_off + par * 1024: rot_off + par * 1024 + 1024]
                if fc >= 2:
                    p.wait(nc.gpsimd, "te", te_done[fc - 2])
                else:
                    p.wait(nc.gpsimd, "te", w_release_te)
                p.wait(nc.gpsimd, sem)
                loads[fc] = (sem, p.inc(nc.gpsimd.dma_start(dst, w_dram[fc]), sem, 16))

            load(0)
            load(1)
            banks = [(PS_A, 0), (PS_B, 0), (PS_A, 512), (PS_B, 512)]
            p.wait(nc.tensor, "act")
            for fc in range(NCH):
                par = fc % 2
                wt = POOL[:, rot_off + par * 1024: rot_off + par * 1024 + 1024]
                p.wait(nc.tensor, loads[fc][0])
                if fc == 0:
                    p.wait(nc.tensor, "dve", o_dve)
                    p.wait(nc.tensor, "s16", o_s16)
                for dc in range(NCH):
                    for qs in range(4):
                        PSx, coff = banks[qs]
                        if dc == 0 and ev_done.get(qs) is not None:
                            p.wait(nc.tensor, "dve", ev_done[qs])
                        tt = nc.tensor.matmul(
                            PSx[:, coff:coff + 512], wt[:, dc * 128:(dc + 1) * 128],
                            o_src[:, dc * T + qs * QW: dc * T + qs * QW + QW],
                            start=(dc == 0), stop=(dc == NCH - 1))
                te_done[fc] = p.inc(tt, "te", 1)
                if fc + 2 < NCH:
                    load(fc + 2)
                p.wait(nc.vector, "te", te_done[fc])
                for qs in range(4):
                    PSx, coff = banks[qs]
                    xs = X[:, fc * T + qs * QW: fc * T + qs * QW + QW]
                    ev_done[qs] = vop("scalar_tensor_tensor", xs,
                                      PSx[:, coff:coff + 512],
                                      bias_fn(fc), xs, ALU.add, ALU.add)
            return p.cnt["dve"], p.cnt["te"]

        def emit_ln_ffn(ts, dst, dst_off, xb_off, xsq_off):
            tcol = lambda c: c * T + ts * QW
            state = {}

            def stats():
                p.wait(nc.tensor, "dve")
                p.wait(nc.scalar, "dve")
                mmd = {}
                for c in range(NCH):
                    xb = POOL[:, xb_off + (c % 2) * 512:
                              xb_off + (c % 2) * 512 + 512]
                    xsq = POOL[:, xsq_off + (c % 2) * 512:
                               xsq_off + (c % 2) * 512 + 512]
                    if c >= 2:
                        p.wait(nc.vector, "te", mmd[c - 2])
                        p.wait(nc.scalar, "te", mmd[c - 2])
                    v = vop("tensor_copy", xb, X[:, tcol(c):tcol(c) + QW])
                    sq = sop("activation", xsq, X[:, tcol(c):tcol(c) + QW],
                             AF.Square)
                    p.wait(nc.tensor, "dve", v)
                    nc.tensor.matmul(PS_C[0:1, 0:512], ones_b[:, 0:1], xb,
                                     start=(c == 0), stop=(c == NCH - 1))
                    p.wait(nc.tensor, "act", sq)
                    mmd[c] = p.inc(
                        nc.tensor.matmul(PS_C[0:1, 512:1024], ones_b[:, 0:1],
                                         xsq, start=(c == 0),
                                         stop=(c == NCH - 1)),
                        "te", 1)
                p.wait(nc.vector, "te", mmd[NCH - 1])
                vop("tensor_scalar_mul", MEAN, PS_C[0:1, 0:512], 1.0 / D)
                vop("tensor_mul", VARV, MEAN, MEAN)
                v = vop("scalar_tensor_tensor", VARV, PS_C[0:1, 512:1024],
                        1.0 / D, VARV, ALU.mult, ALU.subtract)
                p.wait(nc.scalar, "dve", v)
                sop("activation", VARV, VARV, AF.Ln, bias=epsb[0:1, 0:1])
                state["a"] = sop("activation", VARV, VARV, AF.Exp, scale=-0.5)

            def bcast():
                p.wait(nc.tensor, "act", state["a"])
                p.wait(nc.tensor, "dve")
                nc.tensor.matmul(PS_C[:, 0:512], ones_f[0:1, :], MEAN,
                                 start=True, stop=True)
                state["t"] = p.inc(
                    nc.tensor.matmul(PS_C[:, 512:1024], ones_f[0:1, :], VARV,
                                     start=True, stop=True), "te", 1)

            def apply_chunk(c):
                if c == 0:
                    p.wait(nc.vector, "te", state["t"])
                vop("tensor_sub", TSC, X[:, tcol(c):tcol(c) + QW], PS_C[:, 0:512])
                vop("tensor_mul", dst[:, dst_off(c):dst_off(c) + QW],
                    TSC, PS_C[:, 512:1024])

            return stats, bcast, apply_chunk

        def emit_ffn(store_out=False):
            ld_w2 = p.inc(nc.gpsimd.dma_start(POOL[:, OF_W2:OF_W2 + 32768], w2_h,
                                              max_dma_last_dim=8192), "gw2", 16)
            w1_loads = {}
            umm_done = {}
            gelu_done = {}

            def load_w1(ts, g):
                par = g % 2
                dst = POOL[:, OF_W1 + par * 4096: OF_W1 + par * 4096 + 4096]
                prevkey = (ts, g - 2) if g >= 2 else ((ts - 1, g + 6) if ts > 0 else None)
                sem = "g16" if par == 0 else "g16b"
                if prevkey is not None and prevkey in umm_done:
                    p.wait(nc.gpsimd, "te", umm_done[prevkey])
                p.wait(nc.gpsimd, sem)
                w1_loads[(ts, g)] = (sem, p.inc(nc.gpsimd.dma_start(dst, w1_h[g]), sem, 16))

            ln_dve_next = [None]
            obank_ev = {}
            for ts in range(NTS):
                par_ts = ts % 2
                hoff = lambda c: par_ts * 4096 + c * 512
                if ts == 0:
                    ln_dve = emit_layernorm(0, BIG2, hoff, OF_XB, OF_XSQ)
                    load_w1(0, 0)
                    load_w1(0, 1)
                else:
                    ln_dve = ln_dve_next[0]
                ln_next = None
                if ts + 1 < NTS:
                    hoff_n = (lambda pn: (lambda c: pn * 4096 + c * 512))((ts + 1) % 2)
                    ln_next = emit_ln_ffn(ts + 1, BIG2, hoff_n, OF_XB, OF_XSQ)
                ubanks = [(PS_A, 0), (PS_B, 0), (PS_A, 512), (PS_B, 512)]
                for uc in range(NUC):
                    if ln_next is not None:
                        if uc == 16:
                            ln_next[0]()
                        elif uc == 28:
                            ln_next[1]()
                    g, j = uc // 4, uc % 4
                    PSx, ucoff = ubanks[uc % 4]
                    p.wait(nc.tensor, w1_loads[(ts, g)][0])
                    if uc == 0:
                        p.wait(nc.tensor, "dve", ln_dve)
                    if uc < 4 and obank_ev.get(uc % 3) is not None:
                        p.wait(nc.tensor, "dve", obank_ev[uc % 3])
                    prevk = (ts, uc - 4) if uc >= 4 else \
                        ((ts - 1, uc + 28) if ts > 0 else None)
                    if prevk is not None and prevk in gelu_done:
                        p.wait(nc.tensor, "act", gelu_done[prevk])
                    wg = POOL[:, OF_W1 + (g % 2) * 4096: OF_W1 + (g % 2) * 4096 + 4096]
                    for dc in range(NCH):
                        tt = nc.tensor.matmul(
                            PSx[:, ucoff:ucoff + 512],
                            wg[:, dc * 512 + j * 128: dc * 512 + j * 128 + 128],
                            BIG2[:, hoff(dc):hoff(dc) + 512],
                            start=(dc == 0), stop=(dc == NCH - 1))
                    umm_done[(ts, g)] = p.inc(tt, "te", 1)
                    if j == 3 and g + 2 < 8:
                        load_w1(ts, g + 2)
                    p.wait(nc.scalar, "te", umm_done[(ts, g)])
                    gelu_done[(ts, uc)] = sop("activation",
                                              BIG1[:, uc * 512:uc * 512 + 512],
                                              PSx[:, ucoff:ucoff + 512], AF.Gelu,
                                              bias=b_b1(uc))
                if ts + 1 < NTS:
                    load_w1(ts + 1, 0)
                    load_w1(ts + 1, 1)
                obanks = [(PS_A, 512), (PS_B, 512), (PS_D, 0)]
                ev = {}
                p.wait(nc.tensor, "act", gelu_done[(ts, NUC - 1)])
                if ts == 0:
                    p.wait(nc.tensor, "gw2", ld_w2)
                    p.wait(nc.tensor, "dve")
                for fo in range(NCH):
                    PSx, coff = obanks[fo % 3]
                    if obank_ev.get(fo % 3) is not None:
                        p.wait(nc.tensor, "dve", obank_ev[fo % 3])
                    for uc in range(NUC):
                        tt = nc.tensor.matmul(
                            PSx[:, coff:coff + 512],
                            POOL[:, OF_W2 + uc * 1024 + fo * 128:
                                 OF_W2 + uc * 1024 + fo * 128 + 128],
                            BIG1[:, uc * 512:uc * 512 + 512],
                            start=(uc == 0), stop=(uc == NUC - 1))
                    t = p.inc(tt, "te", 1)
                    p.wait(nc.vector, "te", t)
                    xs = X[:, fo * T + ts * QW: fo * T + ts * QW + QW]
                    ev[fo] = vop("scalar_tensor_tensor", xs, PSx[:, coff:coff + 512],
                                 b_b2(fo), xs, ALU.add, ALU.add)
                    obank_ev[fo % 3] = ev[fo]
                    if store_out:
                        eng, sem = (nc.sync, "s16") if fo % 2 == 0 else \
                            (nc.scalar, "s16b")
                        p.wait(eng, "dve", ev[fo])
                        p.inc(eng.dma_start(
                            yT[fo * 128:(fo + 1) * 128, ts * QW:ts * QW + QW],
                            xs), sem, 16)
                    if ln_next is not None:
                        ln_next[2](fo)
                if ln_next is not None:
                    ln_dve_next[0] = p.cnt["dve"]
            return p.cnt["dve"]

        ln1_ts = emit_layernorm_exp(list(range(NTS)), BIG1, OA_XSQ2,
                                    wait_s16=x_chunk)
        h1_done = p.cnt["dve"]

        if upto >= 1:
            p.wait(nc.tensor, "g16", aw_loaded)
            st1_ev = {}
            for sl in range(4):
                for wi, (woff, ooff) in enumerate(((OA_WQK, OA_QK64),
                                                   (OA_WVV, OA_VDUP))):
                    gi = sl * 2 + wi
                    PSx = PS_A if gi % 2 == 0 else PS_B
                    p.wait(nc.tensor, "dve", ln1_ts[sl])
                    if gi >= 2:
                        p.wait(nc.tensor, "act", st1_ev[gi - 2])
                    for c in range(NCH):
                        tt = nc.tensor.matmul(
                            PSx[:, 0:512],
                            POOL[:, woff + c * 128: woff + c * 128 + 128],
                            BIG1[:, c * T + sl * 512: c * T + sl * 512 + 512],
                            start=(c == 0), stop=(c == NCH - 1))
                    t = p.inc(tt, "te", 1)
                    p.wait(nc.scalar, "te", t)
                    st1_ev[gi] = sop("activation",
                                     POOL[:, ooff + sl * 512: ooff + sl * 512 + 512],
                                     PSx[:, 0:512], AF.Identity)

            k_ev = {}
            v_ev = {}
            p.wait(nc.tensor, "act")
            for c in range(NCH):
                for sl in range(4):
                    gi = c * 4 + sl
                    PSx = PS_A if sl % 2 == 0 else PS_B
                    if gi >= 2:
                        p.wait(nc.tensor, "dve", k_ev[gi - 2])
                    lt = POOL[64:128, OA_FCQK + c * 128: OA_FCQK + c * 128 + 128]
                    rh = POOL[64:128, OA_QK64 + sl * 512: OA_QK64 + sl * 512 + 512]
                    t = p.inc(nc.tensor.matmul(PSx[:, 512:1024], lt, rh, start=True,
                                               stop=True, tile_position=(64, 0)), "te", 1)
                    p.wait(nc.vector, "te", t)
                    k_ev[c * 4 + sl] = vop(
                        "tensor_scalar_add",
                        KAR[:, c * T + sl * 512: c * T + sl * 512 + 512],
                        PSx[:, 512:1024], b_fck(c))
                p.wait(nc.vector, "act")
                emit_knorm(c, OA_SQ)
                for sl in range(4):
                    gi = c * 4 + sl
                    PSx = PS_A if sl % 2 == 0 else PS_B
                    if gi >= 2:
                        p.wait(nc.tensor, "act", v_ev[gi - 2])
                    if c % 2 == 0:
                        lt = POOL[0:64, OA_FCVV + c * 128: OA_FCVV + c * 128 + 128]
                        rh = POOL[0:64, OA_VDUP + sl * 512: OA_VDUP + sl * 512 + 512]
                        tp = (0, 0)
                    else:
                        lt = POOL[64:128, OA_FCVV + c * 128: OA_FCVV + c * 128 + 128]
                        rh = POOL[64:128, OA_VDUP + sl * 512: OA_VDUP + sl * 512 + 512]
                        tp = (64, 0)
                    t = p.inc(nc.tensor.matmul(PSx[:, 0:512], lt, rh, start=True,
                                               stop=True, tile_position=tp), "te", 1)
                    p.wait(nc.scalar, "te", t)
                    v_ev[c * 4 + sl] = sop(
                        "activation", POOL[:, OA_VTS + sl * 512: OA_VTS + sl * 512 + 512],
                        PSx[:, 0:512], AF.Identity, bias=b_fcv(c))
                p.wait(nc.tensor, "act")
                emit_vtranspose(OA_VTS, c, entry=(c == 0))
            kv_done = p.cnt["dve"]
            kv_te = p.cnt["te"]

            for g2 in (0, 2):
                p.wait(nc.tensor, "dve")
                q_ev = {}
                for qq in (g2, g2 + 1):
                    for c in range(NCH):
                        gi = (qq - g2) * NCH + c
                        PSx = PS_A if gi % 2 == 0 else PS_B
                        if gi >= 2:
                            p.wait(nc.tensor, "act", q_ev[gi - 2])
                        lt = POOL[0:64, OA_FCQK + c * 128: OA_FCQK + c * 128 + 128]
                        rh = POOL[0:64, OA_QK64 + qq * 512: OA_QK64 + qq * 512 + 512]
                        t = p.inc(nc.tensor.matmul(PSx[:, 0:512], lt, rh, start=True,
                                                   stop=True, tile_position=(0, 0)),
                                  "te", 1)
                        p.wait(nc.scalar, "te", t)
                        q_ev[gi] = sop(
                            "activation",
                            BIG2[:, (qq % 2) * 4096 + c * 512:
                                 (qq % 2) * 4096 + c * 512 + 512],
                            PSx[:, 0:512], AF.Identity, bias=b_fcq(c))
                p.wait(nc.vector, "act")
                eb = emit_qnorm2_and_ebias(OA_SQ)
                for qq in (g2, g2 + 1):
                    od, os16 = emit_pairloop(qq, (qq % 2) * 4096, BIG1, OA_PSB,
                                             OA_BNC, OA_XSQ, eb,
                                             kv_done if qq == 0 else p.cnt["dve"],
                                             flush_tail=(qq == 3))
            emit_outproj_all(BIG1, fco_h, OA_FCO, b_fco, od, os16, kv_te)
            p.barrier()
        if debug:
            p.barrier()
            for c in range(NCH):
                p.inc(nc.gpsimd.dma_start(dbgh[c * 128:(c + 1) * 128, :],
                                          BIG1[:, c * T:(c + 1) * T],
                                          max_dma_last_dim=2048), "g16", 16)
            p.barrier()

        if upto >= 2:
            emit_ffn()
            p.barrier()

        if upto >= 3:
            vop("memset", POOL[:, P_VAR:P_VAR + 16896]
                .rearrange("p (kc h f) -> p kc h f", h=16, f=66)[:, :, :, 64:66], 1.0)
            ln3_ts = emit_layernorm_exp(list(range(NTS)), BIG1, OA_XSQ2)
            h3_done = p.cnt["dve"]

            kv_loads = {}

            def load_kv(c):
                pr = c % 2
                if c == 0:
                    p.wait(nc.gpsimd, "dve", h3_done)
                kdst = POOL[:, OB_KV + pr * 1024: OB_KV + pr * 1024 + 1024]
                vdst = POOL[:, OB_KV + 2048 + pr * 1024: OB_KV + 2048 + pr * 1024 + 1024]
                sem = "g16" if pr == 0 else "g16b"
                if c >= 2:
                    p.wait(nc.gpsimd, "te", kv_loads[c - 2][2])
                p.wait(nc.gpsimd, sem)
                a = p.inc(nc.gpsimd.dma_start(kdst, wqkv_kv[c, 0]), sem, 16)
                b = p.inc(nc.gpsimd.dma_start(vdst, wqkv_kv[c, 1]), sem, 16)
                kv_loads[c] = [sem, b, None]

            load_kv(0)
            load_kv(1)
            kvb_ev = {}
            for c in range(NCH):
                pr = c % 2
                kt_ = POOL[:, OB_KV + pr * 1024: OB_KV + pr * 1024 + 1024]
                vt_ = POOL[:, OB_KV + 2048 + pr * 1024: OB_KV + 2048 + pr * 1024 + 1024]
                p.wait(nc.tensor, kv_loads[c][0])
                for which, wt in (("k", kt_), ("v", vt_)):
                    for sl in range(4):
                        gi = c * 8 + (4 if which == "v" else 0) + sl
                        PSx = PS_A if sl % 2 == 0 else PS_B
                        if c == 0:
                            p.wait(nc.tensor, "dve", ln3_ts[sl])
                        if gi >= 2:
                            pv_ = kvb_ev[gi - 2]
                            if isinstance(pv_, tuple):
                                p.wait(nc.tensor, pv_[0], pv_[1])
                            else:
                                p.wait(nc.tensor, "dve", pv_)
                        for dc in range(NCH):
                            tt = nc.tensor.matmul(
                                PSx[:, 0:512], wt[:, dc * 128:(dc + 1) * 128],
                                BIG1[:, dc * T + sl * 512: dc * T + sl * 512 + 512],
                                start=(dc == 0), stop=(dc == NCH - 1))
                        t = p.inc(tt, "te", 1)
                        if which == "k":
                            p.wait(nc.vector, "te", t)
                            kvb_ev[gi] = vop(
                                "tensor_copy",
                                KAR[:, c * T + sl * 512: c * T + sl * 512 + 512],
                                PSx[:, 0:512])
                        else:
                            p.wait(nc.scalar, "te", t)
                            kvb_ev[gi] = ("act", sop(
                                "activation",
                                POOL[:, OB_VTS + sl * 512: OB_VTS + sl * 512 + 512],
                                PSx[:, 0:512], AF.Identity))
                kv_loads[c][2] = p.cnt["te"]
                if c + 2 < NCH:
                    load_kv(c + 2)
                emit_knorm(c, OB_SQ)
                emit_vtranspose(OB_VTS, c, entry=(c == 0))
            kvb_done = p.cnt["dve"]
            kvb_te = p.cnt["te"]

            q_loads = {}

            def load_q(qq2, c):
                pr = c % 2
                sem = "g16" if pr == 0 else "g16b"
                dst = POOL[:, OB_QROT + pr * 1024: OB_QROT + pr * 1024 + 1024]
                prevkey = (qq2, c - 2) if c >= 2 else ((qq2 - 2, c + 6) if qq2 > 0 else None)
                if prevkey is not None and prevkey in q_loads:
                    p.wait(nc.gpsimd, "te", q_loads[prevkey][1])
                p.wait(nc.gpsimd, sem)
                q_loads[(qq2, c)] = [sem,
                                     p.inc(nc.gpsimd.dma_start(dst, wqkv_q[c]), sem, 16)]

            load_q(0, 0)
            load_q(0, 1)
            for g2 in (0, 2):
                p.wait(nc.tensor, "dve")
                qb_ev = {}
                for c in range(NPAIR):
                    pr = c % 2
                    PSx = PS_A if pr == 0 else PS_B
                    wt = POOL[:, OB_QROT + pr * 1024: OB_QROT + pr * 1024 + 1024]
                    p.wait(nc.tensor, q_loads[(g2, c)][0])
                    if c >= 2:
                        p.wait(nc.tensor, "dve", qb_ev[c - 2])
                    for q2 in (g2, g2 + 1):
                        coff2 = (q2 % 2) * 512
                        for dc in range(NCH):
                            tt = nc.tensor.matmul(
                                PSx[:, coff2:coff2 + 512],
                                wt[:, dc * 128:(dc + 1) * 128],
                                BIG1[:, dc * T + q2 * 512: dc * T + q2 * 512 + 512],
                                start=(dc == 0), stop=(dc == NCH - 1))
                    t = p.inc(tt, "te", 1)
                    q_loads[(g2, c)][1] = t
                    if c + 2 < NPAIR:
                        load_q(g2, c + 2)
                    elif g2 == 0:
                        load_q(2, c - 6)
                    p.wait(nc.vector, "te", t)
                    if c == 0:
                        p.wait(nc.vector, "act")
                    vop("tensor_copy",
                        BIG2[:, 0 + c * 512: 0 + c * 512 + 512], PSx[:, 0:512])
                    qb_ev[c] = vop("tensor_copy",
                                   BIG2[:, 4096 + c * 512: 4096 + c * 512 + 512],
                                   PSx[:, 512:1024])
                eb = emit_qnorm2_and_ebias(OB_SQ)
                for qq in (g2, g2 + 1):
                    od, os16 = emit_pairloop(qq, (qq % 2) * 4096, BIG1, OB_PSB,
                                             OB_BNC, OB_XSQ, eb,
                                             kvb_done if qq == 0 else p.cnt["dve"],
                                             flush_tail=(qq == 3))
            emit_outproj_all(BIG1, wproj_h, OB_WPROJ, b_proj, od, os16, kvb_te)
            p.barrier()

        if upto >= 4:
            emit_ffn(store_out=True)
            p.barrier()

        p.wait(nc.sync, "s16")
        p.wait(nc.sync, "s16b")
        p.wait(nc.scalar, "s16")
        p.wait(nc.scalar, "s16b")
    return nc



def _pack_col_tiles(a):
    w = a.shape[1]
    return np.ascontiguousarray(
        a.reshape(8, 128, w).transpose(1, 0, 2).reshape(128, 8 * w))


def host_prep(inp):
    f = np.float32
    import ml_dtypes
    z = np.asarray(inp["L0_attn_a_z"], f)
    wq, wk, wv = (np.asarray(inp[k], f) for k in
                  ("L0_attn_a_wq", "L0_attn_a_wk", "L0_attn_a_wv"))
    fcq_w, fck_w, fcv_w = (np.asarray(inp[k], f) for k in
                           ("L0_attn_a_fcq_w", "L0_attn_a_fck_w", "L0_attn_a_fcv_w"))
    fco_w = np.asarray(inp["L0_attn_a_fco_w"], f)
    w_qkv = np.asarray(inp["L0_attn_b_w_qkv"], f)
    w_proj = np.asarray(inp["L0_attn_b_w_proj"], f)
    w1 = np.asarray(inp["L0_ffn_w1"], f)
    w2 = np.asarray(inp["L0_ffn_w2"], f)

    shared = {}
    shared["wqk"] = _pack_col_tiles(np.concatenate([wq, wk], axis=1))
    shared["wvv"] = _pack_col_tiles(np.concatenate([wv, wv], axis=1))
    shared["fcqk"] = np.ascontiguousarray(np.concatenate([fcq_w, fck_w], axis=0))
    shared["fcvv"] = np.ascontiguousarray(np.concatenate([fcv_w, fcv_w], axis=0))
    shared["fco_h"] = np.ascontiguousarray(
        fco_w.reshape(8, 128, 8, 128).transpose(2, 1, 0, 3).reshape(8, 128, 1024))
    kv = np.stack([
        np.stack([_pack_col_tiles(w_qkv[:, 1024 + c * 128:1024 + (c + 1) * 128]),
                  _pack_col_tiles(w_qkv[:, 2048 + c * 128:2048 + (c + 1) * 128])])
        for c in range(8)])
    shared["wqkv_kv"] = np.ascontiguousarray(kv)
    shared["wqkv_q"] = np.ascontiguousarray(
        np.stack([_pack_col_tiles(w_qkv[:, c * 128:(c + 1) * 128]) for c in range(8)]))
    shared["wproj_h"] = np.ascontiguousarray(
        w_proj.reshape(8, 128, 8, 128).transpose(2, 1, 0, 3).reshape(8, 128, 1024))
    shared["w1_h"] = np.ascontiguousarray(
        np.stack([_pack_col_tiles(w1[:, g * 512:(g + 1) * 512]) for g in range(8)]))
    shared["w2_h"] = np.ascontiguousarray(
        w2.reshape(32, 128, 8, 128).transpose(1, 0, 2, 3).reshape(128, 32768))
    bc = np.zeros((128, 80), f)
    bc[:, 0:8] = np.asarray(inp["L0_attn_a_fcq_b"], f).reshape(8, 128).T
    bc[:, 8:16] = np.asarray(inp["L0_attn_a_fck_b"], f).reshape(8, 128).T
    bc[:, 16:24] = np.asarray(inp["L0_attn_a_fcv_b"], f).reshape(8, 128).T
    bc[:, 24:32] = np.asarray(inp["L0_attn_a_fco_b"], f).reshape(8, 128).T
    bc[:, 32:40] = np.asarray(inp["L0_attn_b_b_proj"], f).reshape(8, 128).T
    bc[:, 40:48] = np.asarray(inp["L0_ffn_b2"], f).reshape(8, 128).T
    bc[:, 48:80] = np.asarray(inp["L0_ffn_b1"], f).reshape(32, 128).T
    shared["biascols"] = bc
    shared["identb"] = np.eye(128, dtype=ml_dtypes.bfloat16)

    in_maps = []
    for c in range(z.shape[0]):
        m = dict(shared)
        m["xT"] = np.ascontiguousarray(z[c].T)
        in_maps.append(m)
    return in_maps



from concourse.bass_utils import run_bass_kernel_spmd


def kernel(**inputs):
    in_maps = host_prep(inputs)
    nc = build(upto=4, debug=0)
    res = run_bass_kernel_spmd(nc, in_maps, core_ids=list(range(8)))
    out = np.stack([np.asarray(res.results[c]["yT"]).T for c in range(8)])
    return np.ascontiguousarray(out.astype(np.float32))


# version 27
# speedup vs baseline: 1.1915x; 1.0054x over previous
import numpy as np
from contextlib import ExitStack

import concourse.bass as bass
import concourse.mybir as mybir

F32 = mybir.dt.float32
BF16 = mybir.dt.bfloat16
AF = mybir.ActivationFunctionType
ALU = mybir.AluOpType
AX = mybir.AxisListType

T = 2048
D = 1024
NCH = 8
NPAIR = 8
NQQ = 4
QW = 512
KC = 16
NUC = 32
NTS = 4
EPS = 1e-5

P_KAR = 0
P_VAR = 16384
VSTR = 1056
MB = 33280
OA_QK64 = MB + 0
OA_VDUP = MB + 2048
OA_PSB = MB + 2048
OA_WQK = MB + 4096
OA_WVV = MB + 5120
OA_VTS = MB + 4096
OA_FCO = MB + 4096
OA_FCQK = MB + 6144
OA_FCVV = MB + 7168
OA_BNC = MB + 7168
OA_XB = MB + 2048
OA_XSQ2 = MB + 3072
OA_SQ = MB + 9216
OA_XSQ = MB + 8192
OB_KV = MB + 0
OB_PSB = MB + 0
OB_QROT = MB + 4096
OB_VTS = MB + 6144
OB_WPROJ = MB + 6144
OB_BNC = MB + 2048
OB_XB = MB + 2048
OB_SQ = MB + 9216
OB_XSQ = MB + 3072
OF_W2 = 0
OF_W1 = 32768
OF_XB = 40960
OF_XSQ = 41984
POOL_COLS = 43008


class Prog:
    def __init__(self, nc):
        self.nc = nc
        self._stack = None
        self.sems = {}
        self.cnt = {}

    def sem(self, name):
        s = self._stack.enter_context(self.nc.semaphore(name))
        self.sems[name] = s
        self.cnt[name] = 0
        return s

    def inc(self, instr, name, n=1):
        instr.then_inc(self.sems[name], n)
        self.cnt[name] += n
        return self.cnt[name]

    def wait(self, eng, name, v=None):
        eng.wait_ge(self.sems[name], self.cnt[name] if v is None else v)

    def barrier(self):
        nc = self.nc
        for eng in (nc.tensor, nc.scalar, nc.vector, nc.sync, nc.gpsimd):
            for nm in ("te", "act", "dve", "s16", "s16b", "g16", "g16b",
                       "gw2"):
                self.wait(eng, nm)


def build(upto=4, debug=0):
    nc = bass.Bass("TRN2", target_bir_lowering=False)
    p = Prog(nc)

    xT = nc.dram_tensor("xT", [D, T], F32, kind="ExternalInput").ap()
    wqk_d = nc.dram_tensor("wqk", [128, 1024], F32, kind="ExternalInput").ap()
    wvv_d = nc.dram_tensor("wvv", [128, 1024], F32, kind="ExternalInput").ap()
    fcqk_d = nc.dram_tensor("fcqk", [128, 1024], F32, kind="ExternalInput").ap()
    fcvv_d = nc.dram_tensor("fcvv", [128, 1024], F32, kind="ExternalInput").ap()
    fco_h = nc.dram_tensor("fco_h", [8, 128, 1024], F32, kind="ExternalInput").ap()
    wqkv_kv = nc.dram_tensor("wqkv_kv", [8, 2, 128, 1024], F32, kind="ExternalInput").ap()
    wqkv_q = nc.dram_tensor("wqkv_q", [8, 128, 1024], F32, kind="ExternalInput").ap()
    wproj_h = nc.dram_tensor("wproj_h", [8, 128, 1024], F32, kind="ExternalInput").ap()
    w1_h = nc.dram_tensor("w1_h", [8, 128, 4096], F32, kind="ExternalInput").ap()
    w2_h = nc.dram_tensor("w2_h", [128, 32768], F32, kind="ExternalInput").ap()
    biascols = nc.dram_tensor("biascols", [128, 80], F32, kind="ExternalInput").ap()
    identb_d = nc.dram_tensor("identb", [128, 128], BF16, kind="ExternalInput").ap()
    yT = nc.dram_tensor("yT", [D, T], F32, kind="ExternalOutput").ap()
    if debug:
        dbgh = nc.dram_tensor("dbgh", [D, T], F32, kind="ExternalOutput").ap()

    with ExitStack() as ctx:
        p._stack = ctx
        sbt = lambda nm, shp, dt: ctx.enter_context(nc.sbuf_tensor(nm, shp, dt))
        pst = lambda nm, shp, dt: ctx.enter_context(nc.psum_tensor(nm, shp, dt))
        for nm in ("te", "act", "dve", "s16", "s16b", "g16", "g16b", "gw2"):
            p.sem(nm)

        X = sbt("X", [128, NCH * T], F32)
        BIG1 = sbt("BIG1", [128, 16384], BF16)
        BIG2 = sbt("BIG2", [128, 8192], BF16)
        POOL = sbt("POOL", [128, POOL_COLS], BF16)
        identb = sbt("identb_s", [128, 128], BF16)
        ones_b = sbt("ones_b", [128, 128], BF16)
        ones_f = sbt("ones_f", [128, 128], F32)
        bcols = sbt("bcols", [128, 80], F32)
        ebias = sbt("ebias", [128, 8], F32)
        sstat = sbt("sstat", [128, 32], F32)
        SCR = sbt("SCR", [128, 1024], F32)
        PF = sbt("PF", [128, 1024], F32)
        epsb = sbt("epsb", [1, 1], F32)

        PS_A = pst("PS_A", [128, 1024], F32)
        PS_B = pst("PS_B", [128, 1024], F32)
        PS_C = pst("PS_C", [128, 1024], F32)
        PS_D = pst("PS_D", [128, 512], F32)
        PS_E = pst("PS_E", [128, 512], F32)

        KAR = POOL[:, P_KAR:P_KAR + 16384]

        def vop(name, *a, **k):
            p.wait(nc.vector, "dve")
            return p.inc(getattr(nc.vector, name)(*a, **k), "dve", 1)

        def sop(name, *a, **k):
            p.wait(nc.scalar, "act")
            return p.inc(getattr(nc.scalar, name)(*a, **k), "act", 1)

        MEAN = SCR[0:1, 0:512]
        VARV = SCR[0:1, 512:1024]
        TSC = SCR[:, 512:1024]
        RECIP0 = PF[64:65, 0:512]
        RECIP1 = PF[64:65, 512:1024]
        S_QMAX = lambda c: sstat[0:1, 0 + c:1 + c]
        S_KMAX = lambda c: sstat[0:1, 8 + c:9 + c]
        S_KSL = lambda s: sstat[0:1, 16 + s:17 + s]
        S_EROW = sstat[0:1, 24:32]

        b_fcq = lambda c: bcols[:, 0 + c:1 + c]
        b_fck = lambda c: bcols[:, 8 + c:9 + c]
        b_fcv = lambda c: bcols[:, 16 + c:17 + c]
        b_fco = lambda c: bcols[:, 24 + c:25 + c]
        b_proj = lambda c: bcols[:, 32 + c:33 + c]
        b_b2 = lambda c: bcols[:, 40 + c:41 + c]
        b_b1 = lambda uc: bcols[:, 48 + uc:49 + uc]

        vop("memset", ones_b[:, :], 1.0)
        vop("memset", ones_f[:, :], 1.0)
        vop("memset", epsb[:, :], EPS)
        vop("memset", SCR[0:64, 0:512], 1.0)
        vop("memset", POOL[:, P_VAR:P_VAR + 16896], 1.0)

        p.inc(nc.sync.dma_start(identb[:, :], identb_d), "s16", 16)
        p.inc(nc.sync.dma_start(bcols[:, :], biascols), "s16", 16)
        p.wait(nc.vector, "s16")
        p.wait(nc.scalar, "s16")
        p.wait(nc.tensor, "s16")
        x_chunk = []
        for c in range(NCH):
            sem = p.sem("xk%d" % c)
            eng = nc.sync if c % 2 == 0 else nc.scalar
            p.inc(eng.dma_start(X[:, c * T:(c + 1) * T],
                                xT[c * 128:(c + 1) * 128, :]), "xk%d" % c, 16)
            x_chunk.append(("xk%d" % c, 16))

        p.inc(nc.gpsimd.dma_start(POOL[:, OA_WQK:OA_WQK + 1024], wqk_d), "g16", 16)
        p.inc(nc.gpsimd.dma_start(POOL[:, OA_WVV:OA_WVV + 1024], wvv_d), "g16", 16)
        p.inc(nc.gpsimd.dma_start(POOL[:, OA_FCQK:OA_FCQK + 1024], fcqk_d), "g16", 16)
        p.inc(nc.gpsimd.dma_start(POOL[:, OA_FCVV:OA_FCVV + 1024], fcvv_d), "g16", 16)
        aw_loaded = p.cnt["g16"]


        def emit_layernorm(ts, dst, dst_off, xb_off, xsq_off, wait_s16=None):
            tcol = lambda c: c * T + ts * QW
            p.wait(nc.tensor, "dve")
            mmd = {}
            for c in range(NCH):
                par = c % 2
                xb = POOL[:, xb_off + par * 512:xb_off + par * 512 + 512]
                xsq = POOL[:, xsq_off + par * 512:xsq_off + par * 512 + 512]
                if wait_s16 is not None and c == 0:
                    p.wait(nc.vector, "s16", wait_s16)
                if c >= 2:
                    p.wait(nc.vector, "te", mmd[c - 2])
                vop("tensor_copy", xb, X[:, tcol(c):tcol(c) + QW])
                v = vop("tensor_mul", xsq, xb, xb)
                p.wait(nc.tensor, "dve", v)
                nc.tensor.matmul(PS_C[0:1, 0:512], ones_b[:, 0:1], xb,
                                 start=(c == 0), stop=(c == NCH - 1))
                mmd[c] = p.inc(nc.tensor.matmul(PS_C[0:1, 512:1024], ones_b[:, 0:1],
                                                xsq,
                                                start=(c == 0), stop=(c == NCH - 1)),
                               "te", 1)
            p.wait(nc.vector, "te")
            vop("tensor_scalar_mul", MEAN, PS_C[0:1, 0:512], 1.0 / D)
            vop("tensor_mul", VARV, MEAN, MEAN)
            v = vop("scalar_tensor_tensor", VARV, PS_C[0:1, 512:1024], 1.0 / D,
                    VARV, ALU.mult, ALU.subtract)
            p.wait(nc.scalar, "dve", v)
            sop("activation", VARV, VARV, AF.Ln, bias=epsb[0:1, 0:1])
            a = sop("activation", VARV, VARV, AF.Exp, scale=-0.5)
            p.wait(nc.tensor, "act", a)
            nc.tensor.matmul(PS_C[:, 0:512], ones_f[0:1, :], MEAN, start=True, stop=True)
            t = p.inc(nc.tensor.matmul(PS_C[:, 512:1024], ones_f[0:1, :], VARV,
                                       start=True, stop=True), "te", 1)
            p.wait(nc.vector, "te", t)
            for c in range(NCH):
                vop("tensor_sub", TSC, X[:, tcol(c):tcol(c) + QW], PS_C[:, 0:512])
                vop("tensor_mul", dst[:, dst_off(c):dst_off(c) + QW],
                    TSC, PS_C[:, 512:1024])
            return p.cnt["dve"]

        def emit_layernorm_exp(ts_list, dst, xsq_off, wait_s16=None):
            tcol2 = lambda ts, c: c * T + ts * QW
            mmd = {}

            def stats(i, ts):
                if i % 2 == 0:
                    sum_dst, sq_dst = PS_D[0:1, 0:512], PS_E[0:1, 0:512]
                else:
                    sum_dst, sq_dst = PS_A[0:1, 0:512], PS_A[0:1, 512:1024]
                for c in range(NCH):
                    gi = i * NCH + c
                    xsq = POOL[:, xsq_off + (gi % 2) * 512:
                               xsq_off + (gi % 2) * 512 + 512]
                    if i == 0 and wait_s16 is not None:
                        sem_, v_ = wait_s16[c]
                        p.wait(nc.scalar, sem_, v_)
                        p.wait(nc.tensor, sem_, v_)
                    if i == 0 and c == 0:
                        p.wait(nc.scalar, "dve")
                        p.wait(nc.tensor, "dve")
                    if gi >= 2 and (gi - 2) in mmd:
                        p.wait(nc.scalar, "te", mmd[gi - 2])
                    sq = sop("activation", xsq,
                             X[:, tcol2(ts, c):tcol2(ts, c) + QW], AF.Square)
                    nc.tensor.matmul(sum_dst, ones_f[:, 0:1],
                                     X[:, tcol2(ts, c):tcol2(ts, c) + QW],
                                     start=(c == 0), stop=(c == NCH - 1))
                    p.wait(nc.tensor, "act", sq)
                    mmd[gi] = p.inc(
                        nc.tensor.matmul(sq_dst, ones_b[:, 0:1], xsq,
                                         start=(c == 0), stop=(c == NCH - 1)),
                        "te", 1)

            def finalize_apply(i, ts):
                if i % 2 == 0:
                    sum_row, sq_row = PS_D[0:1, 0:512], PS_E[0:1, 0:512]
                else:
                    sum_row, sq_row = PS_A[0:1, 0:512], PS_A[0:1, 512:1024]
                p.wait(nc.vector, "te", mmd[i * NCH + NCH - 1])
                vop("tensor_scalar_mul", MEAN, sum_row, 1.0 / D)
                vop("tensor_mul", VARV, MEAN, MEAN)
                v = vop("scalar_tensor_tensor", VARV, sq_row, 1.0 / D,
                        VARV, ALU.mult, ALU.subtract)
                p.wait(nc.scalar, "dve", v)
                sop("activation", VARV, VARV, AF.Ln, bias=epsb[0:1, 0:1])
                a = sop("activation", VARV, VARV, AF.Exp, scale=-0.5)
                p.wait(nc.tensor, "act", a)
                p.wait(nc.tensor, "dve")
                nc.tensor.matmul(PS_C[:, 0:512], ones_f[0:1, :], MEAN,
                                 start=True, stop=True)
                t = p.inc(nc.tensor.matmul(PS_C[:, 512:1024], ones_f[0:1, :],
                                           VARV, start=True, stop=True), "te", 1)
                p.wait(nc.vector, "te", t)
                for c in range(NCH):
                    vop("tensor_sub", TSC, X[:, tcol2(ts, c):tcol2(ts, c) + QW],
                        PS_C[:, 0:512])
                    vop("tensor_mul", dst[:, tcol2(ts, c):tcol2(ts, c) + QW],
                        TSC, PS_C[:, 512:1024])

            stats(0, ts_list[0])
            out = []
            for i, ts in enumerate(ts_list):
                if i + 1 < len(ts_list):
                    stats(i + 1, ts_list[i + 1])
                finalize_apply(i, ts)
                out.append(p.cnt["dve"])
            return out

        def emit_vtranspose(vts_off, cpair):
            p.wait(nc.tensor, "dve")
            p.wait(nc.tensor, "act")
            round_ev = {}
            round_eng = {}
            for r4 in range(4):
                boff = (r4 % 2) * 512
                if r4 >= 2:
                    p.wait(nc.tensor, round_eng[r4 - 2], round_ev[r4 - 2])
                for j in range(4):
                    kt = r4 * 4 + j
                    psv = PS_C[:, boff + j * 64: boff + (j + 1) * 64].bitcast(BF16)
                    src = POOL[:, vts_off + kt * 128: vts_off + kt * 128 + 128]
                    t = p.inc(nc.tensor.transpose(psv, src, identb[:, :]), "te", 1)
                if r4 % 2 == 0:
                    p.wait(nc.vector, "te", t)
                else:
                    p.wait(nc.scalar, "te", t)
                for j in range(4):
                    kt = r4 * 4 + j
                    psv = PS_C[:, boff + j * 64: boff + (j + 1) * 64].bitcast(BF16)
                    dst = POOL[:, P_VAR + kt * VSTR + 132 * cpair:
                               P_VAR + kt * VSTR + 132 * cpair + 132]
                    da = dst.rearrange("p (h f) -> p h f", h=2)[:, :, 0:64]
                    sa = psv.rearrange("p (h f) -> p h f", h=2)
                    if r4 % 2 == 0:
                        v = vop("tensor_copy", da, sa)
                    else:
                        v = sop("activation", da, sa, AF.Identity)
                round_ev[r4] = v
                round_eng[r4] = "dve" if r4 % 2 == 0 else "act"
            return p.cnt["dve"]

        def emit_knorm(cpair, sq_off):
            p.wait(nc.scalar, "te")
            p.wait(nc.scalar, "dve")
            psl = [PS_D[0:1, 0:512], PS_E[0:1, 0:512]]
            tmm = {}
            mx = {}
            for sl in range(4):
                ksq = POOL[:, (sq_off if sl % 2 == 0 else MB + 8192):
                           (sq_off if sl % 2 == 0 else MB + 8192) + 512]
                kc_ap = KAR[:, cpair * T + sl * 512: cpair * T + sl * 512 + 512]
                if sl >= 2:
                    p.wait(nc.scalar, "te", tmm[sl - 2])
                sq = sop("activation", ksq, kc_ap, AF.Square)
                p.wait(nc.tensor, "act", sq)
                if sl >= 2:
                    p.wait(nc.tensor, "dve", mx[sl - 2])
                tmm[sl] = p.inc(nc.tensor.matmul(psl[sl % 2], ones_b[:, 0:1],
                                                 ksq, start=True, stop=True),
                                "te", 1)
                p.wait(nc.vector, "te", tmm[sl])
                mx[sl] = vop("reduce_max", S_KSL(sl), psl[sl % 2], axis=AX.X)
            vop("reduce_max", S_KMAX(cpair), sstat[0:1, 16:20], axis=AX.X)

        def emit_qnorm2_and_ebias(sq_off):
            p.wait(nc.scalar, "te")
            p.wait(nc.scalar, "dve")
            psl = [PS_D[0:1, 0:512], PS_E[0:1, 0:512]]
            tmm = {}
            mx = {}
            for c in range(NPAIR):
                for qqi in range(2):
                    gi = c * 2 + qqi
                    qsq = POOL[:, (sq_off if gi % 2 == 0 else MB + 8192):
                               (sq_off if gi % 2 == 0 else MB + 8192) + 512]
                    q_ap = BIG2[:, qqi * 4096 + c * 512: qqi * 4096 + c * 512 + 512]
                    if gi >= 2:
                        p.wait(nc.scalar, "te", tmm[gi - 2])
                    sq = sop("activation", qsq, q_ap, AF.Square)
                    p.wait(nc.tensor, "act", sq)
                    if gi >= 2:
                        p.wait(nc.tensor, "dve", mx[gi - 2])
                    tmm[gi] = p.inc(nc.tensor.matmul(psl[gi % 2], ones_b[:, 0:1],
                                                     qsq, start=True, stop=True),
                                    "te", 1)
                    p.wait(nc.vector, "te", tmm[gi])
                    mx[gi] = vop("reduce_max", sstat[0:1, 20 + qqi:21 + qqi],
                                 psl[gi % 2], axis=AX.X)
                vop("reduce_max", S_QMAX(c), sstat[0:1, 20:22], axis=AX.X)
            vop("tensor_add", S_EROW, sstat[0:1, 0:8], sstat[0:1, 8:16])
            v = vop("tensor_scalar_mul", S_EROW, S_EROW, -1.0 / 16.0)
            p.wait(nc.tensor, "dve", v)
            t = p.inc(nc.tensor.matmul(PS_D[:, 0:8], ones_f[0:1, :], S_EROW,
                                       start=True, stop=True), "te", 1)
            p.wait(nc.vector, "te", t)
            p.wait(nc.vector, "act")
            return vop("tensor_copy", ebias[:, :], PS_D[:, 0:8])

        roll = {"exp": {0: None, 1: None}, "pv": {0: None, 1: None}}

        prev_mul = {0: None, 1: None}
        pending = [None]

        def emit_pairloop(qq, qdst_off, o_dst, psb_off, bnc_off, raw_off, eb_dve,
                          o_free_dve, flush_tail=False):
            s_done_all = {}

            def obank(cp, h):
                if cp == 0:
                    return PS_C[:, h * 512:(h + 1) * 512]
                return (PS_D if h == 0 else PS_E)[:, 0:512]

            def emit_S_next(cn, kc):
                par = kc % 2
                PSx = PS_A if par == 0 else PS_B
                if roll["exp"][par] is not None:
                    p.wait(nc.tensor, "act", roll["exp"][par])
                lt0 = KAR[0:64, cn * T + kc * 128: cn * T + kc * 128 + 128]
                rh0 = BIG2[0:64, qdst_off + cn * 512: qdst_off + cn * 512 + 512]
                nc.tensor.matmul(PSx[:, 0:512], lt0, rh0, start=True, stop=True,
                                 tile_position=(0, 0))
                lt1 = KAR[64:128, cn * T + kc * 128: cn * T + kc * 128 + 128]
                rh1 = BIG2[64:128, qdst_off + cn * 512: qdst_off + cn * 512 + 512]
                s_done_all[(cn, kc)] = p.inc(
                    nc.tensor.matmul(PSx[:, 512:1024], lt1, rh1, start=True,
                                     stop=True, tile_position=(64, 0)), "te", 1)

            def flush_part(i):
                if pending[0] is None:
                    return
                rc_, raw0_, bncap_, ocols_, cp_, odst_ = pending[0]
                h = i // 2
                bank = obank(cp_, h)
                row = 0 if h == 0 else 32
                hilo = SCR[row:row + 1, 512:1024].bitcast(BF16)
                rhs = hilo[:, 0:512] if i % 2 == 0 else hilo[:, 512:1024]
                if i == 0:
                    p.wait(nc.tensor, "dve")
                p.inc(nc.tensor.matmul(bank[0:64, 0:512],
                                       ones_b[row:row + 1, 0:64], rhs,
                                       start=(i % 2 == 0), stop=(i % 2 == 1)),
                      "te", 1)

            def flush_pending():
                if pending[0] is None:
                    return
                rc_, raw0_, bncap_, ocols_, cp_, odst_ = pending[0]
                pending[0] = None
                p.wait(nc.vector, "te")
                vop("tensor_mul", odst_[0:64, ocols_:ocols_ + QW], raw0_,
                    obank(cp_, 0)[0:64, 0:512])
                bm = vop("tensor_mul", bncap_, bncap_, obank(cp_, 1)[0:64, 0:512])
                prev_mul[cp_] = bm
                p.wait(nc.sync, "dve", bm)
                p.inc(nc.sync.dma_start(odst_[64:128, ocols_:ocols_ + QW], bncap_),
                      "s16", 16)

            for c in range(NPAIR):
                exp_done = {}
                s_done = {}
                pv_done = {}

                def emit_S(kc):
                    if (c, kc) in s_done_all:
                        s_done[kc] = s_done_all.pop((c, kc))
                        return
                    emit_S_next(c, kc)
                    s_done[kc] = s_done_all.pop((c, kc))

                def emit_exp(kc):
                    par = kc % 2
                    PSx = PS_A if par == 0 else PS_B
                    psb = POOL[:, psb_off + par * 1024: psb_off + par * 1024 + 1024]
                    wv = s_done[kc]
                    if roll["pv"][par] is not None and roll["pv"][par] > wv:
                        wv = roll["pv"][par]
                    p.wait(nc.scalar, "te", wv)
                    if c == 0 and kc == 0:
                        p.wait(nc.scalar, "dve", eb_dve)
                    exp_done[kc] = p.inc(
                        nc.scalar.activation(psb, PSx[:, :], AF.Exp,
                                             bias=ebias[:, c:c + 1], scale=0.125),
                        "act", 1)
                    roll["exp"][par] = exp_done[kc]

                def emit_PV(kc):
                    par = kc % 2
                    cp = c % 2
                    psb = POOL[:, psb_off + par * 1024: psb_off + par * 1024 + 1024]
                    p.wait(nc.tensor, "act", exp_done[kc])
                    if kc == 0:
                        g = prev_mul[cp] if prev_mul[cp] is not None else 0
                        p.wait(nc.tensor, "dve", max(g, o_free_dve))
                    vbase = P_VAR + kc * VSTR + 132 * c
                    lt0 = POOL[:, vbase:vbase + 65]
                    nc.tensor.matmul(obank(cp, 0)[0:65, 0:512], lt0, psb[:, 0:512],
                                     start=(kc == 0), stop=(kc == KC - 1))
                    lt1 = POOL[:, vbase + 66:vbase + 66 + 65]
                    pv_done[kc] = p.inc(
                        nc.tensor.matmul(obank(cp, 1)[0:65, 0:512], lt1,
                                         psb[:, 512:1024],
                                         start=(kc == 0), stop=(kc == KC - 1)), "te", 1)
                    roll["pv"][par] = pv_done[kc]

                emit_S(0)
                for kc in range(KC):
                    emit_exp(kc)
                    if kc + 1 < KC:
                        emit_S(kc + 1)
                    if kc >= 1:
                        emit_PV(kc - 1)
                    if 9 <= kc <= 12:
                        flush_part(kc - 9)
                    if kc == 13:
                        flush_pending()
                    if kc == KC - 2 and c + 1 < NPAIR:
                        emit_S_next(c + 1, 0)
                    if kc == KC - 1 and c + 1 < NPAIR:
                        emit_S_next(c + 1, 1)
                emit_PV(KC - 1)

                p.wait(nc.vector, "te", pv_done[KC - 1])
                cp = c % 2
                raw0 = POOL[0:64, raw_off + cp * 512:raw_off + cp * 512 + 512]
                bncap = POOL[0:64, bnc_off + cp * 512:bnc_off + cp * 512 + 512]
                vop("tensor_copy", PF[0:1, 0:512], obank(cp, 0)[64:65, 0:512])
                hsums = vop("tensor_copy", PF[0:1, 512:1024],
                            obank(cp, 1)[64:65, 0:512])
                p.wait(nc.sync, "dve", hsums)
                p.inc(nc.sync.dma_start(SCR[0:1, 0:512], PF[0:1, 0:512]),
                      "s16", 16)
                p.inc(nc.sync.dma_start(SCR[32:33, 0:512], PF[0:1, 512:1024]),
                      "s16", 16)
                vop("tensor_copy", raw0, obank(cp, 0)[0:64, 0:512])
                vop("tensor_copy", bncap, obank(cp, 1)[0:64, 0:512])
                p.wait(nc.vector, "s16")
                vop("reciprocal", PF[0:33, 0:512], SCR[0:33, 0:512])
                hl = SCR[0:33, 512:1024].bitcast(BF16)
                vop("tensor_copy", hl[:, 0:512], PF[0:33, 0:512])
                vop("tensor_copy", PF[0:33, 512:1024], hl[:, 0:512])
                rc = vop("tensor_sub", hl[:, 512:1024], PF[0:33, 0:512],
                         PF[0:33, 512:1024])
                pending[0] = (rc, raw0, bncap, c * T + qq * QW, cp, o_dst)
            if flush_tail:
                for i in range(4):
                    flush_part(i)
                flush_pending()
            return p.cnt["dve"], p.cnt["s16"]

        def emit_outproj_all(o_src, w_dram, rot_off, bias_fn, o_dve, o_s16,
                             w_release_te):
            loads = {}
            te_done = {}
            ev_done = {}

            def load(fc):
                par = fc % 2
                sem = "g16" if par == 0 else "g16b"
                dst = POOL[:, rot_off + par * 1024: rot_off + par * 1024 + 1024]
                if fc >= 2:
                    p.wait(nc.gpsimd, "te", te_done[fc - 2])
                else:
                    p.wait(nc.gpsimd, "te", w_release_te)
                p.wait(nc.gpsimd, sem)
                loads[fc] = (sem, p.inc(nc.gpsimd.dma_start(dst, w_dram[fc]), sem, 16))

            load(0)
            load(1)
            banks = [(PS_A, 0), (PS_B, 0), (PS_A, 512), (PS_B, 512)]
            p.wait(nc.tensor, "act")
            for fc in range(NCH):
                par = fc % 2
                wt = POOL[:, rot_off + par * 1024: rot_off + par * 1024 + 1024]
                p.wait(nc.tensor, loads[fc][0])
                if fc == 0:
                    p.wait(nc.tensor, "dve", o_dve)
                    p.wait(nc.tensor, "s16", o_s16)
                for dc in range(NCH):
                    for qs in range(4):
                        PSx, coff = banks[qs]
                        if dc == 0 and ev_done.get(qs) is not None:
                            p.wait(nc.tensor, "dve", ev_done[qs])
                        tt = nc.tensor.matmul(
                            PSx[:, coff:coff + 512], wt[:, dc * 128:(dc + 1) * 128],
                            o_src[:, dc * T + qs * QW: dc * T + qs * QW + QW],
                            start=(dc == 0), stop=(dc == NCH - 1))
                te_done[fc] = p.inc(tt, "te", 1)
                if fc + 2 < NCH:
                    load(fc + 2)
                p.wait(nc.vector, "te", te_done[fc])
                for qs in range(4):
                    PSx, coff = banks[qs]
                    xs = X[:, fc * T + qs * QW: fc * T + qs * QW + QW]
                    ev_done[qs] = vop("scalar_tensor_tensor", xs,
                                      PSx[:, coff:coff + 512],
                                      bias_fn(fc), xs, ALU.add, ALU.add)
            return p.cnt["dve"], p.cnt["te"]

        def emit_ln_ffn(ts, dst, dst_off, xb_off, xsq_off):
            tcol = lambda c: c * T + ts * QW
            state = {}

            def stats():
                p.wait(nc.tensor, "dve")
                p.wait(nc.scalar, "dve")
                mmd = {}
                for c in range(NCH):
                    xb = POOL[:, xb_off + (c % 2) * 512:
                              xb_off + (c % 2) * 512 + 512]
                    xsq = POOL[:, xsq_off + (c % 2) * 512:
                               xsq_off + (c % 2) * 512 + 512]
                    if c >= 2:
                        p.wait(nc.vector, "te", mmd[c - 2])
                        p.wait(nc.scalar, "te", mmd[c - 2])
                    v = vop("tensor_copy", xb, X[:, tcol(c):tcol(c) + QW])
                    sq = sop("activation", xsq, X[:, tcol(c):tcol(c) + QW],
                             AF.Square)
                    p.wait(nc.tensor, "dve", v)
                    nc.tensor.matmul(PS_C[0:1, 0:512], ones_b[:, 0:1], xb,
                                     start=(c == 0), stop=(c == NCH - 1))
                    p.wait(nc.tensor, "act", sq)
                    mmd[c] = p.inc(
                        nc.tensor.matmul(PS_C[0:1, 512:1024], ones_b[:, 0:1],
                                         xsq, start=(c == 0),
                                         stop=(c == NCH - 1)),
                        "te", 1)
                p.wait(nc.vector, "te", mmd[NCH - 1])
                vop("tensor_scalar_mul", MEAN, PS_C[0:1, 0:512], 1.0 / D)
                vop("tensor_mul", VARV, MEAN, MEAN)
                v = vop("scalar_tensor_tensor", VARV, PS_C[0:1, 512:1024],
                        1.0 / D, VARV, ALU.mult, ALU.subtract)
                p.wait(nc.scalar, "dve", v)
                sop("activation", VARV, VARV, AF.Ln, bias=epsb[0:1, 0:1])
                state["a"] = sop("activation", VARV, VARV, AF.Exp, scale=-0.5)

            def bcast():
                p.wait(nc.tensor, "act", state["a"])
                p.wait(nc.tensor, "dve")
                nc.tensor.matmul(PS_C[:, 0:512], ones_f[0:1, :], MEAN,
                                 start=True, stop=True)
                state["t"] = p.inc(
                    nc.tensor.matmul(PS_C[:, 512:1024], ones_f[0:1, :], VARV,
                                     start=True, stop=True), "te", 1)

            def apply_chunk(c):
                if c == 0:
                    p.wait(nc.vector, "te", state["t"])
                vop("tensor_sub", TSC, X[:, tcol(c):tcol(c) + QW], PS_C[:, 0:512])
                vop("tensor_mul", dst[:, dst_off(c):dst_off(c) + QW],
                    TSC, PS_C[:, 512:1024])

            return stats, bcast, apply_chunk

        def emit_ffn(store_out=False):
            ld_w2 = p.inc(nc.gpsimd.dma_start(POOL[:, OF_W2:OF_W2 + 32768], w2_h,
                                              max_dma_last_dim=8192), "gw2", 16)
            w1_loads = {}
            umm_done = {}
            gelu_done = {}

            def load_w1(ts, g):
                par = g % 2
                dst = POOL[:, OF_W1 + par * 4096: OF_W1 + par * 4096 + 4096]
                prevkey = (ts, g - 2) if g >= 2 else ((ts - 1, g + 6) if ts > 0 else None)
                sem = "g16" if par == 0 else "g16b"
                if prevkey is not None and prevkey in umm_done:
                    p.wait(nc.gpsimd, "te", umm_done[prevkey])
                p.wait(nc.gpsimd, sem)
                w1_loads[(ts, g)] = (sem, p.inc(nc.gpsimd.dma_start(dst, w1_h[g]), sem, 16))

            ln_dve_next = [None]
            obank_ev = {}
            for ts in range(NTS):
                par_ts = ts % 2
                hoff = lambda c: par_ts * 4096 + c * 512
                if ts == 0:
                    ln_dve = emit_layernorm(0, BIG2, hoff, OF_XB, OF_XSQ)
                    load_w1(0, 0)
                    load_w1(0, 1)
                else:
                    ln_dve = ln_dve_next[0]
                ln_next = None
                if ts + 1 < NTS:
                    hoff_n = (lambda pn: (lambda c: pn * 4096 + c * 512))((ts + 1) % 2)
                    ln_next = emit_ln_ffn(ts + 1, BIG2, hoff_n, OF_XB, OF_XSQ)
                ubanks = [(PS_A, 0), (PS_B, 0), (PS_A, 512), (PS_B, 512)]
                for uc in range(NUC):
                    if ln_next is not None:
                        if uc == 16:
                            ln_next[0]()
                        elif uc == 28:
                            ln_next[1]()
                    g, j = uc // 4, uc % 4
                    PSx, ucoff = ubanks[uc % 4]
                    p.wait(nc.tensor, w1_loads[(ts, g)][0])
                    if uc == 0:
                        p.wait(nc.tensor, "dve", ln_dve)
                    if uc < 4 and obank_ev.get(uc % 4) is not None:
                        p.wait(nc.tensor, "dve", obank_ev[uc % 4])
                    prevk = (ts, uc - 4) if uc >= 4 else \
                        ((ts - 1, uc + 28) if ts > 0 else None)
                    if prevk is not None and prevk in gelu_done:
                        p.wait(nc.tensor, "act", gelu_done[prevk])
                    wg = POOL[:, OF_W1 + (g % 2) * 4096: OF_W1 + (g % 2) * 4096 + 4096]
                    for dc in range(NCH):
                        tt = nc.tensor.matmul(
                            PSx[:, ucoff:ucoff + 512],
                            wg[:, dc * 512 + j * 128: dc * 512 + j * 128 + 128],
                            BIG2[:, hoff(dc):hoff(dc) + 512],
                            start=(dc == 0), stop=(dc == NCH - 1))
                    umm_done[(ts, g)] = p.inc(tt, "te", 1)
                    if j == 3 and g + 2 < 8:
                        load_w1(ts, g + 2)
                    p.wait(nc.scalar, "te", umm_done[(ts, g)])
                    gelu_done[(ts, uc)] = sop("activation",
                                              BIG1[:, uc * 512:uc * 512 + 512],
                                              PSx[:, ucoff:ucoff + 512], AF.Gelu,
                                              bias=b_b1(uc))
                if ts + 1 < NTS:
                    load_w1(ts + 1, 0)
                    load_w1(ts + 1, 1)
                obanks = [(PS_A, 512), (PS_B, 512), (PS_D, 0), (PS_E, 0)]
                ev = {}
                p.wait(nc.tensor, "act", gelu_done[(ts, NUC - 1)])
                if ts == 0:
                    p.wait(nc.tensor, "gw2", ld_w2)
                    p.wait(nc.tensor, "dve")
                for fo in range(NCH):
                    PSx, coff = obanks[fo % 4]
                    if obank_ev.get(fo % 4) is not None:
                        p.wait(nc.tensor, "dve", obank_ev[fo % 4])
                    for uc in range(NUC):
                        tt = nc.tensor.matmul(
                            PSx[:, coff:coff + 512],
                            POOL[:, OF_W2 + uc * 1024 + fo * 128:
                                 OF_W2 + uc * 1024 + fo * 128 + 128],
                            BIG1[:, uc * 512:uc * 512 + 512],
                            start=(uc == 0), stop=(uc == NUC - 1))
                    t = p.inc(tt, "te", 1)
                    p.wait(nc.vector, "te", t)
                    xs = X[:, fo * T + ts * QW: fo * T + ts * QW + QW]
                    ev[fo] = vop("scalar_tensor_tensor", xs, PSx[:, coff:coff + 512],
                                 b_b2(fo), xs, ALU.add, ALU.add)
                    obank_ev[fo % 4] = ev[fo]
                    if store_out:
                        eng, sem = (nc.sync, "s16") if fo % 2 == 0 else \
                            (nc.scalar, "s16b")
                        p.wait(eng, "dve", ev[fo])
                        p.inc(eng.dma_start(
                            yT[fo * 128:(fo + 1) * 128, ts * QW:ts * QW + QW],
                            xs), sem, 16)
                    if ln_next is not None:
                        ln_next[2](fo)
                if ln_next is not None:
                    ln_dve_next[0] = p.cnt["dve"]
            return p.cnt["dve"]

        ln1_ts = emit_layernorm_exp(list(range(NTS)), BIG1, OA_XSQ2,
                                    wait_s16=x_chunk)
        h1_done = p.cnt["dve"]

        if upto >= 1:
            p.wait(nc.tensor, "g16", aw_loaded)
            st1_ev = {}
            for sl in range(4):
                for wi, (woff, ooff) in enumerate(((OA_WQK, OA_QK64),
                                                   (OA_WVV, OA_VDUP))):
                    gi = sl * 2 + wi
                    PSx = PS_A if gi % 2 == 0 else PS_B
                    p.wait(nc.tensor, "dve", ln1_ts[sl])
                    if gi >= 2:
                        p.wait(nc.tensor, "act", st1_ev[gi - 2])
                    for c in range(NCH):
                        tt = nc.tensor.matmul(
                            PSx[:, 0:512],
                            POOL[:, woff + c * 128: woff + c * 128 + 128],
                            BIG1[:, c * T + sl * 512: c * T + sl * 512 + 512],
                            start=(c == 0), stop=(c == NCH - 1))
                    t = p.inc(tt, "te", 1)
                    p.wait(nc.scalar, "te", t)
                    st1_ev[gi] = sop("activation",
                                     POOL[:, ooff + sl * 512: ooff + sl * 512 + 512],
                                     PSx[:, 0:512], AF.Identity)

            k_ev = {}
            v_ev = {}
            p.wait(nc.tensor, "act")
            for c in range(NCH):
                for sl in range(4):
                    gi = c * 4 + sl
                    PSx = PS_A if sl % 2 == 0 else PS_B
                    if gi >= 2:
                        p.wait(nc.tensor, "dve", k_ev[gi - 2])
                    lt = POOL[64:128, OA_FCQK + c * 128: OA_FCQK + c * 128 + 128]
                    rh = POOL[64:128, OA_QK64 + sl * 512: OA_QK64 + sl * 512 + 512]
                    t = p.inc(nc.tensor.matmul(PSx[:, 512:1024], lt, rh, start=True,
                                               stop=True, tile_position=(64, 0)), "te", 1)
                    p.wait(nc.vector, "te", t)
                    k_ev[c * 4 + sl] = vop(
                        "tensor_scalar_add",
                        KAR[:, c * T + sl * 512: c * T + sl * 512 + 512],
                        PSx[:, 512:1024], b_fck(c))
                p.wait(nc.vector, "act")
                emit_knorm(c, OA_SQ)
                for sl in range(4):
                    gi = c * 4 + sl
                    PSx = PS_A if sl % 2 == 0 else PS_B
                    if gi >= 2:
                        p.wait(nc.tensor, "act", v_ev[gi - 2])
                    if c % 2 == 0:
                        lt = POOL[0:64, OA_FCVV + c * 128: OA_FCVV + c * 128 + 128]
                        rh = POOL[0:64, OA_VDUP + sl * 512: OA_VDUP + sl * 512 + 512]
                        tp = (0, 0)
                    else:
                        lt = POOL[64:128, OA_FCVV + c * 128: OA_FCVV + c * 128 + 128]
                        rh = POOL[64:128, OA_VDUP + sl * 512: OA_VDUP + sl * 512 + 512]
                        tp = (64, 0)
                    t = p.inc(nc.tensor.matmul(PSx[:, 0:512], lt, rh, start=True,
                                               stop=True, tile_position=tp), "te", 1)
                    p.wait(nc.scalar, "te", t)
                    v_ev[c * 4 + sl] = sop(
                        "activation", POOL[:, OA_VTS + sl * 512: OA_VTS + sl * 512 + 512],
                        PSx[:, 0:512], AF.Identity, bias=b_fcv(c))
                p.wait(nc.tensor, "act")
                emit_vtranspose(OA_VTS, c)
            kv_done = p.cnt["dve"]
            kv_te = p.cnt["te"]

            for g2 in (0, 2):
                p.wait(nc.tensor, "dve")
                q_ev = {}
                for qq in (g2, g2 + 1):
                    for c in range(NCH):
                        gi = (qq - g2) * NCH + c
                        PSx = PS_A if gi % 2 == 0 else PS_B
                        if gi >= 2:
                            p.wait(nc.tensor, "act", q_ev[gi - 2])
                        lt = POOL[0:64, OA_FCQK + c * 128: OA_FCQK + c * 128 + 128]
                        rh = POOL[0:64, OA_QK64 + qq * 512: OA_QK64 + qq * 512 + 512]
                        t = p.inc(nc.tensor.matmul(PSx[:, 0:512], lt, rh, start=True,
                                                   stop=True, tile_position=(0, 0)),
                                  "te", 1)
                        p.wait(nc.scalar, "te", t)
                        q_ev[gi] = sop(
                            "activation",
                            BIG2[:, (qq % 2) * 4096 + c * 512:
                                 (qq % 2) * 4096 + c * 512 + 512],
                            PSx[:, 0:512], AF.Identity, bias=b_fcq(c))
                p.wait(nc.vector, "act")
                eb = emit_qnorm2_and_ebias(OA_SQ)
                for qq in (g2, g2 + 1):
                    od, os16 = emit_pairloop(qq, (qq % 2) * 4096, BIG1, OA_PSB,
                                             OA_BNC, OA_XSQ, eb,
                                             kv_done if qq == 0 else p.cnt["dve"],
                                             flush_tail=(qq == 3))
            emit_outproj_all(BIG1, fco_h, OA_FCO, b_fco, od, os16, kv_te)
            p.barrier()
        if debug:
            p.barrier()
            for c in range(NCH):
                p.inc(nc.gpsimd.dma_start(dbgh[c * 128:(c + 1) * 128, :],
                                          BIG1[:, c * T:(c + 1) * T],
                                          max_dma_last_dim=2048), "g16", 16)
            p.barrier()

        if upto >= 2:
            emit_ffn()
            p.barrier()

        if upto >= 3:
            vop("memset", POOL[:, P_VAR:P_VAR + 16896]
                .rearrange("p (kc h f) -> p kc h f", h=16, f=66)[:, :, :, 64:66], 1.0)
            ln3_ts = emit_layernorm_exp(list(range(NTS)), BIG1, OA_XSQ2)
            h3_done = p.cnt["dve"]

            kv_loads = {}

            def load_kv(c):
                pr = c % 2
                if c == 0:
                    p.wait(nc.gpsimd, "dve", h3_done)
                kdst = POOL[:, OB_KV + pr * 1024: OB_KV + pr * 1024 + 1024]
                vdst = POOL[:, OB_KV + 2048 + pr * 1024: OB_KV + 2048 + pr * 1024 + 1024]
                sem = "g16" if pr == 0 else "g16b"
                if c >= 2:
                    p.wait(nc.gpsimd, "te", kv_loads[c - 2][2])
                p.wait(nc.gpsimd, sem)
                a = p.inc(nc.gpsimd.dma_start(kdst, wqkv_kv[c, 0]), sem, 16)
                b = p.inc(nc.gpsimd.dma_start(vdst, wqkv_kv[c, 1]), sem, 16)
                kv_loads[c] = [sem, b, None]

            load_kv(0)
            load_kv(1)
            kvb_ev = {}
            for c in range(NCH):
                pr = c % 2
                kt_ = POOL[:, OB_KV + pr * 1024: OB_KV + pr * 1024 + 1024]
                vt_ = POOL[:, OB_KV + 2048 + pr * 1024: OB_KV + 2048 + pr * 1024 + 1024]
                p.wait(nc.tensor, kv_loads[c][0])
                for which, wt in (("k", kt_), ("v", vt_)):
                    for sl in range(4):
                        gi = c * 8 + (4 if which == "v" else 0) + sl
                        PSx = PS_A if sl % 2 == 0 else PS_B
                        if c == 0:
                            p.wait(nc.tensor, "dve", ln3_ts[sl])
                        if gi >= 2:
                            pv_ = kvb_ev[gi - 2]
                            if isinstance(pv_, tuple):
                                p.wait(nc.tensor, pv_[0], pv_[1])
                            else:
                                p.wait(nc.tensor, "dve", pv_)
                        for dc in range(NCH):
                            tt = nc.tensor.matmul(
                                PSx[:, 0:512], wt[:, dc * 128:(dc + 1) * 128],
                                BIG1[:, dc * T + sl * 512: dc * T + sl * 512 + 512],
                                start=(dc == 0), stop=(dc == NCH - 1))
                        t = p.inc(tt, "te", 1)
                        if which == "k":
                            p.wait(nc.vector, "te", t)
                            kvb_ev[gi] = vop(
                                "tensor_copy",
                                KAR[:, c * T + sl * 512: c * T + sl * 512 + 512],
                                PSx[:, 0:512])
                        else:
                            p.wait(nc.scalar, "te", t)
                            kvb_ev[gi] = ("act", sop(
                                "activation",
                                POOL[:, OB_VTS + sl * 512: OB_VTS + sl * 512 + 512],
                                PSx[:, 0:512], AF.Identity))
                kv_loads[c][2] = p.cnt["te"]
                if c + 2 < NCH:
                    load_kv(c + 2)
                emit_knorm(c, OB_SQ)
                emit_vtranspose(OB_VTS, c)
            kvb_done = p.cnt["dve"]
            kvb_te = p.cnt["te"]

            q_loads = {}

            def load_q(qq2, c):
                pr = c % 2
                sem = "g16" if pr == 0 else "g16b"
                dst = POOL[:, OB_QROT + pr * 1024: OB_QROT + pr * 1024 + 1024]
                prevkey = (qq2, c - 2) if c >= 2 else ((qq2 - 2, c + 6) if qq2 > 0 else None)
                if prevkey is not None and prevkey in q_loads:
                    p.wait(nc.gpsimd, "te", q_loads[prevkey][1])
                p.wait(nc.gpsimd, sem)
                q_loads[(qq2, c)] = [sem,
                                     p.inc(nc.gpsimd.dma_start(dst, wqkv_q[c]), sem, 16)]

            load_q(0, 0)
            load_q(0, 1)
            for g2 in (0, 2):
                p.wait(nc.tensor, "dve")
                qb_ev = {}
                for c in range(NPAIR):
                    pr = c % 2
                    PSx = PS_A if pr == 0 else PS_B
                    wt = POOL[:, OB_QROT + pr * 1024: OB_QROT + pr * 1024 + 1024]
                    p.wait(nc.tensor, q_loads[(g2, c)][0])
                    if c >= 2:
                        p.wait(nc.tensor, "dve", qb_ev[c - 2])
                    for q2 in (g2, g2 + 1):
                        coff2 = (q2 % 2) * 512
                        for dc in range(NCH):
                            tt = nc.tensor.matmul(
                                PSx[:, coff2:coff2 + 512],
                                wt[:, dc * 128:(dc + 1) * 128],
                                BIG1[:, dc * T + q2 * 512: dc * T + q2 * 512 + 512],
                                start=(dc == 0), stop=(dc == NCH - 1))
                    t = p.inc(tt, "te", 1)
                    q_loads[(g2, c)][1] = t
                    if c + 2 < NPAIR:
                        load_q(g2, c + 2)
                    elif g2 == 0:
                        load_q(2, c - 6)
                    p.wait(nc.vector, "te", t)
                    if c == 0:
                        p.wait(nc.vector, "act")
                    vop("tensor_copy",
                        BIG2[:, 0 + c * 512: 0 + c * 512 + 512], PSx[:, 0:512])
                    qb_ev[c] = vop("tensor_copy",
                                   BIG2[:, 4096 + c * 512: 4096 + c * 512 + 512],
                                   PSx[:, 512:1024])
                eb = emit_qnorm2_and_ebias(OB_SQ)
                for qq in (g2, g2 + 1):
                    od, os16 = emit_pairloop(qq, (qq % 2) * 4096, BIG1, OB_PSB,
                                             OB_BNC, OB_XSQ, eb,
                                             kvb_done if qq == 0 else p.cnt["dve"],
                                             flush_tail=(qq == 3))
            emit_outproj_all(BIG1, wproj_h, OB_WPROJ, b_proj, od, os16, kvb_te)
            p.barrier()

        if upto >= 4:
            emit_ffn(store_out=True)
            p.barrier()

        p.wait(nc.sync, "s16")
        p.wait(nc.sync, "s16b")
        p.wait(nc.scalar, "s16")
        p.wait(nc.scalar, "s16b")
    return nc



def _pack_col_tiles(a):
    w = a.shape[1]
    return np.ascontiguousarray(
        a.reshape(8, 128, w).transpose(1, 0, 2).reshape(128, 8 * w))


def host_prep(inp):
    f = np.float32
    import ml_dtypes
    z = np.asarray(inp["L0_attn_a_z"], f)
    wq, wk, wv = (np.asarray(inp[k], f) for k in
                  ("L0_attn_a_wq", "L0_attn_a_wk", "L0_attn_a_wv"))
    fcq_w, fck_w, fcv_w = (np.asarray(inp[k], f) for k in
                           ("L0_attn_a_fcq_w", "L0_attn_a_fck_w", "L0_attn_a_fcv_w"))
    fco_w = np.asarray(inp["L0_attn_a_fco_w"], f)
    w_qkv = np.asarray(inp["L0_attn_b_w_qkv"], f)
    w_proj = np.asarray(inp["L0_attn_b_w_proj"], f)
    w1 = np.asarray(inp["L0_ffn_w1"], f)
    w2 = np.asarray(inp["L0_ffn_w2"], f)

    shared = {}
    shared["wqk"] = _pack_col_tiles(np.concatenate([wq, wk], axis=1))
    shared["wvv"] = _pack_col_tiles(np.concatenate([wv, wv], axis=1))
    shared["fcqk"] = np.ascontiguousarray(np.concatenate([fcq_w, fck_w], axis=0))
    shared["fcvv"] = np.ascontiguousarray(np.concatenate([fcv_w, fcv_w], axis=0))
    shared["fco_h"] = np.ascontiguousarray(
        fco_w.reshape(8, 128, 8, 128).transpose(2, 1, 0, 3).reshape(8, 128, 1024))
    kv = np.stack([
        np.stack([_pack_col_tiles(w_qkv[:, 1024 + c * 128:1024 + (c + 1) * 128]),
                  _pack_col_tiles(w_qkv[:, 2048 + c * 128:2048 + (c + 1) * 128])])
        for c in range(8)])
    shared["wqkv_kv"] = np.ascontiguousarray(kv)
    shared["wqkv_q"] = np.ascontiguousarray(
        np.stack([_pack_col_tiles(w_qkv[:, c * 128:(c + 1) * 128]) for c in range(8)]))
    shared["wproj_h"] = np.ascontiguousarray(
        w_proj.reshape(8, 128, 8, 128).transpose(2, 1, 0, 3).reshape(8, 128, 1024))
    shared["w1_h"] = np.ascontiguousarray(
        np.stack([_pack_col_tiles(w1[:, g * 512:(g + 1) * 512]) for g in range(8)]))
    shared["w2_h"] = np.ascontiguousarray(
        w2.reshape(32, 128, 8, 128).transpose(1, 0, 2, 3).reshape(128, 32768))
    bc = np.zeros((128, 80), f)
    bc[:, 0:8] = np.asarray(inp["L0_attn_a_fcq_b"], f).reshape(8, 128).T
    bc[:, 8:16] = np.asarray(inp["L0_attn_a_fck_b"], f).reshape(8, 128).T
    bc[:, 16:24] = np.asarray(inp["L0_attn_a_fcv_b"], f).reshape(8, 128).T
    bc[:, 24:32] = np.asarray(inp["L0_attn_a_fco_b"], f).reshape(8, 128).T
    bc[:, 32:40] = np.asarray(inp["L0_attn_b_b_proj"], f).reshape(8, 128).T
    bc[:, 40:48] = np.asarray(inp["L0_ffn_b2"], f).reshape(8, 128).T
    bc[:, 48:80] = np.asarray(inp["L0_ffn_b1"], f).reshape(32, 128).T
    shared["biascols"] = bc
    shared["identb"] = np.eye(128, dtype=ml_dtypes.bfloat16)

    in_maps = []
    for c in range(z.shape[0]):
        m = dict(shared)
        m["xT"] = np.ascontiguousarray(z[c].T)
        in_maps.append(m)
    return in_maps




from concourse.bass_utils import run_bass_kernel_spmd


def kernel(**inputs):
    in_maps = host_prep(inputs)
    nc = build(upto=4, debug=0)
    res = run_bass_kernel_spmd(nc, in_maps, core_ids=list(range(8)))
    out = np.stack([np.asarray(res.results[c]["yT"]).T for c in range(8)])
    return np.ascontiguousarray(out.astype(np.float32))
